# version 03
# baseline (speedup 1.0000x reference)
import dataclasses
from contextlib import ExitStack

import numpy as np
import ml_dtypes
import concourse.bass as bass
import concourse.mybir as mybir
from concourse.bass_utils import run_bass_kernel_spmd

F32 = mybir.dt.float32
BF16 = mybir.dt.bfloat16
AF = mybir.ActivationFunctionType
ALU = mybir.AluOpType
AXX = mybir.AxisListType.X

NCORE = 8
BPC = 2
T = 2048
NTOK = BPC * T
D = 1024
NH, HD = 16, 64
CHUNK = 256

ENGINES = ("sync", "gpsimd", "vector", "scalar", "tensor")


def _ap(t, apdims, off=0):
    return dataclasses.replace(t, ap=apdims, offset=t.offset + off)


class Prog:

    NPOOL = {"sync": 10, "gpsimd": 6}

    def __init__(self):
        self.nc = bass.Bass("TRN2", target_bir_lowering=False)
        self.ctx = ExitStack()
        self.streams = {e: [] for e in ENGINES}
        self.semnames = ["PE", "DV", "AC", "GP"]
        self.pools = {}
        for eng, n in self.NPOOL.items():
            self.pools[eng] = [f"{eng[:2].upper()}{i}" for i in range(n)]
            self.semnames += self.pools[eng]
        self.pool_idx = {eng: 0 for eng in self.NPOOL}
        self.cnt = {s: 0 for s in self.semnames}
        self._sems = None
        self.prefix = ""
        self._stage_stack = None
        self._dram = {}
        self.dram_kind_override = {}

    def sbuf(self, name, shape, dt):
        stk = self._stage_stack if self._stage_stack is not None else self.ctx
        return stk.enter_context(self.nc.sbuf_tensor(self.prefix + name, shape, dt))

    def psum(self, name, shape, dt=F32):
        stk = self._stage_stack if self._stage_stack is not None else self.ctx
        return stk.enter_context(self.nc.psum_tensor(self.prefix + name, shape, dt))

    def dram(self, name, shape, dt, kind=None):
        if name in self._dram:
            return self._dram[name]
        k = self.dram_kind_override.get(name, kind)
        t = (self.nc.dram_tensor(name, shape, dt, kind=k) if k
             else self.nc.dram_tensor(name, shape, dt))
        self._dram[name] = t
        return t

    def barrier(self):
        waits = tuple((s, self.cnt[s]) for s in self.semnames if self.cnt[s] > 0)
        if waits:
            for eng in ENGINES:
                self.wait(eng, waits)

    def stage_begin(self, tag):
        self.barrier()
        self.prefix = tag + "_"
        self._stage_stack = self.ctx.enter_context(ExitStack())

    def stage_end(self):
        self._stage_stack.close()
        self._stage_stack = None
        self.prefix = ""

    def op(self, eng, build, waits=()):
        s = {"vector": "DV", "scalar": "AC", "tensor": "PE", "gpsimd": "GP"}[eng]
        self.cnt[s] += 1
        val = self.cnt[s]
        wl = tuple(w for w in waits if w is not None)

        def f(e, _build=build, _wl=wl, _s=s):
            for (ws, wv) in _wl:
                if wv > 0:
                    e.wait_ge(self._sems[ws], wv)
            _build(e).then_inc(self._sems[_s], 1)
        self.streams[eng].append(f)
        return (s, val)

    def dma(self, eng, builders, waits=()):
        if not isinstance(builders, (list, tuple)):
            builders = [builders]
        pool = self.pools[eng]
        name = pool[self.pool_idx[eng] % len(pool)]
        self.pool_idx[eng] += 1
        wl = tuple(w for w in waits if w is not None)
        if self.cnt[name] > 0:
            wl = ((name, self.cnt[name]),) + wl
        for i, b in enumerate(builders):
            self.cnt[name] += 16

            def f(e, _b=b, _wl=(wl if i == 0 else ()), _name=name):
                for (ws, wv) in _wl:
                    if wv > 0:
                        e.wait_ge(self._sems[ws], wv)
                _b(e).then_inc(self._sems[_name], 16)
            self.streams[eng].append(f)
        return (name, self.cnt[name])

    def last(self, sem):
        return (sem, self.cnt[sem])

    def all_dma(self):
        return tuple((n, self.cnt[n]) for eng in self.pools
                     for n in self.pools[eng] if self.cnt[n] > 0)

    def wait(self, eng, waits):
        wl = tuple(w for w in waits if w is not None)

        def f(e, _wl=wl):
            for (ws, wv) in _wl:
                if wv > 0:
                    e.wait_ge(self._sems[ws], wv)
        self.streams[eng].append(f)

    def finalize(self):
        sems = {s: self.ctx.enter_context(self.nc.semaphore(s)) for s in self.semnames}
        self._sems = sems
        with self.ctx:
            with self.nc.Block() as block:
                for ename in ENGINES:
                    fns = self.streams[ename]
                    if not fns:
                        continue

                    def body(e, _fns=tuple(fns)):
                        for f in _fns:
                            f(e)
                    getattr(block, ename)(body)
        return self.nc


def host_consts():
    c = {}
    pos = np.arange(T, dtype=np.float64)
    inv_freq = 1.0 / 10000.0 ** (np.arange(0, HD, 2, dtype=np.float64) / HD)
    freqs = pos[:, None] * inv_freq[None, :]
    emb = np.concatenate([freqs, freqs], axis=-1)
    c["cos_t"] = np.ascontiguousarray(
        np.cos(emb).astype(np.float32).reshape(16, 128, 64).transpose(1, 0, 2).reshape(128, 1024))
    c["sin_t"] = np.ascontiguousarray(
        np.sin(emb).astype(np.float32).reshape(16, 128, 64).transpose(1, 0, 2).reshape(128, 1024))
    l_idx = np.arange(CHUNK)[None, :]
    j0 = np.arange(128)[:, None]
    c["mask0"] = (l_idx >= j0).astype(np.float32)
    c["mask1"] = (l_idx >= j0 + 128).astype(np.float32)
    c["ones_f"] = np.ones((128, 128), np.float32)
    c["ident"] = np.eye(128, dtype=np.float32)
    return c


CONSTS = host_consts()


def build_A(p=None):
    own = p is None
    if own:
        p = Prog()
    nc = p.nc
    p.stage_begin("a")

    x_h = p.dram("x", [D, NTOK], F32, kind="ExternalInput")
    wqkv_h = p.dram("w_qkv", [D, 3 * D], F32, kind="ExternalInput")
    wout_h = p.dram("w_out", [D, D], F32, kind="ExternalInput")
    g1r_h = p.dram("g1r", [1, D], F32, kind="ExternalInput")
    g1c_h = p.dram("g1c", [128, 8], F32, kind="ExternalInput")
    g2c_h = p.dram("g2c", [128, 8], F32, kind="ExternalInput")
    wdp1_h = p.dram("wdp1c", [128, 2048], F32, kind="ExternalInput")
    wdp2c_h = p.dram("wdp2c", [128, 2], F32, kind="ExternalInput")
    cos_h = p.dram("cos_t", [128, 1024], F32, kind="ExternalInput")
    sin_h = p.dram("sin_t", [128, 1024], F32, kind="ExternalInput")
    m0_h = p.dram("mask0", [128, CHUNK], F32, kind="ExternalInput")
    m1_h = p.dram("mask1", [128, CHUNK], F32, kind="ExternalInput")
    onesf_h = p.dram("ones_f", [128, 128], F32, kind="ExternalInput")
    ident_h = p.dram("ident", [128, 128], F32, kind="ExternalInput")

    v_h = p.dram("V", [D, NTOK], F32, kind="ExternalOutput")
    rp_h = p.dram("RP", [1, 1], F32, kind="ExternalOutput")
    rstd_h = p.dram("rstdA", [1, NTOK], F32)
    mrs_h = p.dram("mrsA", [1, NTOK], F32)
    attn_h = p.dram("attnA", [D, T], BF16)

    cos_sb = p.sbuf("cos_sb", [128, 1024], F32)
    sin_sb = p.sbuf("sin_sb", [128, 1024], F32)
    m0_sb = p.sbuf("m0_sb", [128, CHUNK], F32)
    m1_sb = p.sbuf("m1_sb", [128, CHUNK], F32)
    onesf_sb = p.sbuf("onesf_sb", [128, 128], F32)
    onesb_sb = p.sbuf("onesb_sb", [128, 128], BF16)
    identb_sb = p.sbuf("identb_sb", [128, 128], BF16)
    g1c_sb = p.sbuf("g1c_sb", [128, 8], F32)
    g2c_sb = p.sbuf("g2c_sb", [128, 8], F32)
    g1rb_sb = p.sbuf("g1rb_sb", [1, D], BF16)
    wdp1_sb = p.sbuf("wdp1_sb", [128, 2048], BF16)
    wdp2_sb = p.sbuf("wdp2_sb", [128, 2], BF16)

    wbig_sb = p.sbuf("wbig_sb", [128, 8 * 1536], BF16)
    wout_view = wbig_sb[:, 0:8 * 1024]
    x_sb = p.sbuf("x_sb", [128, 2 * 512], F32)
    h_sb = p.sbuf("h_sb", [128, 8 * 512], BF16)
    q_sb = p.sbuf("q_sb", [128, 16 * 1024], BF16)
    k_sb = p.sbuf("k_sb", [128, 16 * 1024], BF16)
    vv_sb = p.sbuf("vv_sb", [128, 16 * 1024], BF16)
    attn_sb = p.sbuf("attn_sb", [128, 8 * 512], BF16)
    qt_sb = p.sbuf("qt_sb", [64, 2048], BF16)
    kt_sb = p.sbuf("kt_sb", [64, 2048], BF16)
    vaug_sb = p.sbuf("vaug_sb", [128, 16 * 65], BF16)
    at_sb = p.sbuf("at_sb", [128, 2 * CHUNK], BF16)
    att_tmp2 = p.sbuf("att_tmp", [64, 2 * CHUNK], BF16)
    saug_sb = p.sbuf("saug_sb", [64, 65], BF16)
    s32_sb = p.sbuf("s32_sb", [64, 65], F32)
    rcp_sb = p.sbuf("rcp_sb", [128, 512], F32)
    rope_r = p.sbuf("rope_r", [128, 512], F32)
    rope_t1 = p.sbuf("rope_t1", [128, 512], F32)
    rope_e = p.sbuf("rope_e", [128, 512], F32)
    srow_sb = p.sbuf("srow_sb", [1, 2048], F32)
    rhob_sb = p.sbuf("rhob_sb", [1, 512], BF16)
    mrgb_sb = p.sbuf("mrgb_sb", [1, 512], BF16)
    mrsb_sb = p.sbuf("mrsb_sb", [1, 512], BF16)
    rb_sb = p.sbuf("rb_sb", [128, 512], F32)
    v_out = p.sbuf("v_out", [128, 2 * 512], F32)
    t_sb = p.sbuf("t_sb", [128, 512], F32)
    wr_sb = p.sbuf("wr_sb", [128, 512], F32)
    xm_sb = p.sbuf("xm_sb", [128, 16], F32)
    xmg_sb = p.sbuf("xmg_sb", [128, 16], BF16)
    red_sb = p.sbuf("red_sb", [128, 1], F32)
    dps_sb = p.sbuf("dps_sb", [2, 256], BF16)
    dpt_sb = p.sbuf("dpt_sb", [128, 2], BF16)
    rp_sb = p.sbuf("rp_sb", [1, 8], F32)

    PS = [p.psum(f"bank{i}", [128, 512]) for i in range(8)]
    PSB = [b_[:, :].bitcast(BF16) for b_ in PS]

    def s_(i):
        return srow_sb[0:1, i * 512:(i + 1) * 512]

    LD_C = p.dma("sync", [
        lambda e: e.dma_start(out=cos_sb[:, :], in_=cos_h.ap()),
        lambda e: e.dma_start(out=sin_sb[:, :], in_=sin_h.ap()),
        lambda e: e.dma_start(out=m0_sb[:, :], in_=m0_h.ap()),
        lambda e: e.dma_start(out=m1_sb[:, :], in_=m1_h.ap()),
        lambda e: e.dma_start(out=onesf_sb[:, :], in_=onesf_h.ap()),
        lambda e: e.dma_start(out=g1c_sb[:, :], in_=g1c_h.ap()),
        lambda e: e.dma_start(out=g2c_sb[:, :], in_=g2c_h.ap()),
    ])
    LG_C = p.dma("gpsimd", [
        lambda e: e.dma_start(out=onesb_sb[:, :], in_=onesf_h.ap()),
        lambda e: e.dma_start(out=identb_sb[:, :], in_=ident_h.ap()),
        lambda e: e.dma_start(out=g1rb_sb[:, :], in_=g1r_h.ap()),
        lambda e: e.dma_start(out=wdp1_sb[:, :], in_=wdp1_h.ap()),
        lambda e: e.dma_start(out=wdp2_sb[:, :], in_=wdp2c_h.ap()),
    ])

    prev_dvh = None
    st_attn = [None, None]

    for b in range(BPC):
        st_r_list = []
        st_m_list = []
        for g in range(4):
            t0 = b * T + g * 512
            pe_sx = None
            pe_ssq = None
            slot_ld = [None, None]
            for ci in range(8):
                sl_ = ci % 2
                ldx = p.dma("sync", lambda e, ci=ci, sl_=sl_, t0=t0: e.dma_start(
                    out=x_sb[:, sl_ * 512:(sl_ + 1) * 512],
                    in_=x_h.ap()[ci * 128:(ci + 1) * 128, t0:t0 + 512]),
                    waits=((pe_ssq, prev_dvh) if ci < 2 else (pe_sx, pe_ssq)))
                slot_ld[sl_] = ldx
                pe_sx = p.op("tensor", lambda e, sl_=sl_, ci=ci: e.matmul(
                    PS[0][0:1, :], onesf_sb[:, 0:1], x_sb[:, sl_ * 512:(sl_ + 1) * 512],
                    start=(ci == 0), stop=(ci == 7)),
                    waits=(ldx, LD_C, LG_C))
                ac_sq = p.op("scalar", lambda e, sl_=sl_: e.activation(
                    t_sb[:, :], x_sb[:, sl_ * 512:(sl_ + 1) * 512], AF.Square),
                    waits=(ldx, pe_ssq))
                pe_ssq = p.op("tensor", lambda e, ci=ci: e.matmul(
                    PS[1][0:1, :], onesf_sb[:, 0:1], t_sb[:, :],
                    start=(ci == 0), stop=(ci == 7)), waits=(ac_sq,))
            dv = p.op("vector", lambda e: e.tensor_scalar_mul(
                s_(0), PS[0][0:1, :], 1.0 / 1024), waits=(pe_ssq,))
            dv = p.op("vector", lambda e: e.tensor_scalar_mul(
                s_(1), PS[1][0:1, :], 1.0 / 1024), waits=(dv,))
            dv = p.op("vector", lambda e: e.tensor_mul(s_(2), s_(0), s_(0)), waits=(dv,))
            dv = p.op("vector", lambda e: e.tensor_sub(s_(1), s_(1), s_(2)), waits=(dv,))
            dv = p.op("vector", lambda e: e.tensor_scalar_add(s_(2), s_(1), 1e-5),
                      waits=(dv,))
            ac = p.op("scalar", lambda e: e.activation(s_(2), s_(2), AF.Sqrt), waits=(dv,))
            dv = p.op("vector", lambda e: e.reciprocal(s_(2), s_(2)), waits=(ac,))
            st_r = p.dma("sync", lambda e, t0=t0: e.dma_start(
                out=rstd_h.ap()[0:1, t0:t0 + 512], in_=s_(2)), waits=(dv,))
            st_r_list.append(st_r)
            dv = p.op("vector", lambda e: e.tensor_mul(s_(3), s_(2), s_(2)), waits=(dv,))
            dv = p.op("vector", lambda e: e.tensor_mul(s_(3), s_(3), s_(1)), waits=(dv,))
            dv = p.op("vector", lambda e: e.tensor_scalar_add(s_(3), s_(3), 1e-6),
                      waits=(dv,))
            ac = p.op("scalar", lambda e: e.activation(s_(3), s_(3), AF.Sqrt), waits=(dv,))
            dv = p.op("vector", lambda e: e.tensor_scalar_max(s_(3), s_(3), 1e-6),
                      waits=(ac,))
            dv = p.op("vector", lambda e: e.tensor_scalar_min(s_(3), s_(3), 1e6),
                      waits=(dv,))
            dv = p.op("vector", lambda e: e.reciprocal(s_(1), s_(3)), waits=(dv,))
            dv = p.op("vector", lambda e: e.tensor_mul(s_(3), s_(2), s_(1)), waits=(dv,))
            dv_rho = p.op("vector", lambda e: e.tensor_copy(rhob_sb[0:1, :], s_(3)),
                          waits=(dv,))
            dv = p.op("vector", lambda e: e.tensor_mul(s_(1), s_(0), s_(3)),
                      waits=(dv_rho,))
            dv_mrg = p.op("vector", lambda e: e.tensor_copy(mrgb_sb[0:1, :], s_(1)),
                          waits=(dv,))
            dv = p.op("vector", lambda e: e.tensor_mul(s_(1), s_(0), s_(2)),
                      waits=(dv_mrg, st_r))
            dv_mrs = p.op("vector", lambda e: e.tensor_scalar_mul(s_(1), s_(1), -1.0),
                          waits=(dv,))
            st_m = p.dma("sync", lambda e, t0=t0: e.dma_start(
                out=mrs_h.ap()[0:1, t0:t0 + 512], in_=s_(1)), waits=(dv_mrs,))
            st_m_list.append(st_m)
            pe_b = p.op("tensor", lambda e: e.matmul(
                PS[2][:, :], onesb_sb[0:1, :], rhob_sb[0:1, :],
                start=True, stop=True), waits=(dv_rho,))
            dv_rb = p.op("vector", lambda e: e.tensor_copy(rb_sb[:, :], PS[2][:, :]),
                         waits=(pe_b,))
            dvh = dv_rb
            ldh = [None, None]
            for ci in range(8):
                sl_ = ci % 2
                ldx = p.dma("sync", lambda e, sl_=sl_, ci=ci, t0=t0: e.dma_start(
                    out=x_sb[:, sl_ * 512:(sl_ + 1) * 512],
                    in_=x_h.ap()[ci * 128:(ci + 1) * 128, t0:t0 + 512]),
                    waits=((pe_ssq, p.last("AC")) if ci < 2 else (dvh,)))
                pe_r1 = p.op("tensor", lambda e, ci=ci: e.matmul(
                    PS[3][:, :], g1rb_sb[0:1, ci * 128:(ci + 1) * 128],
                    mrgb_sb[0:1, :], start=True, stop=True),
                    waits=(dvh if ci > 0 else dv_mrg,))
                dvt = p.op("vector", lambda e, ci=ci, sl_=sl_: e.scalar_tensor_tensor(
                    t_sb[:, :], x_sb[:, sl_ * 512:(sl_ + 1) * 512], g1c_sb[:, ci:ci + 1],
                    rb_sb[:, :], op0=ALU.mult, op1=ALU.mult),
                    waits=(dv_rb, ldx, p.last("DV"), p.last("PE")))
                dvh = p.op("vector", lambda e, ci=ci: e.tensor_sub(
                    h_sb[:, ci * 512:(ci + 1) * 512], t_sb[:, :], PS[3][:, :]),
                    waits=(dvt, pe_r1))
            dv_h = dvh
            for wh in range(2):
                lg_wq = p.dma("gpsimd", [
                    (lambda e, ci=ci, wh=wh: e.dma_start(
                        out=wbig_sb[:, ci * 1536:(ci + 1) * 1536],
                        in_=wqkv_h.ap()[ci * 128:(ci + 1) * 128,
                                        wh * 1536:(wh + 1) * 1536]))
                    for ci in range(8)], waits=(p.last("PE"),))
                for tt_ in range(4):
                    ttg = g * 4 + tt_
                    for sl3 in range(3):
                        sl = wh * 3 + sl3
                        bank = PS[4] if (tt_ * 3 + sl3) % 2 == 0 else PS[5]
                        pe_q = None
                        for ci in range(8):
                            pe_q = p.op("tensor",
                                        lambda e, ci=ci, tt_=tt_, sl3=sl3, bank=bank: e.matmul(
                                            bank[:, :],
                                            h_sb[:, ci * 512 + tt_ * 128:ci * 512 + tt_ * 128 + 128],
                                            wbig_sb[:, ci * 1536 + sl3 * 512:ci * 1536 + (sl3 + 1) * 512],
                                            start=(ci == 0), stop=(ci == 7)),
                                        waits=((dv_h, lg_wq, p.last("DV"), p.last("AC"))
                                               if ci == 0 else ()))
                        if sl < 4:
                            dst = q_sb if sl < 2 else k_sb
                            col0 = ttg * 1024 + (sl % 2) * 512
                            cos_rep = _ap(cos_sb[:, 0:64],
                                          [[1024, 128], [0, 8], [1, 64]], off=ttg * 64)
                            sin_rep = _ap(sin_sb[:, 0:64],
                                          [[1024, 128], [0, 8], [1, 64]], off=ttg * 64)
                            bank_ev = _ap(bank[:, 0:256], [[512, 128], [2, 256]])
                            bank_od = _ap(bank[:, 0:256], [[512, 128], [2, 256]], off=1)
                            r_ev = _ap(rope_r[:, 0:256], [[512, 128], [2, 256]])
                            r_od = _ap(rope_r[:, 0:256], [[512, 128], [2, 256]], off=1)
                            dv = p.op("vector", lambda e, bank_od=bank_od, r_ev=r_ev:
                                      e.tensor_scalar_mul(r_ev, bank_od, -1.0),
                                      waits=(pe_q,))
                            dv = p.op("vector", lambda e, bank_ev=bank_ev, r_od=r_od:
                                      e.tensor_copy(r_od, bank_ev), waits=(dv,))
                            dv = p.op("vector", lambda e, bank=bank, cos_rep=cos_rep:
                                      e.tensor_mul(rope_t1[:, :], bank[:, :], cos_rep),
                                      waits=(dv,))
                            dv = p.op("vector", lambda e, sin_rep=sin_rep:
                                      e.tensor_mul(rope_r[:, :], rope_r[:, :], sin_rep),
                                      waits=(dv,))
                            dv = p.op("vector", lambda e:
                                      e.tensor_add(rope_t1[:, :], rope_t1[:, :], rope_r[:, :]),
                                      waits=(dv,))
                            dv = p.op("vector", lambda e:
                                      e.tensor_scalar_min(rope_e[:, :], rope_t1[:, :], 0.0),
                                      waits=(dv,))
                            ac = p.op("scalar", lambda e:
                                      e.activation(rope_e[:, :], rope_e[:, :], AF.Exp),
                                      waits=(dv,))
                            dv = p.op("vector", lambda e, dst=dst, col0=col0:
                                      e.scalar_tensor_tensor(
                                          dst[:, col0:col0 + 512], rope_t1[:, :], 0.0,
                                          rope_e[:, :], op0=ALU.max, op1=ALU.add),
                                      waits=(ac,))
                        else:
                            col0 = ttg * 1024 + (sl - 4) * 512
                            ac = p.op("scalar", lambda e, col0=col0, bank=bank: e.activation(
                                vv_sb[:, col0:col0 + 512], bank[:, :], AF.Copy),
                                waits=(pe_q,))
            prev_dvh = p.last("DV")
        dv_qkv = p.last("DV")
        ac_qkv = p.last("AC")
        for hh in range(NH):
            dv = p.op("vector", lambda e: e.memset(vaug_sb[:, :], 1.0),
                      waits=(dv_qkv, ac_qkv, p.last("PE")))
            for ti in range(16):
                dv = p.op("vector", lambda e, ti=ti, hh=hh: e.tensor_copy(
                    vaug_sb[:, ti * 65:ti * 65 + 64],
                    vv_sb[:, ti * 1024 + hh * 64:ti * 1024 + hh * 64 + 64]),
                    waits=(dv,))
            dv_vaug = dv
            dv_t = dv_vaug
            for ti in range(16):
                for (src_, dstt) in ((q_sb, qt_sb), (k_sb, kt_sb)):
                    pe_t = p.op("tensor", lambda e, ti=ti, hh=hh, src_=src_: e.transpose(
                        PSB[6][0:64, 0:128],
                        src_[:, ti * 1024 + hh * 64:ti * 1024 + hh * 64 + 64],
                        identb_sb[:, :]), waits=(dv_t,))
                    dv_t = p.op("vector", lambda e, ti=ti, dstt=dstt: e.tensor_copy(
                        dstt[:, ti * 128:(ti + 1) * 128], PSB[6][0:64, 0:128]),
                        waits=(pe_t,))
            dv_tr = dv_t
            pe_state = None
            dv_attn = dv_tr
            for c in range(8):
                if c == 0:
                    dv_snap = p.op("vector", lambda e: e.memset(saug_sb[:, :], 0.0),
                                   waits=(dv_tr,))
                else:
                    dv_snap = p.op("vector", lambda e: e.tensor_copy(
                        saug_sb[:, :], s32_sb[:, :]), waits=(dv_s32,))
                pe_at0 = p.op("tensor", lambda e, c=c: e.matmul(
                    PS[2][0:128, 0:256], kt_sb[:, c * 256:c * 256 + 128],
                    qt_sb[:, c * 256:(c + 1) * 256], start=True, stop=True),
                    waits=(dv_attn,))
                pe_at1 = p.op("tensor", lambda e, c=c: e.matmul(
                    PS[3][0:128, 0:256], kt_sb[:, c * 256 + 128:(c + 1) * 256],
                    qt_sb[:, c * 256:(c + 1) * 256], start=True, stop=True))
                dv_at = p.op("vector", lambda e: e.tensor_mul(
                    at_sb[:, 0:256], PS[2][0:128, 0:256], m0_sb[:, :]), waits=(pe_at0,))
                dv_at = p.op("vector", lambda e: e.tensor_mul(
                    at_sb[:, 256:512], PS[3][0:128, 0:256], m1_sb[:, :]),
                    waits=(pe_at1, dv_at))
                pe_num = p.op("tensor", lambda e, c=c: e.matmul(
                    PS[1][0:65, 0:256], saug_sb[:, :], qt_sb[:, c * 256:(c + 1) * 256],
                    start=True, stop=False), waits=(dv_snap, dv_at))
                for jt in range(2):
                    ti = 2 * c + jt
                    pe_num = p.op("tensor", lambda e, ti=ti, jt=jt: e.matmul(
                        PS[1][0:65, 0:256], vaug_sb[:, ti * 65:(ti + 1) * 65],
                        at_sb[:, jt * 256:(jt + 1) * 256],
                        start=False, stop=(jt == 1)))
                for jt in range(2):
                    ti = 2 * c + jt
                    pe_state = p.op("tensor", lambda e, ti=ti, hh=hh, c=c, jt=jt: e.matmul(
                        PS[0][0:64, 0:65],
                        k_sb[:, ti * 1024 + hh * 64:ti * 1024 + hh * 64 + 64],
                        vaug_sb[:, ti * 65:(ti + 1) * 65],
                        start=(jt == 0), stop=(jt == 1)),
                        waits=((dv_snap,) if jt == 0 else ()))
                if c == 0:
                    dv_s32 = p.op("vector", lambda e: e.tensor_copy(
                        s32_sb[:, :], PS[0][0:64, 0:65]), waits=(pe_state,))
                else:
                    dv_s32 = p.op("vector", lambda e: e.tensor_add(
                        s32_sb[:, :], s32_sb[:, :], PS[0][0:64, 0:65]),
                        waits=(pe_state,))
                dv_d = p.op("vector", lambda e: e.tensor_scalar_max(
                    rcp_sb[64:65, 0:256], PS[1][64:65, 0:256], 1e-6), waits=(pe_num,))
                dv_d = p.op("vector", lambda e: e.reciprocal(
                    rcp_sb[64:65, 256:512], rcp_sb[64:65, 0:256]), waits=(dv_d,))
                pe_rb = p.op("tensor", lambda e: e.matmul(
                    PS[4][0:64, 0:256], onesf_sb[64:65, 0:64], rcp_sb[64:65, 256:512],
                    start=True, stop=True), waits=(dv_d,))
                dv_rb2 = p.op("vector", lambda e: e.tensor_copy(
                    rcp_sb[0:64, 0:256], PS[4][0:64, 0:256]), waits=(pe_rb,))
                ats = c % 2
                dv_attn = p.op("vector", lambda e, ats=ats: e.tensor_mul(
                    att_tmp2[:, ats * 256:(ats + 1) * 256], PS[1][0:64, 0:256],
                    rcp_sb[0:64, 0:256]),
                    waits=(dv_rb2, st_attn[ats]))
                st_attn[ats] = p.dma("sync", lambda e, hh=hh, c=c, ats=ats: e.dma_start(
                    out=attn_h.ap()[hh * 64:(hh + 1) * 64, c * 256:(c + 1) * 256],
                    in_=att_tmp2[:, ats * 256:(ats + 1) * 256]),
                    waits=(dv_attn,))
        lg_wo = p.dma("gpsimd", [
            (lambda e, fi=fi: e.dma_start(
                out=wout_view[:, fi * 1024:(fi + 1) * 1024],
                in_=wout_h.ap()[fi * 128:(fi + 1) * 128, :]))
            for fi in range(8)], waits=(p.last("PE"),))
        dv_attn_done = p.last("DV")
        st_attn_all = p.all_dma()
        slot_st = [None, None]
        slot_ac = [None, None]
        for g in range(4):
            t0 = b * T + g * 512
            tl0 = g * 512
            ld_at = p.dma("sync", [
                (lambda e, fi=fi, tl0=tl0: e.dma_start(
                    out=attn_sb[:, fi * 512:(fi + 1) * 512],
                    in_=attn_h.ap()[fi * 128:(fi + 1) * 128, tl0:tl0 + 512]))
                for fi in range(8)],
                waits=st_attn_all + (p.last("PE"),) if g == 0 else (p.last("PE"),))
            ldr = p.dma("sync", lambda e, t0=t0: e.dma_start(
                out=srow_sb[0:1, 0:512], in_=rstd_h.ap()[0:1, t0:t0 + 512]),
                waits=(p.last("DV"),) + tuple(st_r_list))
            ldm = p.dma("gpsimd", lambda e, t0=t0: e.dma_start(
                out=mrsb_sb[0:1, :], in_=mrs_h.ap()[0:1, t0:t0 + 512]),
                waits=tuple(st_m_list) + (p.last("PE"),))
            pe_rs = p.op("tensor", lambda e: e.matmul(
                PS[2][:, :], onesf_sb[0:1, 0:128], srow_sb[0:1, 0:512],
                start=True, stop=True), waits=(dv_attn_done, ldr))
            dv_rs = p.op("vector", lambda e: e.tensor_copy(rb_sb[:, :], PS[2][:, :]),
                         waits=(pe_rs,))
            pe_ss2 = None
            for do in range(8):
                bank = PS[4] if do % 2 == 0 else PS[5]
                sl_ = do % 2
                pe_w = None
                for fi in range(8):
                    pe_w = p.op("tensor", lambda e, fi=fi, do=do, bank=bank: e.matmul(
                        bank[:, :],
                        wout_view[:, fi * 1024 + do * 128:fi * 1024 + do * 128 + 128],
                        attn_sb[:, fi * 512:(fi + 1) * 512],
                        start=(fi == 0), stop=False),
                        waits=((lg_wo, ldm, ld_at, p.last("DV")) if fi == 0 else ()))
                pe_w = p.op("tensor", lambda e, bank=bank: e.matmul(
                    bank[:, :], onesb_sb[0:1, :], mrsb_sb[0:1, :],
                    start=False, stop=True))
                ldx = p.dma("sync", lambda e, sl_=sl_, do=do, t0=t0: e.dma_start(
                    out=x_sb[:, sl_ * 512:(sl_ + 1) * 512],
                    in_=x_h.ap()[do * 128:(do + 1) * 128, t0:t0 + 512]),
                    waits=((p.last("DV"), p.last("PE")) if do < 2 else ()))
                dv_t1 = p.op("vector", lambda e, sl_=sl_: e.tensor_mul(
                    t_sb[:, :], x_sb[:, sl_ * 512:(sl_ + 1) * 512], rb_sb[:, :]),
                    waits=(dv_rs, ldx, pe_ss2))
                dv_vd = p.op("vector", lambda e, sl_=sl_, bank=bank: e.tensor_add(
                    v_out[:, sl_ * 512:(sl_ + 1) * 512], t_sb[:, :], bank[:, :]),
                    waits=(dv_t1, pe_w, slot_st[sl_], slot_ac[sl_]))
                slot_st[sl_] = p.dma("sync", lambda e, sl_=sl_, do=do, t0=t0: e.dma_start(
                    out=v_h.ap()[do * 128:(do + 1) * 128, t0:t0 + 512],
                    in_=v_out[:, sl_ * 512:(sl_ + 1) * 512]),
                    waits=(dv_vd,))
                slot_ac[sl_] = ac_s = p.op("scalar", lambda e, sl_=sl_: e.activation(
                    t_sb[:, :], v_out[:, sl_ * 512:(sl_ + 1) * 512], AF.Square),
                    waits=(dv_vd, pe_ss2))
                pe_ss2 = p.op("tensor", lambda e, do=do: e.matmul(
                    PS[6][0:1, 0:512], onesf_sb[:, 0:1], t_sb[:, :],
                    start=(do == 0), stop=(do == 7)), waits=(ac_s,))
            dv = p.op("vector", lambda e: e.tensor_scalar_mul(
                s_(1), PS[6][0:1, 0:512], 1.0 / 1024), waits=(pe_ss2,))
            dv = p.op("vector", lambda e: e.tensor_scalar_add(s_(1), s_(1), 1e-6),
                      waits=(dv,))
            ac = p.op("scalar", lambda e: e.activation(s_(2), s_(1), AF.Sqrt), waits=(dv,))
            dv = p.op("vector", lambda e: e.tensor_scalar_max(s_(2), s_(2), 1e-6),
                      waits=(ac,))
            dv = p.op("vector", lambda e: e.tensor_scalar_min(s_(2), s_(2), 1e6),
                      waits=(dv,))
            dv = p.op("vector", lambda e: e.reciprocal(s_(1), s_(2)), waits=(dv,))
            dv = p.op("vector", lambda e: e.tensor_scalar_mul(s_(1), s_(1), 1.0 / T),
                      waits=(dv,))
            pe_wr = p.op("tensor", lambda e: e.matmul(
                PS[3][:, :], onesf_sb[0:1, 0:128], s_(1),
                start=True, stop=True), waits=(dv,))
            dv_wr = p.op("vector", lambda e: e.tensor_copy(wr_sb[:, :], PS[3][:, :]),
                         waits=(pe_wr,))
            dv_xm = dv_wr
            st_allv = (slot_st[0], slot_st[1])
            slot_dv = [dv_wr, dv_wr]
            for ci in range(8):
                sl_ = ci % 2
                ldv_ = p.dma("sync", lambda e, sl_=sl_, ci=ci, t0=t0: e.dma_start(
                    out=v_out[:, sl_ * 512:(sl_ + 1) * 512],
                    in_=v_h.ap()[ci * 128:(ci + 1) * 128, t0:t0 + 512]),
                    waits=st_allv + (slot_dv[sl_], slot_ac[sl_]))
                slot_st[sl_] = ldv_
                slot_dv[sl_] = dv_x1 = p.op("vector", lambda e, sl_=sl_: e.tensor_mul(
                    t_sb[:, :], v_out[:, sl_ * 512:(sl_ + 1) * 512], wr_sb[:, :]),
                    waits=(dv_xm, pe_ss2, ldv_))
                dv_r = p.op("vector", lambda e: e.reduce_sum(
                    red_sb[:, 0:1], t_sb[:, :], axis=AXX), waits=(dv_x1,))
                col = ci * 2 + b
                if g == 0:
                    dv_xm = p.op("vector", lambda e, col=col: e.tensor_copy(
                        xm_sb[:, col:col + 1], red_sb[:, 0:1]), waits=(dv_r,))
                else:
                    dv_xm = p.op("vector", lambda e, col=col: e.tensor_add(
                        xm_sb[:, col:col + 1], xm_sb[:, col:col + 1], red_sb[:, 0:1]),
                        waits=(dv_r,))
            prev_dvh = p.last("DV")
    dv = None
    for ci in range(8):
        dv = p.op("vector", lambda e, ci=ci: e.tensor_scalar(
            xmg_sb[:, ci * 2:ci * 2 + 2], xm_sb[:, ci * 2:ci * 2 + 2],
            g2c_sb[:, ci:ci + 1], None, op0=ALU.mult),
            waits=((p.last("DV"),) if ci == 0 else (dv,)))
    dv_xmg = dv
    pe_d1 = None
    for ci in range(8):
        pe_d1 = p.op("tensor", lambda e, ci=ci: e.matmul(
            PS[0][0:2, 0:256], xmg_sb[:, ci * 2:ci * 2 + 2],
            wdp1_sb[:, ci * 256:(ci + 1) * 256],
            start=(ci == 0), stop=(ci == 7)),
            waits=((dv_xmg, p.last("PE"), LG_C) if ci == 0 else ()))
    ac = p.op("scalar", lambda e: e.activation(dps_sb[:, :], PS[0][0:2, 0:256], AF.Silu),
              waits=(pe_d1,))
    pe_d2 = None
    dv = None
    for cj in range(2):
        pe_t = p.op("tensor", lambda e, cj=cj: e.transpose(
            PSB[1][0:128, 0:2], dps_sb[:, cj * 128:(cj + 1) * 128],
            identb_sb[0:2, 0:2]),
            waits=((ac,) if cj == 0 else (dv,)))
        dv = p.op("vector", lambda e: e.tensor_copy(dpt_sb[:, :], PSB[1][0:128, 0:2]),
                  waits=(pe_t,))
        pe_d2 = p.op("tensor", lambda e, cj=cj: e.matmul(
            PS[2][0:1, 0:2], wdp2_sb[:, cj:cj + 1], dpt_sb[:, :],
            start=(cj == 0), stop=(cj == 1)), waits=(dv,))
    ac = p.op("scalar", lambda e: e.activation(rp_sb[0:1, 0:2], PS[2][0:1, 0:2],
                                               AF.Sigmoid), waits=(pe_d2,))
    dv = p.op("vector", lambda e: e.tensor_scalar_add(
        rp_sb[0:1, 2:4], rp_sb[0:1, 0:2], 0.5), waits=(ac,))
    dv = p.op("vector", lambda e: e.tensor_scalar_max(
        rp_sb[0:1, 2:4], rp_sb[0:1, 2:4], 0.5), waits=(dv,))
    dv = p.op("vector", lambda e: e.tensor_scalar_min(
        rp_sb[0:1, 2:4], rp_sb[0:1, 2:4], 1.5), waits=(dv,))
    dv = p.op("vector", lambda e: e.reduce_sum(
        rp_sb[0:1, 4:5], rp_sb[0:1, 2:4], axis=AXX), waits=(dv,))
    p.dma("sync", lambda e: e.dma_start(out=rp_h.ap(), in_=rp_sb[0:1, 4:5]),
          waits=(dv,))
    p.wait("sync", p.all_dma())
    p.stage_end()
    return p.finalize() if own else None

def prep_A_inputs(inputs):
    x = np.asarray(inputs["L0_attn_x"])
    g1 = np.asarray(inputs["L0_attn_g1"])
    g2 = np.asarray(inputs["L0_attn_g2"])
    base = {
        "w_qkv": np.ascontiguousarray(inputs["L0_attn_w_qkv"]),
        "w_out": np.ascontiguousarray(inputs["L0_attn_w_out"]),
        "g1r": np.ascontiguousarray(g1[None, :]),
        "g1c": np.ascontiguousarray(g1.reshape(8, 128).T),
        "g2c": np.ascontiguousarray(g2.reshape(8, 128).T),
        "wdp1c": np.ascontiguousarray(
            np.asarray(inputs["L0_attn_w_dp1"]).reshape(8, 128, 256)
            .transpose(1, 0, 2).reshape(128, 2048)),
        "wdp2c": np.ascontiguousarray(
            np.asarray(inputs["L0_attn_w_dp2"]).reshape(2, 128).T),
        "cos_t": CONSTS["cos_t"], "sin_t": CONSTS["sin_t"],
        "mask0": CONSTS["mask0"], "mask1": CONSTS["mask1"],
        "ones_f": CONSTS["ones_f"], "ident": CONSTS["ident"],
    }
    maps = []
    for c in range(NCORE):
        xs = np.ascontiguousarray(
            x[BPC * c:BPC * (c + 1)].reshape(NTOK, D).T.astype(np.float32))
        maps.append({"x": xs, **base})
    return maps


def build_L2():
    p = Prog()
    p.dram_kind_override = {"X1": None, "X2": None}
    build_B1(p)
    build_B2(p)
    build_B3(p)
    build_B4(p)
    return p.finalize()


def prep_L2_inputs(inputs, x_shards, v_shards, rparts):
    m1 = prep_B1_inputs(inputs, x_shards, v_shards, rparts)
    m2 = prep_B2_inputs(inputs, [None] * NCORE)
    m3 = prep_B3_inputs(inputs, [None] * NCORE)
    m4 = prep_B4_inputs(inputs, [None] * NCORE)
    maps = []
    for c in range(NCORE):
        d = dict(m1[c])
        for m in (m2[c], m3[c], m4[c]):
            for k, v in m.items():
                if k not in ("X1", "X2", "X3") and v is not None:
                    d[k] = v
        maps.append(d)
    return maps


def kernel(**inputs):
    from concourse.bass_utils import run_bass_kernel_spmd
    cores = list(range(NCORE))
    x = np.asarray(inputs["L0_attn_x"], np.float32)
    x_shards = [np.ascontiguousarray(x[2 * c:2 * c + 2].reshape(4096, 1024).T)
                for c in range(NCORE)]

    res = run_bass_kernel_spmd(build_A(), prep_A_inputs(inputs), core_ids=cores)
    v_shards = [np.ascontiguousarray(res.results[c]["V"]) for c in range(NCORE)]
    rparts = np.array([res.results[c]["RP"][0, 0] for c in range(NCORE)], np.float32)

    res = run_bass_kernel_spmd(
        build_L2(), prep_L2_inputs(inputs, x_shards, v_shards, rparts),
        core_ids=cores)
    x3 = [np.ascontiguousarray(res.results[c]["X3"]) for c in range(NCORE)]
    sps = [res.results[c]["SP"] for c in range(NCORE)]

    mapsC = []
    for c in range(NCORE):
        sl = np.stack([sps[r][c * 256:(c + 1) * 256, :] for r in range(NCORE)])
        mapsC.append({"SPS": np.ascontiguousarray(sl.reshape(8, 256 * 2048))})
    res = run_bass_kernel_spmd(build_C(), mapsC, core_ids=cores)
    sinv = np.concatenate([res.results[c]["SR"] for c in range(NCORE)], axis=0)

    res = run_bass_kernel_spmd(build_D(), prep_D_inputs(inputs, x3, sinv),
                               core_ids=cores)
    out = np.empty((16, 2048, 1024), np.float32)
    for c in range(NCORE):
        y = res.results[c]["X4"]
        out[2 * c] = y[:, :2048].T
        out[2 * c + 1] = y[:, 2048:].T
    return out


def build_B1(p=None):
    own = p is None
    if own:
        p = Prog()
    nc = p.nc
    p.stage_begin("b1")
    FDF = 4608
    NDC = 36

    x_h = p.dram("x", [D, NTOK], F32, kind="ExternalInput")
    v_h = p.dram("V", [D, NTOK], F32, kind="ExternalInput")
    rpv_h = p.dram("rparts", [1, 8], F32, kind="ExternalInput")
    g2c_h = p.dram("g2c", [128, 8], F32, kind="ExternalInput")
    wgP_h = p.dram("wgP", [128, NDC * 1024], F32, kind="ExternalInput")
    wuP_h = p.dram("wuP", [128, NDC * 1024], F32, kind="ExternalInput")
    wd_h = p.dram("w_down", [FDF, D], F32, kind="ExternalInput")
    ghc_h = p.dram("ghc", [128, NDC], F32, kind="ExternalInput")
    iota_h = p.dram("iotaF", [128, NDC], F32, kind="ExternalInput")
    onesf_h = p.dram("ones_f", [128, 128], F32, kind="ExternalInput")

    x1_h = p.dram("X1", [D, NTOK], F32, kind="ExternalOutput")
    hid_h = p.dram("hidB1", [FDF, NTOK], BF16)

    onesf_sb = p.sbuf("onesf_sb", [128, 128], F32)
    onesb_sb = p.sbuf("onesb_sb", [128, 128], BF16)
    g2c_sb = p.sbuf("g2c_sb", [128, 8], F32)
    ghc_sb = p.sbuf("ghc_sb", [128, NDC], F32)
    iota_sb = p.sbuf("iota_sb", [128, NDC], F32)
    mcol_sb = p.sbuf("mcol_sb", [128, NDC], F32)
    szb_sb = p.sbuf("szb_sb", [128, 1], F32)
    h2_sb = p.sbuf("h2_sb", [128, 8 * NTOK], BF16)
    vg_sb = p.sbuf("vg_sb", [128, 8 * 512], F32)
    wg_sb = p.sbuf("wg_sb", [128, 2 * 1024], BF16)
    wu_sb = p.sbuf("wu_sb", [128, 2 * 1024], BF16)
    wd_sb = p.sbuf("wd_sb", [128, 2 * 1024], BF16)
    gs_sb = p.sbuf("gs_sb", [128, 512], BF16)
    hid_sb = p.sbuf("hid_sb", [128, 2 * 512], BF16)
    hld_sb = p.sbuf("hld_sb", [128, 2 * 512], BF16)
    hs_sb = p.sbuf("hs_sb", [128, 2 * 512], BF16)
    t_sb = p.sbuf("t_sb", [128, 512], F32)
    rb_sb = p.sbuf("rb_sb", [128, 512], F32)
    row_sb = p.sbuf("row_sb", [1, NTOK], F32)
    srow_sb = p.sbuf("srow_sb", [1, 2048], F32)
    rp_sb = p.sbuf("rp_sb", [1, 8], F32)
    xo_sb = p.sbuf("xo_sb", [128, 2 * 512], F32)

    PS = [p.psum(f"bank{i}", [128, 512]) for i in range(8)]

    def s_(i):
        return srow_sb[0:1, i * 512:(i + 1) * 512]

    LD_C = p.dma("sync", [
        lambda e: e.dma_start(out=onesf_sb[:, :], in_=onesf_h.ap()),
        lambda e: e.dma_start(out=g2c_sb[:, :], in_=g2c_h.ap()),
        lambda e: e.dma_start(out=ghc_sb[:, :], in_=ghc_h.ap()),
        lambda e: e.dma_start(out=iota_sb[:, :], in_=iota_h.ap()),
        lambda e: e.dma_start(out=rp_sb[0:1, :], in_=rpv_h.ap()),
    ])
    LG_C = p.dma("gpsimd", lambda e: e.dma_start(out=onesb_sb[:, :], in_=onesf_h.ap()))

    dv = p.op("vector", lambda e: e.reduce_sum(
        srow_sb[0:1, 1536:1537], rp_sb[0:1, 0:8], axis=AXX), waits=(LD_C,))
    dv = p.op("vector", lambda e: e.tensor_scalar_mul(
        srow_sb[0:1, 1536:1537], srow_sb[0:1, 1536:1537], 3072.0 / 16.0), waits=(dv,))
    dv = p.op("vector", lambda e: e.tensor_scalar_add(
        srow_sb[0:1, 1537:1538], srow_sb[0:1, 1536:1537], 2.0 ** 23 - 0.5), waits=(dv,))
    dv = p.op("vector", lambda e: e.tensor_scalar(
        srow_sb[0:1, 1536:1537], srow_sb[0:1, 1537:1538], 2.0 ** 23, None,
        op0=ALU.subtract), waits=(dv,))
    dv = p.op("vector", lambda e: e.tensor_scalar_max(
        srow_sb[0:1, 1536:1537], srow_sb[0:1, 1536:1537], 1.0), waits=(dv,))
    dv_sz = p.op("vector", lambda e: e.reciprocal(
        srow_sb[0:1, 1537:1538], srow_sb[0:1, 1536:1537]), waits=(dv,))
    pe_sz = p.op("tensor", lambda e: e.matmul(
        PS[0][:, 0:1], onesf_sb[0:1, :], srow_sb[0:1, 1536:1537],
        start=True, stop=True), waits=(dv_sz,))
    dv_szb = p.op("vector", lambda e: e.tensor_copy(szb_sb[:, :], PS[0][:, 0:1]),
                  waits=(pe_sz,))
    dv_mc = p.op("vector", lambda e: e.tensor_scalar(
        mcol_sb[:, :], iota_sb[:, :], szb_sb[:, 0:1], None, op0=ALU.is_lt),
        waits=(dv_szb,))

    prev_dv = None
    for b in range(BPC):
        for g in range(4):
            t0 = b * T + g * 512
            ldv = p.dma("sync", [
                (lambda e, ci=ci, t0=t0: e.dma_start(
                    out=vg_sb[:, ci * 512:(ci + 1) * 512],
                    in_=v_h.ap()[ci * 128:(ci + 1) * 128, t0:t0 + 512]))
                for ci in range(8)], waits=(prev_dv,))
            pe_ss = None
            for ci in range(8):
                ac_sq = p.op("scalar", lambda e, ci=ci: e.activation(
                    t_sb[:, :], vg_sb[:, ci * 512:(ci + 1) * 512], AF.Square),
                    waits=(ldv, pe_ss))
                pe_ss = p.op("tensor", lambda e, ci=ci: e.matmul(
                    PS[1][0:1, :], onesf_sb[:, 0:1], t_sb[:, :],
                    start=(ci == 0), stop=(ci == 7)), waits=(ac_sq,))
            dv = p.op("vector", lambda e: e.tensor_scalar_mul(
                s_(0), PS[1][0:1, :], 1.0 / 1024), waits=(pe_ss,))
            dv = p.op("vector", lambda e: e.tensor_scalar_add(s_(0), s_(0), 1e-6),
                      waits=(dv,))
            ac = p.op("scalar", lambda e: e.activation(s_(0), s_(0), AF.Sqrt),
                      waits=(dv,))
            dv = p.op("vector", lambda e: e.tensor_scalar_max(s_(0), s_(0), 1e-6),
                      waits=(ac,))
            dv = p.op("vector", lambda e: e.tensor_scalar_min(s_(0), s_(0), 1e6),
                      waits=(dv,))
            dv = p.op("vector", lambda e: e.reciprocal(s_(1), s_(0)), waits=(dv,))
            pe_b = p.op("tensor", lambda e: e.matmul(
                PS[2][:, :], onesf_sb[0:1, 0:128], s_(1), start=True, stop=True),
                waits=(dv,))
            dv_rb = p.op("vector", lambda e: e.tensor_copy(rb_sb[:, :], PS[2][:, :]),
                         waits=(pe_b,))
            for ci in range(8):
                prev_dv = p.op("vector", lambda e, ci=ci, t0=t0: e.scalar_tensor_tensor(
                    h2_sb[:, ci * 4096 + t0:ci * 4096 + t0 + 512],
                    vg_sb[:, ci * 512:(ci + 1) * 512], g2c_sb[:, ci:ci + 1],
                    rb_sb[:, :], op0=ALU.mult, op1=ALU.mult),
                    waits=(dv_rb,))
    dv_h2 = p.last("DV")

    dv_ssa = None
    st_hid = [None, None]
    ac_hid = [None, None]
    lg_w = [None, None]
    st_hid_all = []
    for dc in range(NDC):
        wsl = dc % 2
        lg_gu = p.dma("gpsimd", [
            (lambda e, dc=dc, wsl=wsl: e.dma_start(
                out=wg_sb[:, wsl * 1024:(wsl + 1) * 1024],
                in_=wgP_h.ap()[:, dc * 1024:(dc + 1) * 1024])),
            (lambda e, dc=dc, wsl=wsl: e.dma_start(
                out=wu_sb[:, wsl * 1024:(wsl + 1) * 1024],
                in_=wuP_h.ap()[:, dc * 1024:(dc + 1) * 1024])),
        ], waits=(lg_w[wsl],))
        pe_end = None
        for ts_ in range(8):
            tA = ts_ * 512
            bsl = ts_ % 2
            bankG = PS[2 + bsl]
            bankU = PS[4 + bsl]
            pe_g = None
            for ci in range(8):
                pe_g = p.op("tensor", lambda e, ci=ci, tA=tA, wsl=wsl, bankG=bankG: e.matmul(
                    bankG[:, :], wg_sb[:, wsl * 1024 + ci * 128:wsl * 1024 + (ci + 1) * 128],
                    h2_sb[:, ci * 4096 + tA:ci * 4096 + tA + 512],
                    start=(ci == 0), stop=(ci == 7)),
                    waits=((lg_gu, dv_h2, p.last("DV"), p.last("AC"))
                           if ci == 0 else ()))
            pe_u = None
            for ci in range(8):
                pe_u = p.op("tensor", lambda e, ci=ci, tA=tA, wsl=wsl, bankU=bankU: e.matmul(
                    bankU[:, :], wu_sb[:, wsl * 1024 + ci * 128:wsl * 1024 + (ci + 1) * 128],
                    h2_sb[:, ci * 4096 + tA:ci * 4096 + tA + 512],
                    start=(ci == 0), stop=(ci == 7)))
            ac_g = p.op("scalar", lambda e, bankG=bankG: e.activation(
                gs_sb[:, :], bankG[:, :], AF.Silu), waits=(pe_g, p.last("DV")))
            hsl = ts_ % 2
            dv_hd = p.op("vector", lambda e, dc=dc, hsl=hsl, bankU=bankU: e.scalar_tensor_tensor(
                hid_sb[:, hsl * 512:(hsl + 1) * 512], bankU[:, :],
                mcol_sb[:, dc:dc + 1], gs_sb[:, :], op0=ALU.mult, op1=ALU.mult),
                waits=(pe_u, ac_g, dv_mc, st_hid[hsl], ac_hid[hsl]))
            st_hid[hsl] = p.dma("sync", lambda e, dc=dc, tA=tA, hsl=hsl: e.dma_start(
                out=hid_h.ap()[dc * 128:(dc + 1) * 128, tA:tA + 512],
                in_=hid_sb[:, hsl * 512:(hsl + 1) * 512]),
                waits=(dv_hd,))
            st_hid_all.append(st_hid[hsl])
            ac_hid[hsl] = ac_pa = p.op("scalar", lambda e, hsl=hsl: e.activation(
                t_sb[:, :], hid_sb[:, hsl * 512:(hsl + 1) * 512], AF.Square),
                waits=(dv_hd, pe_end))
            pe_sq = p.op("tensor", lambda e: e.matmul(
                PS[6][0:1, :], onesf_sb[:, 0:1], t_sb[:, :],
                start=True, stop=True), waits=(ac_pa, dv_ssa))
            if dc == 0:
                dv_ssa = p.op("vector", lambda e, tA=tA: e.tensor_copy(
                    row_sb[0:1, tA:tA + 512], PS[6][0:1, :]), waits=(pe_sq,))
            else:
                dv_ssa = p.op("vector", lambda e, tA=tA: e.tensor_add(
                    row_sb[0:1, tA:tA + 512], row_sb[0:1, tA:tA + 512], PS[6][0:1, :]),
                    waits=(pe_sq,))
            pe_end = pe_sq
        lg_w[wsl] = pe_end
    dv_ssa_all = p.last("DV")
    st_hid_all = st_hid_all[-12:]

    dv_rr = None
    for gseg in range(8):
        tA = gseg * 512
        dv = p.op("vector", lambda e, tA=tA: e.tensor_scalar(
            row_sb[0:1, tA:tA + 512], row_sb[0:1, tA:tA + 512],
            srow_sb[0:1, 1537:1538], None, op0=ALU.mult), waits=(dv_ssa_all,))
        dv = p.op("vector", lambda e, tA=tA: e.tensor_scalar_add(
            row_sb[0:1, tA:tA + 512], row_sb[0:1, tA:tA + 512], 1e-6), waits=(dv,))
        ac = p.op("scalar", lambda e, tA=tA: e.activation(
            row_sb[0:1, tA:tA + 512], row_sb[0:1, tA:tA + 512], AF.Sqrt), waits=(dv,))
        dv = p.op("vector", lambda e, tA=tA: e.tensor_scalar_max(
            row_sb[0:1, tA:tA + 512], row_sb[0:1, tA:tA + 512], 1e-6), waits=(ac,))
        dv = p.op("vector", lambda e, tA=tA: e.tensor_scalar_min(
            row_sb[0:1, tA:tA + 512], row_sb[0:1, tA:tA + 512], 1e6), waits=(dv,))
        dv_rr = p.op("vector", lambda e, tA=tA: e.reciprocal(
            row_sb[0:1, tA:tA + 512], row_sb[0:1, tA:tA + 512]), waits=(dv,))

    all_st = p.all_dma()
    st_x = [None, None]
    dv_o2 = [None, None]
    for gseg in range(8):
        tA = gseg * 512
        pe_bc = p.op("tensor", lambda e, tA=tA: e.matmul(
            PS[0][:, :], onesf_sb[0:1, 0:128], row_sb[0:1, tA:tA + 512],
            start=True, stop=True),
            waits=(dv_rr, p.last("DV"), p.last("AC")))
        dv_rb2 = p.op("vector", lambda e: e.tensor_copy(rb_sb[:, :], PS[0][:, :]),
                      waits=(pe_bc,))
        pe_y = None
        dv_hs_prev = dv_rb2
        ld_hd = [None, None]
        lg_wd = [None, None]
        for dc in range(NDC):
            wsl = dc % 2
            ld_hd[wsl] = p.dma("sync", lambda e, dc=dc, tA=tA, wsl=wsl: e.dma_start(
                out=hld_sb[:, wsl * 512:(wsl + 1) * 512],
                in_=hid_h.ap()[dc * 128:(dc + 1) * 128, tA:tA + 512]),
                waits=(all_st if (dc < 2 and gseg == 0) else (dv_hs_prev,)))
            lg_wd[wsl] = p.dma("gpsimd", lambda e, dc=dc, wsl=wsl: e.dma_start(
                out=wd_sb[:, wsl * 1024:(wsl + 1) * 1024],
                in_=wd_h.ap()[dc * 128:(dc + 1) * 128, :]),
                waits=((pe_y,) if dc >= 2 else (p.last("PE"),)))
            dv_hs = p.op("vector", lambda e, dc=dc, wsl=wsl: e.scalar_tensor_tensor(
                hs_sb[:, wsl * 512:(wsl + 1) * 512], hld_sb[:, wsl * 512:(wsl + 1) * 512],
                ghc_sb[:, dc:dc + 1], rb_sb[:, :], op0=ALU.mult, op1=ALU.mult),
                waits=(ld_hd[wsl], dv_rb2, pe_y))
            for do in range(8):
                pe_y = p.op("tensor", lambda e, do=do, dc=dc, wsl=wsl: e.matmul(
                    PS[do][:, :], wd_sb[:, wsl * 1024 + do * 128:wsl * 1024 + (do + 1) * 128],
                    hs_sb[:, wsl * 512:(wsl + 1) * 512],
                    start=(dc == 0), stop=(dc == NDC - 1)),
                    waits=((dv_hs, lg_wd[wsl]) if do == 0 else ()))
            dv_hs_prev = dv_hs
        for do in range(8):
            osl = do % 2
            ldx = p.dma("sync", [
                (lambda e, do=do, tA=tA, osl=osl: e.dma_start(
                    out=xo_sb[:, osl * 512:(osl + 1) * 512],
                    in_=x_h.ap()[do * 128:(do + 1) * 128, tA:tA + 512])),
                (lambda e, do=do, tA=tA: e.dma_start(
                    out=vg_sb[:, do * 512:(do + 1) * 512],
                    in_=v_h.ap()[do * 128:(do + 1) * 128, tA:tA + 512])),
            ], waits=(st_x[osl], dv_o2[osl],
                      (p.last("DV") if (gseg == 0 and do < 2) else None)))
            dv_o = p.op("vector", lambda e, do=do, osl=osl: e.tensor_add(
                xo_sb[:, osl * 512:(osl + 1) * 512], xo_sb[:, osl * 512:(osl + 1) * 512],
                vg_sb[:, do * 512:(do + 1) * 512]),
                waits=(ldx, pe_y))
            dv_o2[osl] = dv_o = p.op("vector", lambda e, do=do, osl=osl: e.tensor_add(
                xo_sb[:, osl * 512:(osl + 1) * 512], xo_sb[:, osl * 512:(osl + 1) * 512],
                PS[do][:, :]), waits=(dv_o,))
            st_x[osl] = p.dma("sync", lambda e, do=do, tA=tA, osl=osl: e.dma_start(
                out=x1_h.ap()[do * 128:(do + 1) * 128, tA:tA + 512],
                in_=xo_sb[:, osl * 512:(osl + 1) * 512]),
                waits=(dv_o,))
    p.wait("sync", p.all_dma())
    p.stage_end()
    return p.finalize() if own else None

def prep_B1_inputs(inputs, x_shards, v_shards, rparts):
    wg = np.asarray(inputs["L0_attn_w_gate"])
    wu = np.asarray(inputs["L0_attn_w_up"])
    gh = np.asarray(inputs["L0_attn_g_hidden"])
    g2 = np.asarray(inputs["L0_attn_g2"])
    NDC = 36
    def pmaj(w):
        r = w.reshape(8, 128, NDC, 128).transpose(1, 2, 0, 3).reshape(128, NDC * 1024)
        return np.ascontiguousarray(r)
    base = {
        "g2c": np.ascontiguousarray(g2.reshape(8, 128).T),
        "wgP": pmaj(wg), "wuP": pmaj(wu),
        "w_down": np.ascontiguousarray(inputs["L0_attn_w_down"]),
        "ghc": np.ascontiguousarray(gh.reshape(NDC, 128).T),
        "iotaF": np.ascontiguousarray(
            np.arange(4608, dtype=np.float32).reshape(NDC, 128).T),
        "ones_f": CONSTS["ones_f"],
        "rparts": np.ascontiguousarray(rparts.astype(np.float32).reshape(1, 8)),
    }
    maps = []
    for c in range(NCORE):
        maps.append({"x": x_shards[c], "V": v_shards[c], **base})
    return maps


def build_B2(p=None, elist=None, halves=(0, 1)):
    own = p is None
    if own:
        p = Prog()
    nc = p.nc
    p.stage_begin("b2")
    NE = 18
    if elist is None:
        elist = list(range(NE))

    x1_h = p.dram("X1", [D, NTOK], F32, kind="ExternalInput")
    gwt_h = p.dram("gwtP", [128, 128], F32, kind="ExternalInput")
    ebr_h = p.dram("ebr", [1, 16], F32, kind="ExternalInput")
    w1P_h = p.dram("w1P", [128, NE * 4096], F32, kind="ExternalInput")
    w3P_h = p.dram("w3P", [128, NE * 4096], F32, kind="ExternalInput")
    w2P_h = p.dram("w2P", [128, NE * 4096], F32, kind="ExternalInput")
    b1c_h = p.dram("b1c", [128, NE * 4], F32, kind="ExternalInput")
    b3c_h = p.dram("b3c", [128, NE * 4], F32, kind="ExternalInput")
    b2m_h = p.dram("b2m", [16, D], F32, kind="ExternalInput")
    sb2c_h = p.dram("sb2c", [128, 8], F32, kind="ExternalInput")
    oh_h = p.dram("ohall", [16, 16 * 128], F32, kind="ExternalInput")
    onesf_h = p.dram("ones_f", [128, 128], F32, kind="ExternalInput")
    ident_h = p.dram("ident", [128, 128], F32, kind="ExternalInput")

    x2_h = p.dram("X2", [D, NTOK], F32, kind="ExternalOutput")
    dbg_gw_h = p.dram("DBG_gwT", [16, NTOK], F32, kind="ExternalOutput")

    onesf_sb = p.sbuf("onesf_sb", [128, 128], F32)
    onesb_sb = p.sbuf("onesb_sb", [128, 128], BF16)
    identb_sb = p.sbuf("identb_sb", [128, 128], BF16)
    gwt_sb = p.sbuf("gwt_sb", [128, 128], BF16)
    ebb_sb = p.sbuf("ebb_sb", [128, 16], F32)
    b1c_sb = p.sbuf("b1c_sb", [128, NE * 4], F32)
    b3c_sb = p.sbuf("b3c_sb", [128, NE * 4], F32)
    b2m_sb = p.sbuf("b2m_sb", [16, D], BF16)
    sb2c_sb = p.sbuf("sb2c_sb", [128, 8], F32)
    oh_sb = p.sbuf("oh_sb", [16, 16 * 128], BF16)
    xf_sb = p.sbuf("xf_sb", [128, 8 * NTOK], BF16)
    y_sb = p.sbuf("y_sb", [128, 8 * 2048], F32)
    w1_sb = p.sbuf("w1_sb", [128, 4096], BF16)
    w3_sb = p.sbuf("w3_sb", [128, 4096], BF16)
    w2_sb = p.sbuf("w2_sb", [128, 4096], BF16)
    gs_sb = p.sbuf("gs_sb", [128, 512], BF16)
    ub_sb = p.sbuf("ub_sb", [128, 512], BF16)
    hsc_sb = p.sbuf("hsc_sb", [128, 4 * 512], BF16)
    gbb_sb = p.sbuf("gbb_sb", [128, 512], BF16)
    lgt_sb = p.sbuf("lgt_sb", [128, 16 * 4], F32)
    gw_sb = p.sbuf("gw_sb", [128, 16], BF16)
    gwT_sb = p.sbuf("gwT_sb", [16, NTOK], BF16)
    srow_sb = p.sbuf("srow_sb", [1, 2048], F32)
    rowb_sb = p.sbuf("rowb_sb", [1, 512], BF16)
    rb_sb = p.sbuf("rb_sb", [128, 512], F32)
    t_sb = p.sbuf("t_sb", [128, 512], F32)
    xg_sb = p.sbuf("xg_sb", [128, 8 * 512], F32)
    m2_sb = p.sbuf("m2_sb", [128, 4], F32)

    PS = [p.psum(f"bank{i}", [128, 512]) for i in range(8)]
    PSB = [b_[:, :].bitcast(BF16) for b_ in PS]

    def s_(i):
        return srow_sb[0:1, i * 512:(i + 1) * 512]

    LD_C = p.dma("sync", [
        lambda e: e.dma_start(out=onesf_sb[:, :], in_=onesf_h.ap()),
        lambda e: e.dma_start(out=b1c_sb[:, :], in_=b1c_h.ap()),
        lambda e: e.dma_start(out=b3c_sb[:, :], in_=b3c_h.ap()),
        lambda e: e.dma_start(out=sb2c_sb[:, :], in_=sb2c_h.ap()),
        lambda e: e.dma_start(out=srow_sb[0:1, 0:16], in_=ebr_h.ap()),
    ])
    LG_C = p.dma("gpsimd", [
        lambda e: e.dma_start(out=onesb_sb[:, :], in_=onesf_h.ap()),
        lambda e: e.dma_start(out=identb_sb[:, :], in_=ident_h.ap()),
        lambda e: e.dma_start(out=b2m_sb[:, :], in_=b2m_h.ap()),
        lambda e: e.dma_start(out=oh_sb[:, :], in_=oh_h.ap()),
        lambda e: e.dma_start(out=gwt_sb[:, :], in_=gwt_h.ap()),
    ])
    pe_eb = p.op("tensor", lambda e: e.matmul(
        PS[0][:, 0:16], onesf_sb[0:1, 0:128], srow_sb[0:1, 0:16],
        start=True, stop=True), waits=(LD_C,))
    dv_eb = p.op("vector", lambda e: e.tensor_copy(ebb_sb[:, :], PS[0][:, 0:16]),
                 waits=(pe_eb,))

    prev_dv = None
    for b in range(BPC):
        for g in range(4):
            t0 = b * T + g * 512
            ldx = p.dma("sync", [
                (lambda e, ci=ci, t0=t0: e.dma_start(
                    out=xg_sb[:, ci * 512:(ci + 1) * 512],
                    in_=x1_h.ap()[ci * 128:(ci + 1) * 128, t0:t0 + 512]))
                for ci in range(8)], waits=(prev_dv,))
            pe_sx = None
            for ci in range(8):
                pe_sx = p.op("tensor", lambda e, ci=ci: e.matmul(
                    PS[1][0:1, :], onesf_sb[:, 0:1], xg_sb[:, ci * 512:(ci + 1) * 512],
                    start=(ci == 0), stop=(ci == 7)),
                    waits=((ldx,) if ci == 0 else ()))
            pe_ssq = pe_sx
            for ci in range(8):
                ac_sq = p.op("scalar", lambda e, ci=ci: e.activation(
                    t_sb[:, :], xg_sb[:, ci * 512:(ci + 1) * 512], AF.Square),
                    waits=(ldx, pe_ssq))
                pe_ssq = p.op("tensor", lambda e, ci=ci: e.matmul(
                    PS[2][0:1, :], onesf_sb[:, 0:1], t_sb[:, :],
                    start=(ci == 0), stop=(ci == 7)), waits=(ac_sq,))
            dv = p.op("vector", lambda e: e.tensor_scalar_mul(
                s_(0), PS[1][0:1, :], 1.0 / 1024), waits=(pe_sx, pe_ssq))
            dv = p.op("vector", lambda e: e.tensor_scalar_mul(
                s_(1), PS[2][0:1, :], 1.0 / 1024), waits=(dv,))
            dv = p.op("vector", lambda e: e.tensor_mul(s_(2), s_(0), s_(0)), waits=(dv,))
            dv = p.op("vector", lambda e: e.tensor_sub(s_(1), s_(1), s_(2)), waits=(dv,))
            dv = p.op("vector", lambda e: e.tensor_scalar_add(s_(1), s_(1), 1e-5),
                      waits=(dv,))
            ac = p.op("scalar", lambda e: e.activation(s_(1), s_(1), AF.Sqrt), waits=(dv,))
            dv = p.op("vector", lambda e: e.reciprocal(s_(1), s_(1)), waits=(ac,))
            dv = p.op("vector", lambda e: e.tensor_mul(s_(2), s_(0), s_(1)), waits=(dv,))
            dv_mr = p.op("vector", lambda e: e.tensor_copy(rowb_sb[0:1, :], s_(2)),
                         waits=(dv,))
            pe_b = p.op("tensor", lambda e: e.matmul(
                PS[3][:, :], onesb_sb[0:1, :], rowb_sb[0:1, :],
                start=True, stop=True), waits=(dv_mr, LG_C))
            pe_rs = p.op("tensor", lambda e: e.matmul(
                PS[4][:, :], onesf_sb[0:1, 0:128], s_(1), start=True, stop=True),
                waits=(dv_mr,))
            dv_rb = p.op("vector", lambda e: e.tensor_copy(rb_sb[:, :], PS[4][:, :]),
                         waits=(pe_rs,))
            for ci in range(8):
                dvt = p.op("vector", lambda e, ci=ci: e.tensor_mul(
                    t_sb[:, :], xg_sb[:, ci * 512:(ci + 1) * 512], rb_sb[:, :]),
                    waits=(dv_rb, pe_ssq, prev_dv))
                prev_dv = p.op("vector", lambda e, ci=ci, t0=t0: e.tensor_sub(
                    xf_sb[:, ci * 4096 + t0:ci * 4096 + t0 + 512], t_sb[:, :],
                    PS[3][:, :]), waits=(dvt, pe_b))
    dv_xf = p.last("DV")

    dv_gw = None
    for mt in range(32):
        tm = mt * 128
        tsl = mt % 4
        L = lgt_sb[:, tsl * 16:(tsl + 1) * 16]
        pe_l = None
        for ci in range(8):
            pe_l = p.op("tensor", lambda e, ci=ci, tm=tm: e.matmul(
                PS[5][0:128, 0:16], xf_sb[:, ci * 4096 + tm:ci * 4096 + tm + 128],
                gwt_sb[:, ci * 16:(ci + 1) * 16], start=(ci == 0), stop=(ci == 7)),
                waits=((dv_xf, LG_C, p.last("DV")) if ci == 0 else ()))
        dv = p.op("vector", lambda e, L=L: e.tensor_copy(L, PS[5][0:128, 0:16]),
                  waits=(pe_l,))
        ac = p.op("scalar", lambda e, L=L: e.activation(t_sb[:, 0:16], L, AF.Sigmoid),
                  waits=(dv,))
        dv = p.op("vector", lambda e, L=L: e.tensor_add(L, L, ebb_sb[:, :]),
                  waits=(dv, dv_eb, ac))
        dv = p.op("vector", lambda e, L=L: e.reduce_max(m2_sb[:, 0:1], L, axis=AXX),
                  waits=(dv,))
        dv = p.op("vector", lambda e, L=L: e.tensor_scalar(
            t_sb[:, 16:32], L, m2_sb[:, 0:1], None, op0=ALU.is_ge), waits=(dv,))
        dv = p.op("vector", lambda e, L=L: e.scalar_tensor_tensor(
            t_sb[:, 32:48], t_sb[:, 16:32], -1e30, L, op0=ALU.mult, op1=ALU.add),
            waits=(dv,))
        dv = p.op("vector", lambda e: e.reduce_max(m2_sb[:, 1:2], t_sb[:, 32:48],
                                                   axis=AXX), waits=(dv,))
        dv = p.op("vector", lambda e, L=L: e.tensor_scalar(
            t_sb[:, 16:32], L, m2_sb[:, 1:2], None, op0=ALU.is_ge), waits=(dv,))
        dv = p.op("vector", lambda e: e.tensor_mul(
            t_sb[:, 48:64], t_sb[:, 16:32], t_sb[:, 0:16]), waits=(dv,))
        dv = p.op("vector", lambda e: e.reduce_sum(m2_sb[:, 2:3], t_sb[:, 48:64],
                                                   axis=AXX), waits=(dv,))
        dv = p.op("vector", lambda e: e.tensor_scalar_add(
            m2_sb[:, 2:3], m2_sb[:, 2:3], 1e-10), waits=(dv,))
        dv = p.op("vector", lambda e: e.reciprocal(m2_sb[:, 3:4], m2_sb[:, 2:3]),
                  waits=(dv,))
        dv = p.op("vector", lambda e: e.tensor_scalar(
            gw_sb[:, :], t_sb[:, 48:64], m2_sb[:, 3:4], None, op0=ALU.mult),
            waits=(dv,))
        pe_t = p.op("tensor", lambda e: e.transpose(
            PSB[6][0:16, 0:128], gw_sb[:, :], identb_sb[:, :]),
            waits=(dv, dv_gw))
        dv_gw = p.op("vector", lambda e, tm=tm: e.tensor_copy(
            gwT_sb[:, tm:tm + 128], PSB[6][0:16, 0:128]), waits=(pe_t,))
    dv_gw_all = p.last("DV")
    for seg in range(2):
        p.dma("gpsimd", lambda e, seg=seg: e.dma_start(
            out=dbg_gw_h.ap()[:, seg * 2048:(seg + 1) * 2048],
            in_=gwT_sb[:, seg * 2048:(seg + 1) * 2048]),
            waits=(dv_gw_all,))

    st_x2_all = (None,)
    for half in halves:
        tH0 = half * 2048
        ld_y = p.dma("sync", [
            (lambda e, do=do, tH0=tH0: e.dma_start(
                out=y_sb[:, do * 2048:(do + 1) * 2048],
                in_=x1_h.ap()[do * 128:(do + 1) * 128, tH0:tH0 + 2048]))
            for do in range(8)],
            waits=(p.last("DV"),) + st_x2_all)
        pe_yend = p.last("PE")
        dv_yend = None
        lg_w = None
        for e_ in elist:
            lg_w = p.dma("gpsimd", [
                (lambda e, e_=e_: e.dma_start(
                    out=w1_sb[:, :], in_=w1P_h.ap()[:, e_ * 4096:(e_ + 1) * 4096])),
                (lambda e, e_=e_: e.dma_start(
                    out=w3_sb[:, :], in_=w3P_h.ap()[:, e_ * 4096:(e_ + 1) * 4096])),
                (lambda e, e_=e_: e.dma_start(
                    out=w2_sb[:, :], in_=w2P_h.ap()[:, e_ * 4096:(e_ + 1) * 4096])),
            ], waits=(pe_yend,))
            for ts_ in range(4):
                tA = tH0 + ts_ * 512
                if e_ < 16:
                    pe_gb = p.op("tensor", lambda e, e_=e_, tA=tA: e.matmul(
                        PS[6][:, :], oh_sb[0:16, e_ * 128:(e_ + 1) * 128],
                        gwT_sb[:, tA:tA + 512], start=True, stop=True),
                        waits=(dv_gw_all, p.last("DV"), ld_y))
                    dv_gb = p.op("vector", lambda e: e.tensor_copy(
                        gbb_sb[:, :], PS[6][:, :]), waits=(pe_gb,))
                else:
                    dv_gb = None
                dv_hs = None
                for fc in range(4):
                    bG = PS[0] if fc % 2 == 0 else PS[1]
                    bU = PS[2] if fc % 2 == 0 else PS[3]
                    pe_g = None
                    for ci in range(8):
                        pe_g = p.op("tensor", lambda e, ci=ci, fc=fc, tA=tA, bG=bG: e.matmul(
                            bG[:, :], w1_sb[:, ci * 512 + fc * 128:ci * 512 + (fc + 1) * 128],
                            xf_sb[:, ci * 4096 + tA:ci * 4096 + tA + 512],
                            start=(ci == 0), stop=(ci == 7)),
                            waits=((lg_w, ld_y, p.last("DV"), p.last("AC"))
                                   if ci == 0 else ()))
                    pe_u = None
                    for ci in range(8):
                        pe_u = p.op("tensor", lambda e, ci=ci, fc=fc, tA=tA, bU=bU: e.matmul(
                            bU[:, :], w3_sb[:, ci * 512 + fc * 128:ci * 512 + (fc + 1) * 128],
                            xf_sb[:, ci * 4096 + tA:ci * 4096 + tA + 512],
                            start=(ci == 0), stop=(ci == 7)))
                    ac_g = p.op("scalar", lambda e, e_=e_, fc=fc, bG=bG: e.activation(
                        gs_sb[:, :], bG[:, :], AF.Silu,
                        bias=b1c_sb[:, e_ * 4 + fc:e_ * 4 + fc + 1]),
                        waits=(pe_g, dv_hs))
                    if e_ < 16:
                        dv_u = p.op("vector", lambda e, e_=e_, fc=fc, bU=bU: e.scalar_tensor_tensor(
                            ub_sb[:, :], bU[:, :], b3c_sb[:, e_ * 4 + fc:e_ * 4 + fc + 1],
                            gbb_sb[:, :], op0=ALU.add, op1=ALU.mult),
                            waits=(pe_u, dv_gb, dv_hs))
                    else:
                        dv_u = p.op("vector", lambda e, e_=e_, fc=fc, bU=bU: e.tensor_scalar(
                            ub_sb[:, :], bU[:, :], b3c_sb[:, e_ * 4 + fc:e_ * 4 + fc + 1],
                            None, op0=ALU.add), waits=(pe_u, dv_hs))
                    dv_hs = p.op("vector", lambda e, fc=fc: e.tensor_mul(
                        hsc_sb[:, fc * 512:(fc + 1) * 512], gs_sb[:, :], ub_sb[:, :]),
                        waits=(dv_u, ac_g))
                for do in range(8):
                    pe_y = None
                    for fc in range(4):
                        pe_y = p.op("tensor", lambda e, do=do, fc=fc, e_=e_: e.matmul(
                            PS[do][:, :], w2_sb[:, fc * 1024 + do * 128:fc * 1024 + (do + 1) * 128],
                            hsc_sb[:, fc * 512:(fc + 1) * 512],
                            start=(fc == 0), stop=(fc == 3 and e_ != elist[0])),
                            waits=((dv_hs, dv_yend) if fc == 0 else ()))
                    if e_ == elist[0]:
                        pe_y = p.op("tensor", lambda e, do=do, tA=tA: e.matmul(
                            PS[do][:, :], b2m_sb[0:16, do * 128:(do + 1) * 128],
                            gwT_sb[:, tA:tA + 512], start=False, stop=True),
                            waits=(LG_C,))
                    pe_yend = pe_y
                for do in range(8):
                    dv_yend = p.op("vector", lambda e, do=do, ts_=ts_: e.tensor_add(
                        y_sb[:, do * 2048 + ts_ * 512:do * 2048 + (ts_ + 1) * 512],
                        y_sb[:, do * 2048 + ts_ * 512:do * 2048 + (ts_ + 1) * 512],
                        PS[do][:, :]),
                        waits=((pe_yend, ld_y) if do == 0 else (dv_yend,)))
        st_list = []
        st_slot = [None, None]
        for do in range(8):
            for ts_ in range(4):
                ssl = (do * 4 + ts_) % 2
                dv_o = p.op("vector", lambda e, do=do, ts_=ts_, ssl=ssl: e.tensor_scalar(
                    xg_sb[:, ssl * 512:(ssl + 1) * 512],
                    y_sb[:, do * 2048 + ts_ * 512:do * 2048 + (ts_ + 1) * 512],
                    sb2c_sb[:, do:do + 1], None, op0=ALU.add),
                    waits=(dv_yend, st_slot[ssl]))
                st_slot[ssl] = p.dma("sync", lambda e, do=do, ts_=ts_, tH0=tH0, ssl=ssl: e.dma_start(
                    out=x2_h.ap()[do * 128:(do + 1) * 128,
                                  tH0 + ts_ * 512:tH0 + (ts_ + 1) * 512],
                    in_=xg_sb[:, ssl * 512:(ssl + 1) * 512]), waits=(dv_o,))
                st_list.append(st_slot[ssl])
        st_x2_all = tuple(st_list[-10:])
    p.wait("sync", p.all_dma())
    p.stage_end()
    return p.finalize() if own else None

def build_B3(p=None):
    own = p is None
    if own:
        p = Prog()
    nc = p.nc
    p.stage_begin("b3")

    x2_h = p.dram("X2", [D, NTOK], F32, kind="ExternalInput")
    wqP_h = p.dram("WqkvP", [128, 24 * 1024], F32, kind="ExternalInput")
    bqc_h = p.dram("bqkvc", [128, 24], F32, kind="ExternalInput")
    wpP_h = p.dram("WprojP", [128, 8 * 1024], F32, kind="ExternalInput")
    bpc_h = p.dram("bprojc", [128, 8], F32, kind="ExternalInput")
    wb_h = p.dram("wb_lhs", [128, 16 * 128], F32, kind="ExternalInput")
    w2c2_h = p.dram("w2c2", [64, 16], F32, kind="ExternalInput")
    fb2c_h = p.dram("fb2c", [16, 1], F32, kind="ExternalInput")
    cpar_h = p.dram("cpar", [1, 1], F32, kind="ExternalInput")
    lmul_h = p.dram("lmul", [1, 1], F32, kind="ExternalInput")
    onesf_h = p.dram("ones_f", [128, 128], F32, kind="ExternalInput")
    ident_h = p.dram("ident", [128, 128], F32, kind="ExternalInput")

    x3_h = p.dram("X3", [D, NTOK], F32, kind="ExternalOutput")
    bias_h = p.dram("biasB3", [NH * T, T], BF16)
    q_hbm = p.dram("qB3", [D, T], BF16)
    ao_hbm = p.dram("aoB3", [D, T], BF16)

    onesf_sb = p.sbuf("onesf_sb", [128, 128], F32)
    identb_sb = p.sbuf("identb_sb", [128, 128], BF16)
    wb_sb = p.sbuf("wb_sb", [128, 16 * 128], BF16)
    w2c2_sb = p.sbuf("w2c2_sb", [64, 16], BF16)
    fb2c_sb = p.sbuf("fb2c_sb", [16, 1], F32)
    bqc_sb = p.sbuf("bqc_sb", [128, 24], F32)
    bpc_sb = p.sbuf("bpc_sb", [128, 8], F32)
    sc_sb = p.sbuf("sc_sb", [16, 8], F32)
    row1_sb = p.sbuf("row1_sb", [1, 8], F32)
    tc16_sb = p.sbuf("tc16_sb", [16, 128], F32)
    ctc_sb = p.sbuf("ctc_sb", [16, 128], F32)
    nd_f = p.sbuf("nd_f", [16, 512], F32)
    nd_t = p.sbuf("nd_t", [128, 512], BF16)
    hA_sb = p.sbuf("hA_sb", [64, 512], BF16)
    hB_sb = p.sbuf("hB_sb", [64, 512], BF16)
    stage_sb = p.sbuf("stage_sb", [16, 16 * 512], BF16)
    wq_sb = p.sbuf("wq_sb", [128, 2 * 1024], BF16)
    wp_sb = p.sbuf("wp_sb", [128, 8 * 1024], BF16)
    xg_sb = p.sbuf("xg_sb", [128, 8 * 512], F32)
    xln_sb = p.sbuf("xln_sb", [128, 8 * 512], BF16)
    kvt_sb = p.sbuf("kvt_sb", [128, 16 * 2048], BF16)
    qth_sb = p.sbuf("qth_sb", [128, 2048], BF16)
    vtm_sb = p.sbuf("vtm_sb", [128, 16 * 64], BF16)
    biast_sb = p.sbuf("biast_sb", [128, 2 * 512], BF16)
    sbrow_sb = p.sbuf("sbrow_sb", [128, 2048], F32)
    pn_sb = p.sbuf("pn_sb", [128, 2048], BF16)
    ptb_sb = p.sbuf("ptb_sb", [128, 2 * 128], BF16)
    aost_sb = p.sbuf("aost_sb", [64, 512], BF16)
    ao_sb = p.sbuf("ao_sb", [128, 8 * 512], BF16)
    srow_sb = p.sbuf("srow_sb", [1, 2048], F32)
    rowb_sb = p.sbuf("rowb_sb", [1, 512], BF16)
    rb_sb = p.sbuf("rb_sb", [128, 512], F32)
    t_sb = p.sbuf("t_sb", [128, 512], F32)
    stg_sb = p.sbuf("stg_sb", [128, 2 * 512], BF16)
    xo_sb = p.sbuf("xo_sb", [128, 2 * 512], F32)
    mcol_sb = p.sbuf("mcol_sb", [128, 4], F32)

    PS = [p.psum(f"bank{i}", [128, 512]) for i in range(8)]
    PSB = [b_[:, :].bitcast(BF16) for b_ in PS]

    def s_(i):
        return srow_sb[0:1, i * 512:(i + 1) * 512]

    LD_C = p.dma("sync", [
        lambda e: e.dma_start(out=onesf_sb[:, :], in_=onesf_h.ap()),
        lambda e: e.dma_start(out=fb2c_sb[:, :], in_=fb2c_h.ap()),
        lambda e: e.dma_start(out=bqc_sb[:, :], in_=bqc_h.ap()),
        lambda e: e.dma_start(out=bpc_sb[:, :], in_=bpc_h.ap()),
        lambda e: e.dma_start(out=row1_sb[0:1, 0:1], in_=cpar_h.ap()),
        lambda e: e.dma_start(out=row1_sb[0:1, 1:2], in_=lmul_h.ap()),
    ])
    LG_C = p.dma("gpsimd", [
        lambda e: e.dma_start(out=identb_sb[:, :], in_=ident_h.ap()),
        lambda e: e.dma_start(out=wb_sb[:, :], in_=wb_h.ap()),
        lambda e: e.dma_start(out=w2c2_sb[:, :], in_=w2c2_h.ap()),
    ])

    ac = p.op("scalar", lambda e: e.activation(
        row1_sb[0:1, 2:3], row1_sb[0:1, 0:1], AF.Abs), waits=(LD_C,))
    ac = p.op("scalar", lambda e: e.activation(
        row1_sb[0:1, 3:4], row1_sb[0:1, 1:2], AF.Abs, scale=512.0), waits=(ac,))
    pe = p.op("tensor", lambda e: e.matmul(
        PS[0][0:16, 0:2], onesf_sb[0:1, 0:16], row1_sb[0:1, 2:4],
        start=True, stop=True), waits=(ac,))
    dv_sc = p.op("vector", lambda e: e.tensor_copy(sc_sb[:, 0:2], PS[0][0:16, 0:2]),
                 waits=(pe,))
    gp = p.op("gpsimd", lambda e: e.iota(
        tc16_sb[:, :], pattern=[[16, 128]], base=0, channel_multiplier=1,
        allow_small_or_imprecise_dtypes=True), waits=(LG_C,))
    dv = p.op("vector", lambda e: e.tensor_scalar(
        ctc_sb[:, :], tc16_sb[:, :], sc_sb[:, 1:2], None, op0=ALU.max),
        waits=(gp, dv_sc))
    ac = p.op("scalar", lambda e: e.activation(
        ctc_sb[:, :], ctc_sb[:, :], AF.Ln, bias=1.0, scale=sc_sb[:, 0:1]),
        waits=(dv,))
    dv = p.op("vector", lambda e: e.tensor_scalar_add(ctc_sb[:, :], ctc_sb[:, :], 1e-6),
              waits=(ac,))
    dv_ct = p.op("vector", lambda e: e.reciprocal(ctc_sb[:, :], ctc_sb[:, :]),
                 waits=(dv,))
    dv_nd0 = p.op("vector", lambda e: e.memset(nd_t[:, :], 0.0), waits=(LG_C,))
    dv_nd0 = p.op("vector", lambda e: e.memset(nd_t[64:80, :], 1.0), waits=(dv_nd0,))

    st_stage = None
    dv_ev_last = None
    gp_last = None
    for st in range(128):
        t0q = st * 16
        nsc = t0q // 512 + 1
        for sc in range(nsc):
            s0 = sc * 512
            gp_nd = p.op("gpsimd", lambda e, t0q=t0q, s0=s0: e.iota(
                nd_f[:, :], pattern=[[-1, 512]], base=t0q - s0, channel_multiplier=1,
                allow_small_or_imprecise_dtypes=True),
                waits=(p.last("AC"), gp_last))
            ac_nd = p.op("scalar", lambda e: e.activation(
                nd_f[:, :], nd_f[:, :], AF.Relu, scale=sc_sb[0:16, 0:1]),
                waits=(gp_nd, dv_sc))
            ac_nd = p.op("scalar", lambda e: e.activation(
                nd_f[:, :], nd_f[:, :], AF.Ln, bias=1.0), waits=(ac_nd,))
            dv_nd = p.op("vector", lambda e, st=st: e.tensor_scalar(
                nd_t[0:16, :], nd_f[:, :], ctc_sb[:, st:st + 1], None, op0=ALU.mult),
                waits=(ac_nd, dv_ct, dv_nd0, p.last("PE")))
            gp_last = gp_nd
            for v in range(4):
                pe_h = p.op("tensor", lambda e, v=v: e.matmul(
                    PS[4][:, :], wb_sb[:, v * 128:(v + 1) * 128], nd_t[:, :],
                    start=True, stop=True),
                    waits=(dv_nd, p.last("AC")))
                ac_A = p.op("scalar", lambda e: e.activation(
                    hA_sb[:, :], PS[4][0:64, :], AF.Relu), waits=(pe_h,))
                ac_B = p.op("scalar", lambda e: e.activation(
                    hB_sb[:, :], PS[4][64:128, :], AF.Relu), waits=(pe_h,))
                for j in range(4):
                    hsrc = hA_sb if j < 2 else hB_sb
                    base = (j % 2) * 32
                    t_loc = 4 * v + j
                    tg = t0q + t_loc
                    pe_e = p.op("tensor", lambda e, hsrc=hsrc, base=base: e.matmul(
                        PS[5][0:16, :], w2c2_sb[base:base + 32, :],
                        hsrc[base:base + 32, :], start=True, stop=True),
                        waits=(ac_A, ac_B, dv_ev_last))
                    dv_ev = p.op("vector", lambda e, t_loc=t_loc: e.tensor_scalar(
                        stage_sb[:, t_loc * 512:(t_loc + 1) * 512], PS[5][0:16, :],
                        fb2c_sb[:, 0:1], None, op0=ALU.add),
                        waits=(pe_e, st_stage, gp_last))
                    if s0 + 511 > tg:
                        def _aff(e, t_loc=t_loc, tg=tg, s0=s0):
                            if not hasattr(e, "_fill_reg"):
                                e._fill_reg = e.to_reg(-1e30)
                            return e.affine_select(
                                stage_sb[:, t_loc * 512:(t_loc + 1) * 512],
                                stage_sb[:, t_loc * 512:(t_loc + 1) * 512],
                                pattern=[[-1, 512]], compare_op=ALU.is_ge,
                                fill=e._fill_reg, base=tg - s0,
                                channel_multiplier=0)
                        gp_last = p.op("gpsimd", _aff, waits=(dv_ev,))
                        dv_ev_last = dv_ev
                    else:
                        dv_ev_last = dv_ev
            st_stage = p.dma("sync", lambda e, st=st, s0=s0: e.dma_start(
                out=_ap(bias_h.ap(), [[T * T, 16], [T, 16], [1, 512]],
                        off=st * 16 * T + s0),
                in_=_ap(stage_sb[0:16, 0:512], [[8192, 16], [512, 16], [1, 512]])),
                waits=(dv_ev_last, gp_last))
    bias_done = p.all_dma()

    prev_dv = None
    for b in range(BPC):
        lg_w = [None, None]
        st_q = [None, None]
        for g in range(4):
            t0 = b * T + g * 512
            tl0 = g * 512
            ldx = p.dma("sync", [
                (lambda e, ci=ci, t0=t0: e.dma_start(
                    out=xg_sb[:, ci * 512:(ci + 1) * 512],
                    in_=x2_h.ap()[ci * 128:(ci + 1) * 128, t0:t0 + 512]))
                for ci in range(8)], waits=(prev_dv, p.last("PE")))
            pe_sx = None
            for ci in range(8):
                pe_sx = p.op("tensor", lambda e, ci=ci: e.matmul(
                    PS[1][0:1, :], onesf_sb[:, 0:1], xg_sb[:, ci * 512:(ci + 1) * 512],
                    start=(ci == 0), stop=(ci == 7)),
                    waits=((ldx, LD_C, p.last("DV")) if ci == 0 else ()))
            pe_ssq = pe_sx
            for ci in range(8):
                ac_sq = p.op("scalar", lambda e, ci=ci: e.activation(
                    t_sb[:, :], xg_sb[:, ci * 512:(ci + 1) * 512], AF.Square),
                    waits=(ldx, pe_ssq, p.last("DV") if ci == 0 else None))
                pe_ssq = p.op("tensor", lambda e, ci=ci: e.matmul(
                    PS[2][0:1, :], onesf_sb[:, 0:1], t_sb[:, :],
                    start=(ci == 0), stop=(ci == 7)), waits=(ac_sq,))
            dv = p.op("vector", lambda e: e.tensor_scalar_mul(
                s_(0), PS[1][0:1, :], 1.0 / 1024), waits=(pe_sx, pe_ssq))
            dv = p.op("vector", lambda e: e.tensor_scalar_mul(
                s_(1), PS[2][0:1, :], 1.0 / 1024), waits=(dv,))
            dv = p.op("vector", lambda e: e.tensor_mul(s_(2), s_(0), s_(0)), waits=(dv,))
            dv = p.op("vector", lambda e: e.tensor_sub(s_(1), s_(1), s_(2)), waits=(dv,))
            dv = p.op("vector", lambda e: e.tensor_scalar_add(s_(1), s_(1), 1e-5),
                      waits=(dv,))
            ac = p.op("scalar", lambda e: e.activation(s_(1), s_(1), AF.Sqrt), waits=(dv,))
            dv = p.op("vector", lambda e: e.reciprocal(s_(1), s_(1)), waits=(ac,))
            dv = p.op("vector", lambda e: e.tensor_mul(s_(2), s_(0), s_(1)), waits=(dv,))
            dv_mr = p.op("vector", lambda e: e.tensor_copy(rowb_sb[0:1, :], s_(2)),
                         waits=(dv,))
            pe_b = p.op("tensor", lambda e: e.matmul(
                PS[3][:, :], identb_sb[0:1, 0:1] if False else onesf_sb[0:1, 0:128],
                s_(2), start=True, stop=True), waits=(dv_mr,))
            pe_rs = p.op("tensor", lambda e: e.matmul(
                PS[6][:, :], onesf_sb[0:1, 0:128], s_(1), start=True, stop=True),
                waits=(dv_mr,))
            dv_rb = p.op("vector", lambda e: e.tensor_copy(rb_sb[:, :], PS[6][:, :]),
                         waits=(pe_rs,))
            dv_mb = p.op("vector", lambda e: e.tensor_copy(t_sb[:, :], PS[3][:, :]),
                         waits=(pe_b, dv_rb))
            dvx = None
            for ci in range(8):
                dvx = p.op("vector", lambda e, ci=ci: e.tensor_mul(
                    xln_sb[:, ci * 512:(ci + 1) * 512],
                    xg_sb[:, ci * 512:(ci + 1) * 512], rb_sb[:, :]),
                    waits=((dv_mb, pe_ssq) if ci == 0 else (dvx,)))
                dvx = p.op("vector", lambda e, ci=ci: e.tensor_sub(
                    xln_sb[:, ci * 512:(ci + 1) * 512],
                    xln_sb[:, ci * 512:(ci + 1) * 512], t_sb[:, :]),
                    waits=(dvx,))
            dv_xl = dvx
            pe_q_slot = [None, None]
            for fc in range(24):
                wsl = fc % 2
                lg_w[wsl] = p.dma("gpsimd", lambda e, fc=fc, wsl=wsl: e.dma_start(
                    out=wq_sb[:, wsl * 1024:(wsl + 1) * 1024],
                    in_=wqP_h.ap()[:, fc * 1024:(fc + 1) * 1024]),
                    waits=((pe_q_slot[wsl],) if pe_q_slot[wsl] else (LG_C, p.last("PE"))))
                bank = PS[4] if fc % 2 == 0 else PS[5]
                pe_q = None
                for ci in range(8):
                    pe_q = p.op("tensor", lambda e, ci=ci, fc=fc, wsl=wsl, bank=bank: e.matmul(
                        bank[:, :], wq_sb[:, wsl * 1024 + ci * 128:wsl * 1024 + (ci + 1) * 128],
                        xln_sb[:, ci * 512:(ci + 1) * 512],
                        start=(ci == 0), stop=(ci == 7)),
                        waits=((dv_xl, lg_w[wsl], p.last("AC"), st_q[wsl])
                               if ci == 0 else ()))
                pe_q_slot[wsl] = pe_q
                if fc < 8:
                    ssl = fc % 2
                    ac_e = p.op("scalar", lambda e, fc=fc, ssl=ssl, bank=bank: e.activation(
                        stg_sb[:, ssl * 512:(ssl + 1) * 512], bank[:, :], AF.Identity,
                        bias=bqc_sb[:, fc:fc + 1]), waits=(pe_q, st_q[ssl]))
                    st_q[ssl] = p.dma("sync", lambda e, fc=fc, tl0=tl0, ssl=ssl: e.dma_start(
                        out=q_hbm.ap()[fc * 128:(fc + 1) * 128, tl0:tl0 + 512],
                        in_=stg_sb[:, ssl * 512:(ssl + 1) * 512]), waits=(ac_e,))
                else:
                    ac_e = p.op("scalar", lambda e, fc=fc, tl0=tl0, bank=bank: e.activation(
                        kvt_sb[:, (fc - 8) * 2048 + tl0:(fc - 8) * 2048 + tl0 + 512],
                        bank[:, :], AF.Identity, bias=bqc_sb[:, fc:fc + 1]),
                        waits=(pe_q, prev_dv))
            prev_dv = p.last("DV")
        dv_qkv = p.last("DV")
        ac_qkv = p.last("AC")
        q_done = p.all_dma()
        st_ao = None
        ld_bias = [None, None]
        dv_sb2 = None
        dv_ao = None
        for hh in range(NH):
            hb = (hh % 2) * 64
            ld_q = p.dma("sync", lambda e, hh=hh, hb=hb: e.dma_start(
                out=qth_sb[hb:hb + 64, :], in_=q_hbm.ap()[hh * 64:(hh + 1) * 64, :]),
                waits=q_done + (p.last("PE"),))
            for ti in range(16):
                vfc = 8 + hh // 2
                pe_t = p.op("tensor", lambda e, ti=ti, vfc=vfc, hb=hb: e.transpose(
                    PSB[7][0:128, 0:64],
                    kvt_sb[hb:hb + 64, vfc * 2048 + ti * 128:vfc * 2048 + (ti + 1) * 128],
                    identb_sb[hb:hb + 64, hb:hb + 64]),
                    waits=((ac_qkv, p.last("DV")) if ti == 0 else (dv_vt,)))
                dv_vt = p.op("vector", lambda e, ti=ti: e.tensor_copy(
                    vtm_sb[:, ti * 64:(ti + 1) * 64], PSB[7][0:128, 0:64]),
                    waits=(pe_t,))
            for qt_ in range(16):
                nsc = qt_ // 4 + 1
                kfc = hh // 2
                pe_s = None
                for kc in range(nsc):
                    bank = PS[0] if kc % 2 == 0 else PS[1]
                    bsl = kc % 2
                    ld_bias[bsl] = p.dma("sync", lambda e, hh=hh, qt_=qt_, kc=kc, bsl=bsl: e.dma_start(
                        out=biast_sb[:, bsl * 512:(bsl + 1) * 512],
                        in_=bias_h.ap()[hh * T + qt_ * 128:hh * T + (qt_ + 1) * 128,
                                        kc * 512:(kc + 1) * 512]),
                        waits=(bias_done if (b == 0 and hh == 0 and qt_ == 0) else
                               (ld_bias[bsl], p.last("DV"))))
                    pe_s = p.op("tensor", lambda e, qt_=qt_, kc=kc, kfc=kfc, hb=hb, bank=bank: e.matmul(
                        bank[:, :], qth_sb[hb:hb + 64, qt_ * 128:(qt_ + 1) * 128],
                        kvt_sb[hb:hb + 64, kfc * 2048 + kc * 512:kfc * 2048 + (kc + 1) * 512],
                        start=True, stop=True),
                        waits=((ld_q, dv_sb2) if kc < 2 else (dv_sb2,)))
                    dv_sb2 = p.op("vector", lambda e, kc=kc, bsl=bsl, bank=bank: e.scalar_tensor_tensor(
                        sbrow_sb[:, kc * 512:(kc + 1) * 512], bank[:, :], 0.125,
                        biast_sb[:, bsl * 512:(bsl + 1) * 512],
                        op0=ALU.mult, op1=ALU.add),
                        waits=(pe_s, ld_bias[bsl]))
                dv_m = p.op("vector", lambda e, nsc=nsc: e.reduce_max(
                    t_sb[:, 0:1], sbrow_sb[:, 0:nsc * 512], axis=AXX), waits=(dv_sb2,))
                dv_m = p.op("vector", lambda e: e.tensor_scalar_mul(
                    t_sb[:, 1:2], t_sb[:, 0:1], -1.0), waits=(dv_m,))
                ac_x = p.op("scalar", lambda e, nsc=nsc: e.activation(
                    pn_sb[:, 0:nsc * 512], sbrow_sb[:, 0:nsc * 512], AF.Exp,
                    bias=t_sb[:, 1:2], accum_out=t_sb[:, 2:3]), waits=(dv_m,))
                dv_rc = p.op("vector", lambda e: e.reciprocal(t_sb[:, 3:4], t_sb[:, 2:3]),
                             waits=(ac_x,))
                dv_pn = p.op("vector", lambda e, nsc=nsc: e.tensor_scalar(
                    pn_sb[:, 0:nsc * 512], pn_sb[:, 0:nsc * 512], t_sb[:, 3:4], None,
                    op0=ALU.mult), waits=(dv_rc,))
                nkt = qt_ + 1
                pe_ao = None
                dv_pt = None
                for kt in range(nkt):
                    psl = kt % 2
                    pe_pt = p.op("tensor", lambda e, kt=kt: e.transpose(
                        PSB[6][0:128, 0:128], pn_sb[:, kt * 128:(kt + 1) * 128],
                        identb_sb[:, :]), waits=(dv_pn, dv_pt))
                    dv_pt = p.op("vector", lambda e, psl=psl: e.tensor_copy(
                        ptb_sb[:, psl * 128:(psl + 1) * 128], PSB[6][0:128, 0:128]),
                        waits=(pe_pt, pe_ao))
                    pe_ao = p.op("tensor", lambda e, kt=kt, psl=psl, nkt=nkt: e.matmul(
                        PS[2][0:64, 0:128], vtm_sb[:, kt * 64:(kt + 1) * 64],
                        ptb_sb[:, psl * 128:(psl + 1) * 128],
                        start=(kt == 0), stop=(kt == nkt - 1)),
                        waits=(dv_pt, dv_vt, dv_ao if kt == 0 else None))
                qsl = qt_ % 4
                dv_ao = p.op("vector", lambda e, qsl=qsl: e.tensor_copy(
                    aost_sb[:, qsl * 128:(qsl + 1) * 128], PS[2][0:64, 0:128]),
                    waits=(pe_ao, st_ao))
                if qsl == 3:
                    st_ao = p.dma("sync", lambda e, hh=hh, qt_=qt_: e.dma_start(
                        out=ao_hbm.ap()[hh * 64:(hh + 1) * 64,
                                        (qt_ - 3) * 128:(qt_ + 1) * 128],
                        in_=aost_sb[:, :]), waits=(dv_ao,))
        ao_done = p.all_dma()
        lg_wp = p.dma("gpsimd", lambda e: e.dma_start(
            out=wp_sb[:, :], in_=wpP_h.ap()), waits=(p.last("PE"),))
        st_x3 = [None, None]
        dv_x3 = [None, None]
        for g in range(4):
            t0 = b * T + g * 512
            tl0 = g * 512
            ld_ao = p.dma("sync", [
                (lambda e, ci=ci, tl0=tl0: e.dma_start(
                    out=ao_sb[:, ci * 512:(ci + 1) * 512],
                    in_=ao_hbm.ap()[ci * 128:(ci + 1) * 128, tl0:tl0 + 512]))
                for ci in range(8)],
                waits=(ao_done + (p.last("PE"),) if g == 0 else (p.last("PE"),)))
            for do in range(8):
                bank = PS[4] if do % 2 == 0 else PS[5]
                osl = do % 2
                pe_pj = None
                for ci in range(8):
                    pe_pj = p.op("tensor", lambda e, ci=ci, do=do, bank=bank: e.matmul(
                        bank[:, :], wp_sb[:, ci * 1024 + do * 128:ci * 1024 + (do + 1) * 128],
                        ao_sb[:, ci * 512:(ci + 1) * 512],
                        start=(ci == 0), stop=(ci == 7)),
                        waits=((ld_ao, lg_wp, p.last("DV")) if ci == 0 else ()))
                ldx2 = p.dma("sync", lambda e, do=do, t0=t0, osl=osl: e.dma_start(
                    out=xo_sb[:, osl * 512:(osl + 1) * 512],
                    in_=x2_h.ap()[do * 128:(do + 1) * 128, t0:t0 + 512]),
                    waits=(st_x3[osl], dv_x3[osl]))
                dv_x3[osl] = p.op("vector", lambda e, do=do, osl=osl, bank=bank: e.scalar_tensor_tensor(
                    xo_sb[:, osl * 512:(osl + 1) * 512], bank[:, :],
                    bpc_sb[:, do:do + 1], xo_sb[:, osl * 512:(osl + 1) * 512],
                    op0=ALU.add, op1=ALU.add), waits=(pe_pj, ldx2))
                st_x3[osl] = p.dma("sync", lambda e, do=do, t0=t0, osl=osl: e.dma_start(
                    out=x3_h.ap()[do * 128:(do + 1) * 128, t0:t0 + 512],
                    in_=xo_sb[:, osl * 512:(osl + 1) * 512]), waits=(dv_x3[osl],))
    p.wait("sync", p.all_dma())
    p.stage_end()
    return p.finalize() if own else None


def prep_B3_inputs(inputs, x2_shards):
    Wq = np.asarray(inputs["L1_attn_Wqkv"])
    bq = np.asarray(inputs["L1_attn_bqkv"])
    Wp = np.asarray(inputs["L1_attn_Wproj"])
    bp = np.asarray(inputs["L1_attn_bproj"])
    w1 = np.asarray(inputs["L1_attn_w1"])
    b1 = np.asarray(inputs["L1_attn_b1"])
    w2 = np.asarray(inputs["L1_attn_w2"])
    b2 = np.asarray(inputs["L1_attn_b2"])
    wb = np.zeros((128, 16 * 128), np.float32)
    for v in range(4):
        for j in range(4):
            for w in range(32):
                wb[4 * v + j, v * 128 + j * 32 + w] = w1[0, w]
                wb[64 + 4 * v + j, v * 128 + j * 32 + w] = b1[w]
    base = {
        "WqkvP": np.ascontiguousarray(
            Wq.reshape(8, 128, 24, 128).transpose(1, 2, 0, 3).reshape(128, 24 * 1024)),
        "bqkvc": np.ascontiguousarray(bq.reshape(24, 128).T),
        "WprojP": np.ascontiguousarray(
            Wp.reshape(8, 128, 8, 128).transpose(1, 0, 2, 3).reshape(128, 8 * 1024)),
        "bprojc": np.ascontiguousarray(bp.reshape(8, 128).T),
        "wb_lhs": wb,
        "w2c2": np.ascontiguousarray(np.concatenate([w2, w2], axis=0)),
        "fb2c": np.ascontiguousarray(b2[:, None]),
        "cpar": np.asarray(inputs["L1_attn_c_param"]).reshape(1, 1).astype(np.float32),
        "lmul": np.asarray(inputs["L1_attn_L_multiplier"]).reshape(1, 1).astype(np.float32),
        "ones_f": CONSTS["ones_f"], "ident": CONSTS["ident"],
    }
    return [{"X2": x2_shards[c], **base} for c in range(NCORE)]


def build_B4(p=None):
    own = p is None
    if own:
        p = Prog()
    nc = p.nc
    p.stage_begin("b4")

    x3_h = p.dram("X3", [D, NTOK], F32, kind="ExternalInput")
    phi_h = p.dram("phiP", [128, 8 * 2048], F32, kind="ExternalInput")
    onesf_h = p.dram("ones_f", [128, 128], F32, kind="ExternalInput")

    sp_h = p.dram("SP", [T, 2048], F32, kind="ExternalOutput")

    onesf_sb = p.sbuf("onesf_sb", [128, 128], F32)
    phi_sb = p.sbuf("phi_sb", [128, 8 * 2048], BF16)
    xg_sb = p.sbuf("xg_sb", [128, 8 * 512], F32)
    xln_sb = p.sbuf("xln_sb", [128, 2 * 8 * 512], BF16)
    ex_sb = p.sbuf("ex_sb", [128, 512], F32)
    sacc_sb = p.sbuf("sacc_sb", [128, 2048], F32)
    srow_sb = p.sbuf("srow_sb", [1, 2048], F32)
    t_sb = p.sbuf("t_sb", [128, 512], F32)
    rb_sb = p.sbuf("rb_sb", [128, 512], F32)
    mb_sb = p.sbuf("mb_sb", [128, 512], F32)

    PS = [p.psum(f"bank{i}", [128, 512]) for i in range(8)]

    def s_(i):
        return srow_sb[0:1, i * 512:(i + 1) * 512]

    LD_C = p.dma("sync", lambda e: e.dma_start(out=onesf_sb[:, :], in_=onesf_h.ap()))
    LG_C = p.dma("gpsimd", lambda e: e.dma_start(out=phi_sb[:, :], in_=phi_h.ap()))

    prev_dv = None
    st_sp = None
    for g in range(4):
        for b in range(2):
            t0 = b * T + g * 512
            ldx = p.dma("sync", [
                (lambda e, ci=ci, t0=t0: e.dma_start(
                    out=xg_sb[:, ci * 512:(ci + 1) * 512],
                    in_=x3_h.ap()[ci * 128:(ci + 1) * 128, t0:t0 + 512]))
                for ci in range(8)], waits=(prev_dv, p.last("PE")))
            pe_sx = None
            for ci in range(8):
                pe_sx = p.op("tensor", lambda e, ci=ci: e.matmul(
                    PS[1][0:1, :], onesf_sb[:, 0:1], xg_sb[:, ci * 512:(ci + 1) * 512],
                    start=(ci == 0), stop=(ci == 7)),
                    waits=((ldx, LD_C) if ci == 0 else ()))
            pe_ssq = pe_sx
            for ci in range(8):
                ac_sq = p.op("scalar", lambda e, ci=ci: e.activation(
                    t_sb[:, :], xg_sb[:, ci * 512:(ci + 1) * 512], AF.Square),
                    waits=(ldx, pe_ssq))
                pe_ssq = p.op("tensor", lambda e, ci=ci: e.matmul(
                    PS[2][0:1, :], onesf_sb[:, 0:1], t_sb[:, :],
                    start=(ci == 0), stop=(ci == 7)), waits=(ac_sq,))
            dv = p.op("vector", lambda e: e.tensor_scalar_mul(
                s_(0), PS[1][0:1, :], 1.0 / 1024), waits=(pe_sx, pe_ssq))
            dv = p.op("vector", lambda e: e.tensor_scalar_mul(
                s_(1), PS[2][0:1, :], 1.0 / 1024), waits=(dv,))
            dv = p.op("vector", lambda e: e.tensor_mul(s_(2), s_(0), s_(0)), waits=(dv,))
            dv = p.op("vector", lambda e: e.tensor_sub(s_(1), s_(1), s_(2)), waits=(dv,))
            dv = p.op("vector", lambda e: e.tensor_scalar_add(s_(1), s_(1), 1e-5),
                      waits=(dv,))
            ac = p.op("scalar", lambda e: e.activation(s_(1), s_(1), AF.Sqrt), waits=(dv,))
            dv = p.op("vector", lambda e: e.reciprocal(s_(1), s_(1)), waits=(ac,))
            dv = p.op("vector", lambda e: e.tensor_mul(s_(2), s_(0), s_(1)), waits=(dv,))
            pe_b = p.op("tensor", lambda e: e.matmul(
                PS[3][:, :], onesf_sb[0:1, 0:128], s_(2), start=True, stop=True),
                waits=(dv,))
            pe_rs = p.op("tensor", lambda e: e.matmul(
                PS[6][:, :], onesf_sb[0:1, 0:128], s_(1), start=True, stop=True),
                waits=(dv,))
            dv_rb = p.op("vector", lambda e: e.tensor_copy(rb_sb[:, :], PS[6][:, :]),
                         waits=(pe_rs,))
            dv_mb = p.op("vector", lambda e: e.tensor_copy(mb_sb[:, :], PS[3][:, :]),
                         waits=(pe_b,))
            dvx = None
            for ci in range(8):
                dvx = p.op("vector", lambda e, ci=ci, b=b: e.tensor_mul(
                    xln_sb[:, b * 4096 + ci * 512:b * 4096 + (ci + 1) * 512],
                    xg_sb[:, ci * 512:(ci + 1) * 512], rb_sb[:, :]),
                    waits=((dv_mb, dv_rb, pe_ssq) if ci == 0 else (dvx,)))
                dvx = p.op("vector", lambda e, ci=ci, b=b: e.tensor_sub(
                    xln_sb[:, b * 4096 + ci * 512:b * 4096 + (ci + 1) * 512],
                    xln_sb[:, b * 4096 + ci * 512:b * 4096 + (ci + 1) * 512],
                    mb_sb[:, :]), waits=(dvx,))
            prev_dv = dvx
        dv_xl = p.last("DV")
        for mtl in range(4):
            mt = g * 4 + mtl
            for npc in range(4):
                dv_sa = None
                for b in range(2):
                    bank = PS[4] if b == 0 else PS[5]
                    pe_l = None
                    for ci in range(8):
                        pe_l = p.op("tensor", lambda e, ci=ci, b=b, mtl=mtl, npc=npc, bank=bank: e.matmul(
                            bank[:, :],
                            xln_sb[:, b * 4096 + ci * 512 + mtl * 128:b * 4096 + ci * 512 + mtl * 128 + 128],
                            phi_sb[:, ci * 2048 + npc * 512:ci * 2048 + (npc + 1) * 512],
                            start=(ci == 0), stop=(ci == 7)),
                            waits=((dv_xl, LG_C, p.last("DV"), p.last("AC"))
                                   if ci == 0 else ()))
                    if b == 0:
                        ac_x = p.op("scalar", lambda e, npc=npc, bank=bank: e.activation(
                            sacc_sb[:, npc * 512:(npc + 1) * 512], bank[:, :], AF.Exp),
                            waits=(pe_l, st_sp))
                        dv_sa = ac_x
                    else:
                        ac_x = p.op("scalar", lambda e, bank=bank: e.activation(
                            ex_sb[:, :], bank[:, :], AF.Exp), waits=(pe_l,))
                        dv_sa = p.op("vector", lambda e, npc=npc: e.tensor_add(
                            sacc_sb[:, npc * 512:(npc + 1) * 512],
                            sacc_sb[:, npc * 512:(npc + 1) * 512], ex_sb[:, :]),
                            waits=(ac_x, dv_sa))
                dv_spd = dv_sa
            st_sp = p.dma("sync", lambda e, mt=mt: e.dma_start(
                out=sp_h.ap()[mt * 128:(mt + 1) * 128, :], in_=sacc_sb[:, :]),
                waits=(p.last("DV"), p.last("AC")))
    p.wait("sync", p.all_dma())
    p.stage_end()
    return p.finalize() if own else None


def prep_B4_inputs(inputs, x3_shards):
    phi = np.asarray(inputs["L1_ffn_phi"]).reshape(1024, 2048)
    base = {
        "phiP": np.ascontiguousarray(
            phi.reshape(8, 128, 2048).transpose(1, 0, 2).reshape(128, 16384)),
        "ones_f": CONSTS["ones_f"],
    }
    return [{"X3": x3_shards[c], **base} for c in range(NCORE)]


def build_C(p=None):
    own = p is None
    if own:
        p = Prog()
    nc = p.nc
    p.stage_begin("c")
    sps_h = p.dram("SPS", [8, 256 * 2048], F32, kind="ExternalInput")
    sr_h = p.dram("SR", [256, 2048], F32, kind="ExternalOutput")
    acc_sb = p.sbuf("acc_sb", [128, 2048], F32)
    ld_sb = p.sbuf("ld_sb", [128, 2 * 2048], F32)
    st_sr = None
    for rt in range(2):
        ld0 = p.dma("sync", lambda e, rt=rt: e.dma_start(
            out=acc_sb[:, :],
            in_=_ap(sps_h.ap(), [[2048, 128], [1, 2048]], off=rt * 128 * 2048)),
            waits=(p.last("DV"), st_sr))
        dv = None
        lds = [None, None]
        for r in range(1, 8):
            sl = r % 2
            lds[sl] = p.dma("sync", lambda e, r=r, rt=rt, sl=sl: e.dma_start(
                out=ld_sb[:, sl * 2048:(sl + 1) * 2048],
                in_=_ap(sps_h.ap(), [[2048, 128], [1, 2048]],
                        off=r * 256 * 2048 + rt * 128 * 2048)),
                waits=((dv,) if r >= 3 else ()))
            dv = p.op("vector", lambda e, sl=sl: e.tensor_add(
                acc_sb[:, :], acc_sb[:, :], ld_sb[:, sl * 2048:(sl + 1) * 2048]),
                waits=(lds[sl], ld0, dv))
        dv = p.op("vector", lambda e: e.reciprocal(acc_sb[:, :], acc_sb[:, :]),
                  waits=(dv,))
        st_sr = p.dma("sync", lambda e, rt=rt: e.dma_start(
            out=sr_h.ap()[rt * 128:(rt + 1) * 128, :], in_=acc_sb[:, :]),
            waits=(dv,))
    p.wait("sync", p.all_dma())
    p.stage_end()
    return p.finalize() if own else None


def build_D(p=None):
    own = p is None
    if own:
        p = Prog()
    nc = p.nc
    p.stage_begin("d")

    x3_h = p.dram("X3", [D, NTOK], F32, kind="ExternalInput")
    phi_h = p.dram("phiP", [128, 8 * 2048], F32, kind="ExternalInput")
    sinv_h = p.dram("SINV", [T, 2048], F32, kind="ExternalInput")
    wp_h = p.dram("WP", [64 * 2 * 128, 4096], F32, kind="ExternalInput")
    bh_h = p.dram("BH", [64, 1024], F32, kind="ExternalInput")
    onesf_h = p.dram("ones_f", [128, 128], F32, kind="ExternalInput")
    ident_h = p.dram("ident", [128, 128], F32, kind="ExternalInput")

    x4_h = p.dram("X4", [D, NTOK], F32, kind="ExternalOutput")

    xlnh = p.dram("xlnD", [2 * D, T], BF16)
    slots_hbm = p.dram("slotsD", [128, 2 * 8 * 2048], BF16)
    yyh = p.dram("yyD", [2 * 2048, 1024], BF16)

    onesf_sb = p.sbuf("onesf_sb", [128, 128], F32)
    onesb_sb = p.sbuf("onesb_sb", [128, 128], BF16)
    identb_sb = p.sbuf("identb_sb", [128, 128], BF16)
    phi_sb = p.sbuf("phi_sb", [128, 16384], BF16)
    xg_sb = p.sbuf("xg_sb", [128, 8 * 512], F32)
    xln_sb = p.sbuf("xln_sb", [128, 16384], BF16)
    xlt_sb = p.sbuf("xlt_sb", [128, 16384], BF16)
    dsp_sb = p.sbuf("dsp_sb", [128, 8192], BF16)
    est_sb = p.sbuf("est_sb", [128, 2 * 512], F32)
    siv_sb = p.sbuf("siv_sb", [128, 2 * 512], F32)
    slst_sb = p.sbuf("slst_sb", [128, 2 * 512], BF16)
    w_sb = p.sbuf("w_sb", [128, 2 * 4096], BF16)
    sn_sb = p.sbuf("sn_sb", [128, 4 * 256], BF16)
    yst_sb = p.sbuf("yst_sb", [32, 2 * 512], BF16)
    bst_sb = p.sbuf("bst_sb", [1, 2 * 1024], BF16)
    denb_sb = p.sbuf("denb_sb", [128, 512], F32)
    x3t_sb = p.sbuf("x3t_sb", [128, 2 * 512], F32)
    x4t_sb = p.sbuf("x4t_sb", [128, 2 * 512], F32)
    srow_sb = p.sbuf("srow_sb", [1, 2048], F32)
    rb_sb = p.sbuf("rb_sb", [128, 512], F32)
    mb_sb = p.sbuf("mb_sb", [128, 512], F32)
    t_sb = p.sbuf("t_sb", [128, 512], F32)

    PS = [p.psum(f"bank{i}", [128, 512]) for i in range(8)]
    PSB = [b_[:, :].bitcast(BF16) for b_ in PS]
    bcons = [None] * 8

    def s_(i):
        return srow_sb[0:1, i * 512:(i + 1) * 512]

    LD_C = p.dma("sync", [
        lambda e: e.dma_start(out=onesf_sb[:, :], in_=onesf_h.ap()),
    ])
    LG_C = p.dma("gpsimd", [
        lambda e: e.dma_start(out=onesb_sb[:, :], in_=onesf_h.ap()),
        lambda e: e.dma_start(out=identb_sb[:, :], in_=ident_h.ap()),
        lambda e: e.dma_start(out=phi_sb[:, :], in_=phi_h.ap()),
    ])

    ldS_s = [None, None]; dsp_cons = [None] * 16
    est_cons = [None, None]; slst_st = [None, None]
    tr_cons = [None, None]
    x3_cons = [None, None]; x4_st = [None, None]
    prev_ldx = None

    for b in range(2):
        dvx = None
        for g in range(4):
            t0 = b * T + g * 512
            ldx = p.dma("sync", [
                (lambda e, ci=ci, t0=t0: e.dma_start(
                    out=xg_sb[:, ci * 512:(ci + 1) * 512],
                    in_=x3_h.ap()[ci * 128:(ci + 1) * 128, t0:t0 + 512]))
                for ci in range(8)], waits=(prev_ldx, p.last("AC")))
            pe_sx = None
            for ci in range(8):
                pe_sx = p.op("tensor", lambda e, ci=ci: e.matmul(
                    PS[1][0:1, :], onesf_sb[:, 0:1], xg_sb[:, ci * 512:(ci + 1) * 512],
                    start=(ci == 0), stop=(ci == 7)),
                    waits=((ldx, LD_C, bcons[1]) if ci == 0 else ()))
            pe_ssq = pe_sx
            for ci in range(8):
                ac_sq = p.op("scalar", lambda e, ci=ci: e.activation(
                    t_sb[:, :], xg_sb[:, ci * 512:(ci + 1) * 512], AF.Square),
                    waits=(ldx, pe_ssq))
                pe_ssq = p.op("tensor", lambda e, ci=ci: e.matmul(
                    PS[2][0:1, :], onesf_sb[:, 0:1], t_sb[:, :],
                    start=(ci == 0), stop=(ci == 7)),
                    waits=((ac_sq, bcons[2]) if ci == 0 else (ac_sq,)))
            dv = p.op("vector", lambda e: e.tensor_scalar_mul(
                s_(0), PS[1][0:1, :], 1.0 / 1024), waits=(pe_sx, pe_ssq))
            bcons[1] = dv
            dv = p.op("vector", lambda e: e.tensor_scalar_mul(
                s_(1), PS[2][0:1, :], 1.0 / 1024), waits=(dv,))
            bcons[2] = dv
            dv = p.op("vector", lambda e: e.tensor_mul(s_(2), s_(0), s_(0)), waits=(dv,))
            dv = p.op("vector", lambda e: e.tensor_sub(s_(1), s_(1), s_(2)), waits=(dv,))
            dv = p.op("vector", lambda e: e.tensor_scalar_add(s_(1), s_(1), 1e-5),
                      waits=(dv,))
            ac = p.op("scalar", lambda e: e.activation(s_(1), s_(1), AF.Sqrt), waits=(dv,))
            dv = p.op("vector", lambda e: e.reciprocal(s_(1), s_(1)), waits=(ac,))
            dv = p.op("vector", lambda e: e.tensor_mul(s_(2), s_(0), s_(1)), waits=(dv,))
            pe_b = p.op("tensor", lambda e: e.matmul(
                PS[3][:, :], onesf_sb[0:1, 0:128], s_(2), start=True, stop=True),
                waits=(dv, bcons[3]))
            pe_rs = p.op("tensor", lambda e: e.matmul(
                PS[6][:, :], onesf_sb[0:1, 0:128], s_(1), start=True, stop=True),
                waits=(dv, bcons[6]))
            dv_rb = p.op("vector", lambda e: e.tensor_copy(rb_sb[:, :], PS[6][:, :]),
                         waits=(pe_rs,))
            bcons[6] = dv_rb
            dv_mb = p.op("vector", lambda e: e.tensor_copy(mb_sb[:, :], PS[3][:, :]),
                         waits=(pe_b,))
            bcons[3] = dv_mb
            for ci in range(8):
                dvx = p.op("vector", lambda e, ci=ci, g=g: e.tensor_mul(
                    xln_sb[:, ci * 2048 + g * 512:ci * 2048 + (g + 1) * 512],
                    xg_sb[:, ci * 512:(ci + 1) * 512], rb_sb[:, :]),
                    waits=((dv_mb, dv_rb, pe_ssq) if ci == 0 else (dvx,)))
                dvx = p.op("vector", lambda e, ci=ci, g=g: e.tensor_sub(
                    xln_sb[:, ci * 2048 + g * 512:ci * 2048 + (g + 1) * 512],
                    xln_sb[:, ci * 2048 + g * 512:ci * 2048 + (g + 1) * 512],
                    mb_sb[:, :]), waits=(dvx,))
            prev_ldx = dvx
        dv_xln = dvx
        st_xln = p.dma("sync", [
            (lambda e, dt=dt, b=b: e.dma_start(
                out=xlnh.ap()[b * D + dt * 128:b * D + (dt + 1) * 128, :],
                in_=xln_sb[:, dt * 2048:(dt + 1) * 2048]))
            for dt in range(8)], waits=(dv_xln,))
        for dt in range(8):
            for mt in range(16):
                pr = (dt * 16 + mt) % 2
                bank = 4 + pr
                pe_t = p.op("tensor", lambda e, dt=dt, mt=mt, bank=bank: e.transpose(
                    PSB[bank][0:128, 0:128],
                    xln_sb[:, dt * 2048 + mt * 128:dt * 2048 + mt * 128 + 128],
                    identb_sb[:, :]),
                    waits=(dv_xln, LG_C, bcons[bank], tr_cons[pr]))
                dv_t = p.op("vector", lambda e, dt=dt, mt=mt, bank=bank: e.tensor_copy(
                    xlt_sb[:, mt * 1024 + dt * 128:mt * 1024 + dt * 128 + 128],
                    PSB[bank][0:128, 0:128]), waits=(pe_t,))
                bcons[bank] = dv_t
                tr_cons[pr] = dv_t
        for npc in range(4):
            dv_dsp_last = None
            for mt in range(16):
                bank = mt % 2
                pe_l = None
                for dt in range(8):
                    pe_l = p.op("tensor", lambda e, dt=dt, mt=mt, npc=npc, bank=bank: e.matmul(
                        PS[bank][:, :],
                        xln_sb[:, dt * 2048 + mt * 128:dt * 2048 + mt * 128 + 128],
                        phi_sb[:, dt * 2048 + npc * 512:dt * 2048 + (npc + 1) * 512],
                        start=(dt == 0), stop=(dt == 7)),
                        waits=((dv_xln, LG_C, bcons[bank]) if dt == 0 else ()))
                sl = mt % 2
                ldS_s[sl] = p.dma("sync", lambda e, mt=mt, npc=npc, sl=sl: e.dma_start(
                    out=siv_sb[:, sl * 512:(sl + 1) * 512],
                    in_=sinv_h.ap()[mt * 128:(mt + 1) * 128, npc * 512:(npc + 1) * 512]),
                    waits=(est_cons[sl],))
                ac_e = p.op("scalar", lambda e, sl=sl, bank=bank: e.activation(
                    est_sb[:, sl * 512:(sl + 1) * 512], PS[bank][:, :], AF.Exp),
                    waits=(pe_l, est_cons[sl]))
                bcons[bank] = ac_e
                dv_d = p.op("vector", lambda e, mt=mt, sl=sl: e.tensor_mul(
                    dsp_sb[:, mt * 512:(mt + 1) * 512],
                    est_sb[:, sl * 512:(sl + 1) * 512],
                    siv_sb[:, sl * 512:(sl + 1) * 512]),
                    waits=(ac_e, ldS_s[sl]))
                est_cons[sl] = dv_d
                dsp_cons[mt] = dv_d
                dv_dsp_last = dv_d
            for dt in range(8):
                bank = 2 + dt % 2
                pe_s = None
                for mt in range(16):
                    pe_s = p.op("tensor", lambda e, dt=dt, mt=mt, bank=bank: e.matmul(
                        PS[bank][:, :],
                        xlt_sb[:, mt * 1024 + dt * 128:mt * 1024 + dt * 128 + 128],
                        dsp_sb[:, mt * 512:(mt + 1) * 512],
                        start=(mt == 0), stop=(mt == 15)),
                        waits=((dv_dsp_last, bcons[bank]) if mt == 0 else ()))
                sl2 = dt % 2
                dv_sl = p.op("vector", lambda e, sl2=sl2, bank=bank: e.tensor_copy(
                    slst_sb[:, sl2 * 512:(sl2 + 1) * 512], PS[bank][:, :]),
                    waits=(pe_s, slst_st[sl2]))
                bcons[bank] = dv_sl
                slst_st[sl2] = p.dma("sync", lambda e, b=b, dt=dt, npc=npc, sl2=sl2: e.dma_start(
                    out=slots_hbm.ap()[:, b * 16384 + dt * 2048 + npc * 512:
                                       b * 16384 + dt * 2048 + (npc + 1) * 512],
                    in_=slst_sb[:, sl2 * 512:(sl2 + 1) * 512]), waits=(dv_sl,))
        for mt in range(16):
            dsp_cons[mt] = p.last("PE")

    all_slst = p.all_dma()
    dv_Bstart = p.last("DV")
    sn_cons = [None] * 4
    w_cons = [None, None]
    bst_cons = [None, None]
    yst_st = [None, None]
    for i in range(4, 8):
        if bcons[i] is None:
            bcons[i] = dv_Bstart
    for n in range(64):
        ldsn = [None, None]
        for b in range(2):
            snsl = (n % 2) * 2 + b
            ldsn[b] = p.dma("sync", lambda e, n=n, b=b, snsl=snsl: e.dma_start(
                out=sn_sb[:, snsl * 256:(snsl + 1) * 256],
                in_=_ap(slots_hbm.ap(), [[32768, 128], [2048, 8], [1, 32]],
                        off=b * 16384 + n * 32)),
                waits=(all_slst if n == 0 else (sn_cons[snsl],)))
        bsl = n % 2
        ldb = p.dma("gpsimd", lambda e, n=n, bsl=bsl: e.dma_start(
            out=bst_sb[0:1, bsl * 1024:(bsl + 1) * 1024],
            in_=bh_h.ap()[n:n + 1, :]), waits=(bst_cons[bsl],))
        for ec in range(2):
            wsl = (n * 2 + ec) % 2
            ldw = p.dma("gpsimd", lambda e, n=n, ec=ec, wsl=wsl: e.dma_start(
                out=w_sb[:, wsl * 4096:(wsl + 1) * 4096],
                in_=wp_h.ap()[(n * 2 + ec) * 128:(n * 2 + ec) * 128 + 128, :]),
                waits=(w_cons[wsl],))
            for b in range(2):
                bank = 4 + (n * 4 + ec * 2 + b) % 2
                snsl = (n % 2) * 2 + b
                pe_y = p.op("tensor", lambda e, bsl=bsl, ec=ec, bank=bank: e.matmul(
                    PS[bank][0:32, :], onesb_sb[0:1, 0:32],
                    bst_sb[0:1, bsl * 1024 + ec * 512:bsl * 1024 + (ec + 1) * 512],
                    start=True, stop=False),
                    waits=(ldb, ldw, ldsn[b], bcons[bank], LG_C))
                for dt in range(8):
                    pe_y = p.op("tensor", lambda e, dt=dt, snsl=snsl, wsl=wsl, bank=bank: e.matmul(
                        PS[bank][0:32, :],
                        sn_sb[:, snsl * 256 + dt * 32:snsl * 256 + (dt + 1) * 32],
                        w_sb[:, wsl * 4096 + dt * 512:wsl * 4096 + (dt + 1) * 512],
                        start=False, stop=(dt == 7)))
                sn_cons[snsl] = pe_y
                w_cons[wsl] = pe_y
                bst_cons[bsl] = pe_y
                ysl = (n * 4 + ec * 2 + b) % 2
                dv_y = p.op("vector", lambda e, ysl=ysl, bank=bank: e.tensor_copy(
                    yst_sb[0:32, ysl * 512:(ysl + 1) * 512], PS[bank][0:32, :]),
                    waits=(pe_y, yst_st[ysl]))
                bcons[bank] = dv_y
                yst_st[ysl] = p.dma("sync", lambda e, n=n, ec=ec, b=b, ysl=ysl: e.dma_start(
                    out=yyh.ap()[b * 2048 + n * 32:b * 2048 + (n + 1) * 32,
                                 ec * 512:(ec + 1) * 512],
                    in_=yst_sb[0:32, ysl * 512:(ysl + 1) * 512]), waits=(dv_y,))

    for b in range(2):
        all_pe = p.last("PE")
        all_yy = p.all_dma()
        ldyy = p.dma("sync", [
            (lambda e, npt=npt, b=b: e.dma_start(
                out=xlt_sb[:, npt * 1024:(npt + 1) * 1024],
                in_=yyh.ap()[b * 2048 + npt * 128:b * 2048 + (npt + 1) * 128, :]))
            for npt in range(16)], waits=all_yy + (all_pe,))
        ldxl = p.dma("sync", [
            (lambda e, dt=dt, b=b: e.dma_start(
                out=xln_sb[:, dt * 2048:(dt + 1) * 2048],
                in_=xlnh.ap()[b * D + dt * 128:b * D + (dt + 1) * 128, :]))
            for dt in range(8)], waits=(all_pe,))
        for mc in range(4):
            ac_last = None
            for npt in range(16):
                bank = npt % 2
                pe_e = None
                for dt in range(8):
                    pe_e = p.op("tensor", lambda e, dt=dt, npt=npt, mc=mc, b=b, bank=bank: e.matmul(
                        PS[bank][:, :],
                        phi_sb[:, dt * 2048 + npt * 128:dt * 2048 + npt * 128 + 128],
                        xln_sb[:, dt * 2048 + mc * 512:dt * 2048 + (mc + 1) * 512],
                        start=(dt == 0), stop=(dt == 7)),
                        waits=((ldxl, bcons[bank]) if dt == 0 else ()))
                ac_last = p.op("scalar", lambda e, npt=npt, bank=bank: e.activation(
                    dsp_sb[:, npt * 512:(npt + 1) * 512], PS[bank][:, :], AF.Exp),
                    waits=(pe_e,))
                bcons[bank] = ac_last
            pe_d = None
            for npt in range(16):
                pe_d = p.op("tensor", lambda e, npt=npt: e.matmul(
                    PS[6][0:1, :], onesb_sb[:, 0:1],
                    dsp_sb[:, npt * 512:(npt + 1) * 512],
                    start=(npt == 0), stop=(npt == 15)),
                    waits=((ac_last, bcons[6]) if npt == 0 else ()))
            dv_r = p.op("vector", lambda e: e.reciprocal(s_(1), PS[6][0:1, :]),
                        waits=(pe_d,))
            bcons[6] = dv_r
            pe_bc = p.op("tensor", lambda e: e.matmul(
                PS[7][:, :], onesf_sb[0:1, 0:128], s_(1), start=True, stop=True),
                waits=(dv_r, bcons[7]))
            dv_db = p.op("vector", lambda e: e.tensor_copy(denb_sb[:, :], PS[7][:, :]),
                         waits=(pe_bc,))
            bcons[7] = dv_db
            for et in range(8):
                bank = 2 + et % 2
                pe_o = None
                for npt in range(16):
                    pe_o = p.op("tensor", lambda e, npt=npt, et=et, bank=bank: e.matmul(
                        PS[bank][:, :],
                        xlt_sb[:, npt * 1024 + et * 128:npt * 1024 + et * 128 + 128],
                        dsp_sb[:, npt * 512:(npt + 1) * 512],
                        start=(npt == 0), stop=(npt == 15)),
                        waits=((ldyy, bcons[bank]) if npt == 0 else ()))
                xsl = et % 2
                ldx3 = p.dma("sync", lambda e, et=et, b=b, mc=mc, xsl=xsl: e.dma_start(
                    out=x3t_sb[:, xsl * 512:(xsl + 1) * 512],
                    in_=x3_h.ap()[et * 128:(et + 1) * 128,
                                  b * T + mc * 512:b * T + (mc + 1) * 512]),
                    waits=(x3_cons[xsl],))
                dv_o = p.op("vector", lambda e, xsl=xsl, bank=bank: e.tensor_mul(
                    x4t_sb[:, xsl * 512:(xsl + 1) * 512], PS[bank][:, :], denb_sb[:, :]),
                    waits=(pe_o, x4_st[xsl], dv_db))
                bcons[bank] = dv_o
                dv_o = p.op("vector", lambda e, xsl=xsl: e.tensor_add(
                    x4t_sb[:, xsl * 512:(xsl + 1) * 512],
                    x4t_sb[:, xsl * 512:(xsl + 1) * 512],
                    x3t_sb[:, xsl * 512:(xsl + 1) * 512]), waits=(ldx3, dv_o))
                x3_cons[xsl] = dv_o
                x4_st[xsl] = p.dma("sync", lambda e, et=et, b=b, mc=mc, xsl=xsl: e.dma_start(
                    out=x4_h.ap()[et * 128:(et + 1) * 128,
                                  b * T + mc * 512:b * T + (mc + 1) * 512],
                    in_=x4t_sb[:, xsl * 512:(xsl + 1) * 512]), waits=(dv_o,))
    p.wait("sync", p.all_dma())
    p.stage_end()
    return p.finalize() if own else None


def prep_D_inputs(inputs, x3_shards, sinv_full):
    phi = np.asarray(inputs["L1_ffn_phi"]).reshape(1024, 2048)
    W = np.asarray(inputs["L1_ffn_W"])
    base = {
        "phiP": np.ascontiguousarray(
            phi.reshape(8, 128, 2048).transpose(1, 0, 2).reshape(128, 16384)),
        "WP": np.ascontiguousarray(
            W.reshape(64, 8, 128, 2, 512).transpose(0, 3, 2, 1, 4)
            .reshape(64 * 2 * 128, 4096)),
        "BH": np.ascontiguousarray(np.asarray(inputs["L1_ffn_b"], np.float32)),
        "SINV": np.ascontiguousarray(sinv_full.astype(np.float32)),
        "ones_f": CONSTS["ones_f"], "ident": CONSTS["ident"],
    }
    return [{"X3": x3_shards[c], **base} for c in range(NCORE)]


def prep_B2_inputs(inputs, x1_shards):
    NE = 18
    w1 = np.asarray(inputs["L0_ffn_w1"])
    w3 = np.asarray(inputs["L0_ffn_w3"])
    w2 = np.asarray(inputs["L0_ffn_w2"])
    b1 = np.asarray(inputs["L0_ffn_b1"])
    b3 = np.asarray(inputs["L0_ffn_b3"])
    sw1 = np.asarray(inputs["L0_ffn_sw1"])
    sw3 = np.asarray(inputs["L0_ffn_sw3"])
    sw2 = np.asarray(inputs["L0_ffn_sw2"])
    sb1 = np.asarray(inputs["L0_ffn_sb1"])
    sb3 = np.asarray(inputs["L0_ffn_sb3"])

    def pmaj_w1(wT):
        return wT.reshape(8, 128, 512).transpose(1, 0, 2).reshape(128, 4096)

    def pmaj_w2(wT):
        return wT.reshape(4, 128, 1024).transpose(1, 0, 2).reshape(128, 4096)

    w1P = np.zeros((128, NE * 4096), np.float32)
    w3P = np.zeros((128, NE * 4096), np.float32)
    w2P = np.zeros((128, NE * 4096), np.float32)
    b1c = np.zeros((128, NE * 4), np.float32)
    b3c = np.zeros((128, NE * 4), np.float32)
    for e in range(16):
        w1P[:, e * 4096:(e + 1) * 4096] = pmaj_w1(w1[e].T)
        w3P[:, e * 4096:(e + 1) * 4096] = pmaj_w1(w3[e].T)
        w2P[:, e * 4096:(e + 1) * 4096] = pmaj_w2(w2[e].T)
        b1c[:, e * 4:(e + 1) * 4] = b1[e].reshape(4, 128).T
        b3c[:, e * 4:(e + 1) * 4] = b3[e].reshape(4, 128).T
    sw1T = sw1.T
    sw3T = sw3.T
    sw2T = sw2.T
    for j in range(2):
        e = 16 + j
        w1P[:, e * 4096:(e + 1) * 4096] = pmaj_w1(sw1T[:, j * 512:(j + 1) * 512])
        w3P[:, e * 4096:(e + 1) * 4096] = pmaj_w1(sw3T[:, j * 512:(j + 1) * 512])
        w2P[:, e * 4096:(e + 1) * 4096] = pmaj_w2(sw2T[j * 512:(j + 1) * 512, :])
        b1c[:, e * 4:(e + 1) * 4] = sb1[j * 512:(j + 1) * 512].reshape(4, 128).T
        b3c[:, e * 4:(e + 1) * 4] = sb3[j * 512:(j + 1) * 512].reshape(4, 128).T
    oh = np.zeros((16, 16 * 128), np.float32)
    for e in range(16):
        oh[e, e * 128:(e + 1) * 128] = 1.0
    base = {
        "gwtP": np.ascontiguousarray(
            np.asarray(inputs["L0_ffn_gate_w"]).T.reshape(8, 128, 16)
            .transpose(1, 0, 2).reshape(128, 128)),
        "ebr": np.ascontiguousarray(np.asarray(inputs["L0_ffn_expert_bias"])[None, :]),
        "w1P": np.ascontiguousarray(w1P), "w3P": np.ascontiguousarray(w3P),
        "w2P": np.ascontiguousarray(w2P),
        "b1c": np.ascontiguousarray(b1c), "b3c": np.ascontiguousarray(b3c),
        "b2m": np.ascontiguousarray(np.asarray(inputs["L0_ffn_b2"])),
        "sb2c": np.ascontiguousarray(np.asarray(inputs["L0_ffn_sb2"]).reshape(8, 128).T),
        "ohall": np.ascontiguousarray(oh),
        "ones_f": CONSTS["ones_f"], "ident": CONSTS["ident"],
    }
    return [{"X1": x1_shards[c], **base} for c in range(NCORE)]


# version 04 (final)
# speedup vs baseline: 1.1355x; 1.1355x over previous
import dataclasses
from contextlib import ExitStack

import numpy as np
import ml_dtypes
import concourse.bass as bass
import concourse.mybir as mybir
from concourse.bass_utils import run_bass_kernel_spmd

F32 = mybir.dt.float32
BF16 = mybir.dt.bfloat16
AF = mybir.ActivationFunctionType
ALU = mybir.AluOpType
AXX = mybir.AxisListType.X

NCORE = 8
BPC = 2
T = 2048
NTOK = BPC * T
D = 1024
NH, HD = 16, 64
CHUNK = 256

ENGINES = ("sync", "gpsimd", "vector", "scalar", "tensor")


def _ap(t, apdims, off=0):
    return dataclasses.replace(t, ap=apdims, offset=t.offset + off)


class Prog:

    NPOOL = {"sync": 10, "gpsimd": 6}

    def __init__(self):
        self.nc = bass.Bass("TRN2", target_bir_lowering=False)
        self.ctx = ExitStack()
        self.streams = {e: [] for e in ENGINES}
        self.semnames = ["PE", "DV", "AC", "GP"]
        self.pools = {}
        for eng, n in self.NPOOL.items():
            self.pools[eng] = [f"{eng[:2].upper()}{i}" for i in range(n)]
            self.semnames += self.pools[eng]
        self.pool_idx = {eng: 0 for eng in self.NPOOL}
        self.cnt = {s: 0 for s in self.semnames}
        self._sems = None
        self.prefix = ""
        self._stage_stack = None
        self._dram = {}
        self.dram_kind_override = {}

    def sbuf(self, name, shape, dt):
        stk = self._stage_stack if self._stage_stack is not None else self.ctx
        return stk.enter_context(self.nc.sbuf_tensor(self.prefix + name, shape, dt))

    def psum(self, name, shape, dt=F32):
        stk = self._stage_stack if self._stage_stack is not None else self.ctx
        return stk.enter_context(self.nc.psum_tensor(self.prefix + name, shape, dt))

    def dram(self, name, shape, dt, kind=None):
        if name in self._dram:
            return self._dram[name]
        k = self.dram_kind_override.get(name, kind)
        t = (self.nc.dram_tensor(name, shape, dt, kind=k) if k
             else self.nc.dram_tensor(name, shape, dt))
        self._dram[name] = t
        return t

    def barrier(self):
        waits = tuple((s, self.cnt[s]) for s in self.semnames if self.cnt[s] > 0)
        if waits:
            for eng in ENGINES:
                self.wait(eng, waits)

    def stage_begin(self, tag):
        self.barrier()
        self.prefix = tag + "_"
        self._stage_stack = self.ctx.enter_context(ExitStack())

    def stage_end(self):
        self._stage_stack.close()
        self._stage_stack = None
        self.prefix = ""

    def op(self, eng, build, waits=()):
        s = {"vector": "DV", "scalar": "AC", "tensor": "PE", "gpsimd": "GP"}[eng]
        self.cnt[s] += 1
        val = self.cnt[s]
        wl = tuple(w for w in waits if w is not None)

        def f(e, _build=build, _wl=wl, _s=s):
            for (ws, wv) in _wl:
                if wv > 0:
                    e.wait_ge(self._sems[ws], wv)
            _build(e).then_inc(self._sems[_s], 1)
        self.streams[eng].append(f)
        return (s, val)

    def dma(self, eng, builders, waits=()):
        if not isinstance(builders, (list, tuple)):
            builders = [builders]
        pool = self.pools[eng]
        name = pool[self.pool_idx[eng] % len(pool)]
        self.pool_idx[eng] += 1
        wl = tuple(w for w in waits if w is not None)
        if self.cnt[name] > 0:
            wl = ((name, self.cnt[name]),) + wl
        for i, b in enumerate(builders):
            self.cnt[name] += 16

            def f(e, _b=b, _wl=(wl if i == 0 else ()), _name=name):
                for (ws, wv) in _wl:
                    if wv > 0:
                        e.wait_ge(self._sems[ws], wv)
                _b(e).then_inc(self._sems[_name], 16)
            self.streams[eng].append(f)
        return (name, self.cnt[name])

    def last(self, sem):
        return (sem, self.cnt[sem])

    def all_dma(self):
        return tuple((n, self.cnt[n]) for eng in self.pools
                     for n in self.pools[eng] if self.cnt[n] > 0)

    def wait(self, eng, waits):
        wl = tuple(w for w in waits if w is not None)

        def f(e, _wl=wl):
            for (ws, wv) in _wl:
                if wv > 0:
                    e.wait_ge(self._sems[ws], wv)
        self.streams[eng].append(f)

    def finalize(self):
        sems = {s: self.ctx.enter_context(self.nc.semaphore(s)) for s in self.semnames}
        self._sems = sems
        with self.ctx:
            with self.nc.Block() as block:
                for ename in ENGINES:
                    fns = self.streams[ename]
                    if not fns:
                        continue

                    def body(e, _fns=tuple(fns)):
                        for f in _fns:
                            f(e)
                    getattr(block, ename)(body)
        return self.nc


def host_consts():
    c = {}
    pos = np.arange(T, dtype=np.float64)
    inv_freq = 1.0 / 10000.0 ** (np.arange(0, HD, 2, dtype=np.float64) / HD)
    freqs = pos[:, None] * inv_freq[None, :]
    emb = np.concatenate([freqs, freqs], axis=-1)
    c["cos_t"] = np.ascontiguousarray(
        np.cos(emb).astype(np.float32).reshape(16, 128, 64).transpose(1, 0, 2).reshape(128, 1024))
    c["sin_t"] = np.ascontiguousarray(
        np.sin(emb).astype(np.float32).reshape(16, 128, 64).transpose(1, 0, 2).reshape(128, 1024))
    l_idx = np.arange(CHUNK)[None, :]
    j0 = np.arange(128)[:, None]
    c["mask0"] = (l_idx >= j0).astype(np.float32)
    c["mask1"] = (l_idx >= j0 + 128).astype(np.float32)
    c["ones_f"] = np.ones((128, 128), np.float32)
    c["ident"] = np.eye(128, dtype=np.float32)
    return c


CONSTS = host_consts()


def build_A(p=None):
    own = p is None
    if own:
        p = Prog()
    nc = p.nc
    p.stage_begin("a")

    x_h = p.dram("x", [D, NTOK], F32, kind="ExternalInput")
    wqkv_h = p.dram("w_qkv", [D, 3 * D], F32, kind="ExternalInput")
    wout_h = p.dram("w_out", [D, D], F32, kind="ExternalInput")
    g1r_h = p.dram("g1r", [1, D], F32, kind="ExternalInput")
    g1c_h = p.dram("g1c", [128, 8], F32, kind="ExternalInput")
    g2c_h = p.dram("g2c", [128, 8], F32, kind="ExternalInput")
    wdp1_h = p.dram("wdp1c", [128, 2048], F32, kind="ExternalInput")
    wdp2c_h = p.dram("wdp2c", [128, 2], F32, kind="ExternalInput")
    cos_h = p.dram("cos_t", [128, 1024], F32, kind="ExternalInput")
    sin_h = p.dram("sin_t", [128, 1024], F32, kind="ExternalInput")
    m0_h = p.dram("mask0", [128, CHUNK], F32, kind="ExternalInput")
    m1_h = p.dram("mask1", [128, CHUNK], F32, kind="ExternalInput")
    onesf_h = p.dram("ones_f", [128, 128], F32, kind="ExternalInput")
    ident_h = p.dram("ident", [128, 128], F32, kind="ExternalInput")

    v_h = p.dram("V", [D, NTOK], F32, kind="ExternalOutput")
    rp_h = p.dram("RP", [1, 1], F32, kind="ExternalOutput")
    rstd_h = p.dram("rstdA", [1, NTOK], F32)
    mrs_h = p.dram("mrsA", [1, NTOK], F32)
    attn_h = p.dram("attnA", [D, T], BF16)

    cos_sb = p.sbuf("cos_sb", [128, 1024], F32)
    sin_sb = p.sbuf("sin_sb", [128, 1024], F32)
    m0_sb = p.sbuf("m0_sb", [128, CHUNK], F32)
    m1_sb = p.sbuf("m1_sb", [128, CHUNK], F32)
    onesf_sb = p.sbuf("onesf_sb", [128, 128], F32)
    onesb_sb = p.sbuf("onesb_sb", [128, 128], BF16)
    identb_sb = p.sbuf("identb_sb", [128, 128], BF16)
    g1c_sb = p.sbuf("g1c_sb", [128, 8], F32)
    g2c_sb = p.sbuf("g2c_sb", [128, 8], F32)
    g1rb_sb = p.sbuf("g1rb_sb", [1, D], BF16)
    wdp1_sb = p.sbuf("wdp1_sb", [128, 2048], BF16)
    wdp2_sb = p.sbuf("wdp2_sb", [128, 2], BF16)

    wbig_sb = p.sbuf("wbig_sb", [128, 8 * 1536], BF16)
    wout_view = wbig_sb[:, 0:8 * 1024]
    x_sb = p.sbuf("x_sb", [128, 2 * 512], F32)
    h_sb = p.sbuf("h_sb", [128, 8 * 512], BF16)
    q_sb = p.sbuf("q_sb", [128, 16 * 1024], BF16)
    k_sb = p.sbuf("k_sb", [128, 16 * 1024], BF16)
    vv_sb = p.sbuf("vv_sb", [128, 16 * 1024], BF16)
    attn_sb = p.sbuf("attn_sb", [128, 8 * 512], BF16)
    qt_sb = p.sbuf("qt_sb", [64, 2048], BF16)
    kt_sb = p.sbuf("kt_sb", [64, 2048], BF16)
    vaug_sb = p.sbuf("vaug_sb", [128, 16 * 65], BF16)
    at_sb = p.sbuf("at_sb", [128, 2 * CHUNK], BF16)
    att_tmp2 = p.sbuf("att_tmp", [64, 2 * CHUNK], BF16)
    saug_sb = p.sbuf("saug_sb", [64, 65], BF16)
    s32_sb = p.sbuf("s32_sb", [64, 65], F32)
    rcp_sb = p.sbuf("rcp_sb", [128, 512], F32)
    rope_r = p.sbuf("rope_r", [128, 512], F32)
    rope_t1 = p.sbuf("rope_t1", [128, 512], F32)
    rope_e = p.sbuf("rope_e", [128, 512], F32)
    srow_sb = p.sbuf("srow_sb", [1, 2048], F32)
    rhob_sb = p.sbuf("rhob_sb", [1, 512], BF16)
    mrgb_sb = p.sbuf("mrgb_sb", [1, 512], BF16)
    mrsb_sb = p.sbuf("mrsb_sb", [1, 512], BF16)
    rb_sb = p.sbuf("rb_sb", [128, 512], F32)
    v_out = p.sbuf("v_out", [128, 2 * 512], F32)
    t_sb = p.sbuf("t_sb", [128, 512], F32)
    wr_sb = p.sbuf("wr_sb", [128, 512], F32)
    xm_sb = p.sbuf("xm_sb", [128, 16], F32)
    xmg_sb = p.sbuf("xmg_sb", [128, 16], BF16)
    red_sb = p.sbuf("red_sb", [128, 1], F32)
    dps_sb = p.sbuf("dps_sb", [2, 256], BF16)
    dpt_sb = p.sbuf("dpt_sb", [128, 2], BF16)
    rp_sb = p.sbuf("rp_sb", [1, 8], F32)

    PS = [p.psum(f"bank{i}", [128, 512]) for i in range(8)]
    PSB = [b_[:, :].bitcast(BF16) for b_ in PS]

    def s_(i):
        return srow_sb[0:1, i * 512:(i + 1) * 512]

    LD_C = p.dma("sync", [
        lambda e: e.dma_start(out=cos_sb[:, :], in_=cos_h.ap()),
        lambda e: e.dma_start(out=sin_sb[:, :], in_=sin_h.ap()),
        lambda e: e.dma_start(out=m0_sb[:, :], in_=m0_h.ap()),
        lambda e: e.dma_start(out=m1_sb[:, :], in_=m1_h.ap()),
        lambda e: e.dma_start(out=onesf_sb[:, :], in_=onesf_h.ap()),
        lambda e: e.dma_start(out=g1c_sb[:, :], in_=g1c_h.ap()),
        lambda e: e.dma_start(out=g2c_sb[:, :], in_=g2c_h.ap()),
    ])
    LG_C = p.dma("gpsimd", [
        lambda e: e.dma_start(out=onesb_sb[:, :], in_=onesf_h.ap()),
        lambda e: e.dma_start(out=identb_sb[:, :], in_=ident_h.ap()),
        lambda e: e.dma_start(out=g1rb_sb[:, :], in_=g1r_h.ap()),
        lambda e: e.dma_start(out=wdp1_sb[:, :], in_=wdp1_h.ap()),
        lambda e: e.dma_start(out=wdp2_sb[:, :], in_=wdp2c_h.ap()),
    ])

    prev_dvh = None
    st_attn = [None, None]

    for b in range(BPC):
        st_r_list = []
        st_m_list = []
        for g in range(4):
            t0 = b * T + g * 512
            pe_sx = None
            pe_ssq = None
            slot_ld = [None, None]
            for ci in range(8):
                sl_ = ci % 2
                ldx = p.dma("sync", lambda e, ci=ci, sl_=sl_, t0=t0: e.dma_start(
                    out=x_sb[:, sl_ * 512:(sl_ + 1) * 512],
                    in_=x_h.ap()[ci * 128:(ci + 1) * 128, t0:t0 + 512]),
                    waits=((pe_ssq, prev_dvh) if ci < 2 else (pe_sx, pe_ssq)))
                slot_ld[sl_] = ldx
                pe_sx = p.op("tensor", lambda e, sl_=sl_, ci=ci: e.matmul(
                    PS[0][0:1, :], onesf_sb[:, 0:1], x_sb[:, sl_ * 512:(sl_ + 1) * 512],
                    start=(ci == 0), stop=(ci == 7)),
                    waits=(ldx, LD_C, LG_C))
                ac_sq = p.op("scalar", lambda e, sl_=sl_: e.activation(
                    t_sb[:, :], x_sb[:, sl_ * 512:(sl_ + 1) * 512], AF.Square),
                    waits=(ldx, pe_ssq))
                pe_ssq = p.op("tensor", lambda e, ci=ci: e.matmul(
                    PS[1][0:1, :], onesf_sb[:, 0:1], t_sb[:, :],
                    start=(ci == 0), stop=(ci == 7)), waits=(ac_sq,))
            dv = p.op("vector", lambda e: e.tensor_scalar_mul(
                s_(0), PS[0][0:1, :], 1.0 / 1024), waits=(pe_ssq,))
            dv = p.op("vector", lambda e: e.tensor_scalar_mul(
                s_(1), PS[1][0:1, :], 1.0 / 1024), waits=(dv,))
            dv = p.op("vector", lambda e: e.tensor_mul(s_(2), s_(0), s_(0)), waits=(dv,))
            dv = p.op("vector", lambda e: e.tensor_sub(s_(1), s_(1), s_(2)), waits=(dv,))
            dv = p.op("vector", lambda e: e.tensor_scalar_add(s_(2), s_(1), 1e-5),
                      waits=(dv,))
            ac = p.op("scalar", lambda e: e.activation(s_(2), s_(2), AF.Sqrt), waits=(dv,))
            dv = p.op("vector", lambda e: e.reciprocal(s_(2), s_(2)), waits=(ac,))
            st_r = p.dma("sync", lambda e, t0=t0: e.dma_start(
                out=rstd_h.ap()[0:1, t0:t0 + 512], in_=s_(2)), waits=(dv,))
            st_r_list.append(st_r)
            dv = p.op("vector", lambda e: e.tensor_mul(s_(3), s_(2), s_(2)), waits=(dv,))
            dv = p.op("vector", lambda e: e.tensor_mul(s_(3), s_(3), s_(1)), waits=(dv,))
            dv = p.op("vector", lambda e: e.tensor_scalar_add(s_(3), s_(3), 1e-6),
                      waits=(dv,))
            ac = p.op("scalar", lambda e: e.activation(s_(3), s_(3), AF.Sqrt), waits=(dv,))
            dv = p.op("vector", lambda e: e.tensor_scalar_max(s_(3), s_(3), 1e-6),
                      waits=(ac,))
            dv = p.op("vector", lambda e: e.tensor_scalar_min(s_(3), s_(3), 1e6),
                      waits=(dv,))
            dv = p.op("vector", lambda e: e.reciprocal(s_(1), s_(3)), waits=(dv,))
            dv = p.op("vector", lambda e: e.tensor_mul(s_(3), s_(2), s_(1)), waits=(dv,))
            dv_rho = p.op("vector", lambda e: e.tensor_copy(rhob_sb[0:1, :], s_(3)),
                          waits=(dv,))
            dv = p.op("vector", lambda e: e.tensor_mul(s_(1), s_(0), s_(3)),
                      waits=(dv_rho,))
            dv_mrg = p.op("vector", lambda e: e.tensor_copy(mrgb_sb[0:1, :], s_(1)),
                          waits=(dv,))
            dv = p.op("vector", lambda e: e.tensor_mul(s_(1), s_(0), s_(2)),
                      waits=(dv_mrg, st_r))
            dv_mrs = p.op("vector", lambda e: e.tensor_scalar_mul(s_(1), s_(1), -1.0),
                          waits=(dv,))
            st_m = p.dma("sync", lambda e, t0=t0: e.dma_start(
                out=mrs_h.ap()[0:1, t0:t0 + 512], in_=s_(1)), waits=(dv_mrs,))
            st_m_list.append(st_m)
            pe_b = p.op("tensor", lambda e: e.matmul(
                PS[2][:, :], onesb_sb[0:1, :], rhob_sb[0:1, :],
                start=True, stop=True), waits=(dv_rho,))
            dv_rb = p.op("vector", lambda e: e.tensor_copy(rb_sb[:, :], PS[2][:, :]),
                         waits=(pe_b,))
            dvh = dv_rb
            ldh = [None, None]
            for ci in range(8):
                sl_ = ci % 2
                ldx = p.dma("sync", lambda e, sl_=sl_, ci=ci, t0=t0: e.dma_start(
                    out=x_sb[:, sl_ * 512:(sl_ + 1) * 512],
                    in_=x_h.ap()[ci * 128:(ci + 1) * 128, t0:t0 + 512]),
                    waits=((pe_ssq, p.last("AC")) if ci < 2 else (dvh,)))
                pe_r1 = p.op("tensor", lambda e, ci=ci: e.matmul(
                    PS[3][:, :], g1rb_sb[0:1, ci * 128:(ci + 1) * 128],
                    mrgb_sb[0:1, :], start=True, stop=True),
                    waits=(dvh if ci > 0 else dv_mrg,))
                dvt = p.op("vector", lambda e, ci=ci, sl_=sl_: e.scalar_tensor_tensor(
                    t_sb[:, :], x_sb[:, sl_ * 512:(sl_ + 1) * 512], g1c_sb[:, ci:ci + 1],
                    rb_sb[:, :], op0=ALU.mult, op1=ALU.mult),
                    waits=(dv_rb, ldx, p.last("DV"), p.last("PE")))
                dvh = p.op("vector", lambda e, ci=ci: e.tensor_sub(
                    h_sb[:, ci * 512:(ci + 1) * 512], t_sb[:, :], PS[3][:, :]),
                    waits=(dvt, pe_r1))
            dv_h = dvh
            for wh in range(2):
                lg_wq = p.dma("gpsimd", [
                    (lambda e, ci=ci, wh=wh: e.dma_start(
                        out=wbig_sb[:, ci * 1536:(ci + 1) * 1536],
                        in_=wqkv_h.ap()[ci * 128:(ci + 1) * 128,
                                        wh * 1536:(wh + 1) * 1536]))
                    for ci in range(8)], waits=(p.last("PE"),))
                for tt_ in range(4):
                    ttg = g * 4 + tt_
                    for sl3 in range(3):
                        sl = wh * 3 + sl3
                        bank = PS[4] if (tt_ * 3 + sl3) % 2 == 0 else PS[5]
                        pe_q = None
                        for ci in range(8):
                            pe_q = p.op("tensor",
                                        lambda e, ci=ci, tt_=tt_, sl3=sl3, bank=bank: e.matmul(
                                            bank[:, :],
                                            h_sb[:, ci * 512 + tt_ * 128:ci * 512 + tt_ * 128 + 128],
                                            wbig_sb[:, ci * 1536 + sl3 * 512:ci * 1536 + (sl3 + 1) * 512],
                                            start=(ci == 0), stop=(ci == 7)),
                                        waits=((dv_h, lg_wq, p.last("DV"), p.last("AC"))
                                               if ci == 0 else ()))
                        if sl < 4:
                            dst = q_sb if sl < 2 else k_sb
                            col0 = ttg * 1024 + (sl % 2) * 512
                            cos_rep = _ap(cos_sb[:, 0:64],
                                          [[1024, 128], [0, 8], [1, 64]], off=ttg * 64)
                            sin_rep = _ap(sin_sb[:, 0:64],
                                          [[1024, 128], [0, 8], [1, 64]], off=ttg * 64)
                            bank_ev = _ap(bank[:, 0:256], [[512, 128], [2, 256]])
                            bank_od = _ap(bank[:, 0:256], [[512, 128], [2, 256]], off=1)
                            r_ev = _ap(rope_r[:, 0:256], [[512, 128], [2, 256]])
                            r_od = _ap(rope_r[:, 0:256], [[512, 128], [2, 256]], off=1)
                            dv = p.op("vector", lambda e, bank_od=bank_od, r_ev=r_ev:
                                      e.tensor_scalar_mul(r_ev, bank_od, -1.0),
                                      waits=(pe_q,))
                            dv = p.op("vector", lambda e, bank_ev=bank_ev, r_od=r_od:
                                      e.tensor_copy(r_od, bank_ev), waits=(dv,))
                            dv = p.op("vector", lambda e, bank=bank, cos_rep=cos_rep:
                                      e.tensor_mul(rope_t1[:, :], bank[:, :], cos_rep),
                                      waits=(dv,))
                            dv = p.op("vector", lambda e, sin_rep=sin_rep:
                                      e.tensor_mul(rope_r[:, :], rope_r[:, :], sin_rep),
                                      waits=(dv,))
                            dv = p.op("vector", lambda e:
                                      e.tensor_add(rope_t1[:, :], rope_t1[:, :], rope_r[:, :]),
                                      waits=(dv,))
                            dv = p.op("vector", lambda e:
                                      e.tensor_scalar_min(rope_e[:, :], rope_t1[:, :], 0.0),
                                      waits=(dv,))
                            ac = p.op("scalar", lambda e:
                                      e.activation(rope_e[:, :], rope_e[:, :], AF.Exp),
                                      waits=(dv,))
                            dv = p.op("vector", lambda e, dst=dst, col0=col0:
                                      e.scalar_tensor_tensor(
                                          dst[:, col0:col0 + 512], rope_t1[:, :], 0.0,
                                          rope_e[:, :], op0=ALU.max, op1=ALU.add),
                                      waits=(ac,))
                        else:
                            col0 = ttg * 1024 + (sl - 4) * 512
                            ac = p.op("scalar", lambda e, col0=col0, bank=bank: e.activation(
                                vv_sb[:, col0:col0 + 512], bank[:, :], AF.Copy),
                                waits=(pe_q,))
            prev_dvh = p.last("DV")
        dv_qkv = p.last("DV")
        ac_qkv = p.last("AC")
        for hh in range(NH):
            dv = p.op("vector", lambda e: e.memset(vaug_sb[:, :], 1.0),
                      waits=(dv_qkv, ac_qkv, p.last("PE")))
            for ti in range(16):
                dv = p.op("vector", lambda e, ti=ti, hh=hh: e.tensor_copy(
                    vaug_sb[:, ti * 65:ti * 65 + 64],
                    vv_sb[:, ti * 1024 + hh * 64:ti * 1024 + hh * 64 + 64]),
                    waits=(dv,))
            dv_vaug = dv
            dv_t = dv_vaug
            for ti in range(16):
                for (src_, dstt) in ((q_sb, qt_sb), (k_sb, kt_sb)):
                    pe_t = p.op("tensor", lambda e, ti=ti, hh=hh, src_=src_: e.transpose(
                        PSB[6][0:64, 0:128],
                        src_[:, ti * 1024 + hh * 64:ti * 1024 + hh * 64 + 64],
                        identb_sb[:, :]), waits=(dv_t,))
                    dv_t = p.op("vector", lambda e, ti=ti, dstt=dstt: e.tensor_copy(
                        dstt[:, ti * 128:(ti + 1) * 128], PSB[6][0:64, 0:128]),
                        waits=(pe_t,))
            dv_tr = dv_t
            pe_state = None
            dv_attn = dv_tr
            for c in range(8):
                if c == 0:
                    dv_snap = p.op("vector", lambda e: e.memset(saug_sb[:, :], 0.0),
                                   waits=(dv_tr,))
                else:
                    dv_snap = p.op("vector", lambda e: e.tensor_copy(
                        saug_sb[:, :], s32_sb[:, :]), waits=(dv_s32,))
                pe_at0 = p.op("tensor", lambda e, c=c: e.matmul(
                    PS[2][0:128, 0:256], kt_sb[:, c * 256:c * 256 + 128],
                    qt_sb[:, c * 256:(c + 1) * 256], start=True, stop=True),
                    waits=(dv_attn,))
                pe_at1 = p.op("tensor", lambda e, c=c: e.matmul(
                    PS[3][0:128, 0:256], kt_sb[:, c * 256 + 128:(c + 1) * 256],
                    qt_sb[:, c * 256:(c + 1) * 256], start=True, stop=True))
                dv_at = p.op("vector", lambda e: e.tensor_mul(
                    at_sb[:, 0:256], PS[2][0:128, 0:256], m0_sb[:, :]), waits=(pe_at0,))
                dv_at = p.op("vector", lambda e: e.tensor_mul(
                    at_sb[:, 256:512], PS[3][0:128, 0:256], m1_sb[:, :]),
                    waits=(pe_at1, dv_at))
                pe_num = p.op("tensor", lambda e, c=c: e.matmul(
                    PS[1][0:65, 0:256], saug_sb[:, :], qt_sb[:, c * 256:(c + 1) * 256],
                    start=True, stop=False), waits=(dv_snap, dv_at))
                for jt in range(2):
                    ti = 2 * c + jt
                    pe_num = p.op("tensor", lambda e, ti=ti, jt=jt: e.matmul(
                        PS[1][0:65, 0:256], vaug_sb[:, ti * 65:(ti + 1) * 65],
                        at_sb[:, jt * 256:(jt + 1) * 256],
                        start=False, stop=(jt == 1)))
                for jt in range(2):
                    ti = 2 * c + jt
                    pe_state = p.op("tensor", lambda e, ti=ti, hh=hh, c=c, jt=jt: e.matmul(
                        PS[0][0:64, 0:65],
                        k_sb[:, ti * 1024 + hh * 64:ti * 1024 + hh * 64 + 64],
                        vaug_sb[:, ti * 65:(ti + 1) * 65],
                        start=(jt == 0), stop=(jt == 1)),
                        waits=((dv_snap,) if jt == 0 else ()))
                if c == 0:
                    dv_s32 = p.op("vector", lambda e: e.tensor_copy(
                        s32_sb[:, :], PS[0][0:64, 0:65]), waits=(pe_state,))
                else:
                    dv_s32 = p.op("vector", lambda e: e.tensor_add(
                        s32_sb[:, :], s32_sb[:, :], PS[0][0:64, 0:65]),
                        waits=(pe_state,))
                dv_d = p.op("vector", lambda e: e.tensor_scalar_max(
                    rcp_sb[64:65, 0:256], PS[1][64:65, 0:256], 1e-6), waits=(pe_num,))
                dv_d = p.op("vector", lambda e: e.reciprocal(
                    rcp_sb[64:65, 256:512], rcp_sb[64:65, 0:256]), waits=(dv_d,))
                pe_rb = p.op("tensor", lambda e: e.matmul(
                    PS[4][0:64, 0:256], onesf_sb[64:65, 0:64], rcp_sb[64:65, 256:512],
                    start=True, stop=True), waits=(dv_d,))
                dv_rb2 = p.op("vector", lambda e: e.tensor_copy(
                    rcp_sb[0:64, 0:256], PS[4][0:64, 0:256]), waits=(pe_rb,))
                ats = c % 2
                dv_attn = p.op("vector", lambda e, ats=ats: e.tensor_mul(
                    att_tmp2[:, ats * 256:(ats + 1) * 256], PS[1][0:64, 0:256],
                    rcp_sb[0:64, 0:256]),
                    waits=(dv_rb2, st_attn[ats]))
                st_attn[ats] = p.dma("sync", lambda e, hh=hh, c=c, ats=ats: e.dma_start(
                    out=attn_h.ap()[hh * 64:(hh + 1) * 64, c * 256:(c + 1) * 256],
                    in_=att_tmp2[:, ats * 256:(ats + 1) * 256]),
                    waits=(dv_attn,))
        lg_wo = p.dma("gpsimd", [
            (lambda e, fi=fi: e.dma_start(
                out=wout_view[:, fi * 1024:(fi + 1) * 1024],
                in_=wout_h.ap()[fi * 128:(fi + 1) * 128, :]))
            for fi in range(8)], waits=(p.last("PE"),))
        dv_attn_done = p.last("DV")
        st_attn_all = p.all_dma()
        slot_st = [None, None]
        slot_ac = [None, None]
        for g in range(4):
            t0 = b * T + g * 512
            tl0 = g * 512
            ld_at = p.dma("sync", [
                (lambda e, fi=fi, tl0=tl0: e.dma_start(
                    out=attn_sb[:, fi * 512:(fi + 1) * 512],
                    in_=attn_h.ap()[fi * 128:(fi + 1) * 128, tl0:tl0 + 512]))
                for fi in range(8)],
                waits=st_attn_all + (p.last("PE"),) if g == 0 else (p.last("PE"),))
            ldr = p.dma("sync", lambda e, t0=t0: e.dma_start(
                out=srow_sb[0:1, 0:512], in_=rstd_h.ap()[0:1, t0:t0 + 512]),
                waits=(p.last("DV"),) + tuple(st_r_list))
            ldm = p.dma("gpsimd", lambda e, t0=t0: e.dma_start(
                out=mrsb_sb[0:1, :], in_=mrs_h.ap()[0:1, t0:t0 + 512]),
                waits=tuple(st_m_list) + (p.last("PE"),))
            pe_rs = p.op("tensor", lambda e: e.matmul(
                PS[2][:, :], onesf_sb[0:1, 0:128], srow_sb[0:1, 0:512],
                start=True, stop=True), waits=(dv_attn_done, ldr))
            dv_rs = p.op("vector", lambda e: e.tensor_copy(rb_sb[:, :], PS[2][:, :]),
                         waits=(pe_rs,))
            pe_ss2 = None
            for do in range(8):
                bank = PS[4] if do % 2 == 0 else PS[5]
                sl_ = do % 2
                pe_w = None
                for fi in range(8):
                    pe_w = p.op("tensor", lambda e, fi=fi, do=do, bank=bank: e.matmul(
                        bank[:, :],
                        wout_view[:, fi * 1024 + do * 128:fi * 1024 + do * 128 + 128],
                        attn_sb[:, fi * 512:(fi + 1) * 512],
                        start=(fi == 0), stop=False),
                        waits=((lg_wo, ldm, ld_at, p.last("DV")) if fi == 0 else ()))
                pe_w = p.op("tensor", lambda e, bank=bank: e.matmul(
                    bank[:, :], onesb_sb[0:1, :], mrsb_sb[0:1, :],
                    start=False, stop=True))
                ldx = p.dma("sync", lambda e, sl_=sl_, do=do, t0=t0: e.dma_start(
                    out=x_sb[:, sl_ * 512:(sl_ + 1) * 512],
                    in_=x_h.ap()[do * 128:(do + 1) * 128, t0:t0 + 512]),
                    waits=((p.last("DV"), p.last("PE")) if do < 2 else ()))
                dv_t1 = p.op("vector", lambda e, sl_=sl_: e.tensor_mul(
                    t_sb[:, :], x_sb[:, sl_ * 512:(sl_ + 1) * 512], rb_sb[:, :]),
                    waits=(dv_rs, ldx, pe_ss2))
                dv_vd = p.op("vector", lambda e, sl_=sl_, bank=bank: e.tensor_add(
                    v_out[:, sl_ * 512:(sl_ + 1) * 512], t_sb[:, :], bank[:, :]),
                    waits=(dv_t1, pe_w, slot_st[sl_], slot_ac[sl_]))
                slot_st[sl_] = p.dma("sync", lambda e, sl_=sl_, do=do, t0=t0: e.dma_start(
                    out=v_h.ap()[do * 128:(do + 1) * 128, t0:t0 + 512],
                    in_=v_out[:, sl_ * 512:(sl_ + 1) * 512]),
                    waits=(dv_vd,))
                slot_ac[sl_] = ac_s = p.op("scalar", lambda e, sl_=sl_: e.activation(
                    t_sb[:, :], v_out[:, sl_ * 512:(sl_ + 1) * 512], AF.Square),
                    waits=(dv_vd, pe_ss2))
                pe_ss2 = p.op("tensor", lambda e, do=do: e.matmul(
                    PS[6][0:1, 0:512], onesf_sb[:, 0:1], t_sb[:, :],
                    start=(do == 0), stop=(do == 7)), waits=(ac_s,))
            dv = p.op("vector", lambda e: e.tensor_scalar_mul(
                s_(1), PS[6][0:1, 0:512], 1.0 / 1024), waits=(pe_ss2,))
            dv = p.op("vector", lambda e: e.tensor_scalar_add(s_(1), s_(1), 1e-6),
                      waits=(dv,))
            ac = p.op("scalar", lambda e: e.activation(s_(2), s_(1), AF.Sqrt), waits=(dv,))
            dv = p.op("vector", lambda e: e.tensor_scalar_max(s_(2), s_(2), 1e-6),
                      waits=(ac,))
            dv = p.op("vector", lambda e: e.tensor_scalar_min(s_(2), s_(2), 1e6),
                      waits=(dv,))
            dv = p.op("vector", lambda e: e.reciprocal(s_(1), s_(2)), waits=(dv,))
            dv = p.op("vector", lambda e: e.tensor_scalar_mul(s_(1), s_(1), 1.0 / T),
                      waits=(dv,))
            pe_wr = p.op("tensor", lambda e: e.matmul(
                PS[3][:, :], onesf_sb[0:1, 0:128], s_(1),
                start=True, stop=True), waits=(dv,))
            dv_wr = p.op("vector", lambda e: e.tensor_copy(wr_sb[:, :], PS[3][:, :]),
                         waits=(pe_wr,))
            dv_xm = dv_wr
            st_allv = (slot_st[0], slot_st[1])
            slot_dv = [dv_wr, dv_wr]
            for ci in range(8):
                sl_ = ci % 2
                ldv_ = p.dma("sync", lambda e, sl_=sl_, ci=ci, t0=t0: e.dma_start(
                    out=v_out[:, sl_ * 512:(sl_ + 1) * 512],
                    in_=v_h.ap()[ci * 128:(ci + 1) * 128, t0:t0 + 512]),
                    waits=st_allv + (slot_dv[sl_], slot_ac[sl_]))
                slot_st[sl_] = ldv_
                slot_dv[sl_] = dv_x1 = p.op("vector", lambda e, sl_=sl_: e.tensor_mul(
                    t_sb[:, :], v_out[:, sl_ * 512:(sl_ + 1) * 512], wr_sb[:, :]),
                    waits=(dv_xm, pe_ss2, ldv_))
                dv_r = p.op("vector", lambda e: e.reduce_sum(
                    red_sb[:, 0:1], t_sb[:, :], axis=AXX), waits=(dv_x1,))
                col = ci * 2 + b
                if g == 0:
                    dv_xm = p.op("vector", lambda e, col=col: e.tensor_copy(
                        xm_sb[:, col:col + 1], red_sb[:, 0:1]), waits=(dv_r,))
                else:
                    dv_xm = p.op("vector", lambda e, col=col: e.tensor_add(
                        xm_sb[:, col:col + 1], xm_sb[:, col:col + 1], red_sb[:, 0:1]),
                        waits=(dv_r,))
            prev_dvh = p.last("DV")
    dv = None
    for ci in range(8):
        dv = p.op("vector", lambda e, ci=ci: e.tensor_scalar(
            xmg_sb[:, ci * 2:ci * 2 + 2], xm_sb[:, ci * 2:ci * 2 + 2],
            g2c_sb[:, ci:ci + 1], None, op0=ALU.mult),
            waits=((p.last("DV"),) if ci == 0 else (dv,)))
    dv_xmg = dv
    pe_d1 = None
    for ci in range(8):
        pe_d1 = p.op("tensor", lambda e, ci=ci: e.matmul(
            PS[0][0:2, 0:256], xmg_sb[:, ci * 2:ci * 2 + 2],
            wdp1_sb[:, ci * 256:(ci + 1) * 256],
            start=(ci == 0), stop=(ci == 7)),
            waits=((dv_xmg, p.last("PE"), LG_C) if ci == 0 else ()))
    ac = p.op("scalar", lambda e: e.activation(dps_sb[:, :], PS[0][0:2, 0:256], AF.Silu),
              waits=(pe_d1,))
    pe_d2 = None
    dv = None
    for cj in range(2):
        pe_t = p.op("tensor", lambda e, cj=cj: e.transpose(
            PSB[1][0:128, 0:2], dps_sb[:, cj * 128:(cj + 1) * 128],
            identb_sb[0:2, 0:2]),
            waits=((ac,) if cj == 0 else (dv,)))
        dv = p.op("vector", lambda e: e.tensor_copy(dpt_sb[:, :], PSB[1][0:128, 0:2]),
                  waits=(pe_t,))
        pe_d2 = p.op("tensor", lambda e, cj=cj: e.matmul(
            PS[2][0:1, 0:2], wdp2_sb[:, cj:cj + 1], dpt_sb[:, :],
            start=(cj == 0), stop=(cj == 1)), waits=(dv,))
    ac = p.op("scalar", lambda e: e.activation(rp_sb[0:1, 0:2], PS[2][0:1, 0:2],
                                               AF.Sigmoid), waits=(pe_d2,))
    dv = p.op("vector", lambda e: e.tensor_scalar_add(
        rp_sb[0:1, 2:4], rp_sb[0:1, 0:2], 0.5), waits=(ac,))
    dv = p.op("vector", lambda e: e.tensor_scalar_max(
        rp_sb[0:1, 2:4], rp_sb[0:1, 2:4], 0.5), waits=(dv,))
    dv = p.op("vector", lambda e: e.tensor_scalar_min(
        rp_sb[0:1, 2:4], rp_sb[0:1, 2:4], 1.5), waits=(dv,))
    dv = p.op("vector", lambda e: e.reduce_sum(
        rp_sb[0:1, 4:5], rp_sb[0:1, 2:4], axis=AXX), waits=(dv,))
    p.dma("sync", lambda e: e.dma_start(out=rp_h.ap(), in_=rp_sb[0:1, 4:5]),
          waits=(dv,))
    p.wait("sync", p.all_dma())
    p.stage_end()
    return p.finalize() if own else None

def prep_A_inputs(inputs):
    x = np.asarray(inputs["L0_attn_x"])
    g1 = np.asarray(inputs["L0_attn_g1"])
    g2 = np.asarray(inputs["L0_attn_g2"])
    base = {
        "w_qkv": np.ascontiguousarray(inputs["L0_attn_w_qkv"]),
        "w_out": np.ascontiguousarray(inputs["L0_attn_w_out"]),
        "g1r": np.ascontiguousarray(g1[None, :]),
        "g1c": np.ascontiguousarray(g1.reshape(8, 128).T),
        "g2c": np.ascontiguousarray(g2.reshape(8, 128).T),
        "wdp1c": np.ascontiguousarray(
            np.asarray(inputs["L0_attn_w_dp1"]).reshape(8, 128, 256)
            .transpose(1, 0, 2).reshape(128, 2048)),
        "wdp2c": np.ascontiguousarray(
            np.asarray(inputs["L0_attn_w_dp2"]).reshape(2, 128).T),
        "cos_t": CONSTS["cos_t"], "sin_t": CONSTS["sin_t"],
        "mask0": CONSTS["mask0"], "mask1": CONSTS["mask1"],
        "ones_f": CONSTS["ones_f"], "ident": CONSTS["ident"],
    }
    maps = []
    for c in range(NCORE):
        xs = np.ascontiguousarray(
            x[BPC * c:BPC * (c + 1)].reshape(NTOK, D).T.astype(np.float32))
        maps.append({"x": xs, **base})
    return maps


def build_L2():
    p = Prog()
    p.dram_kind_override = {"X1": None, "X2": None}
    build_B1(p)
    build_B2(p)
    build_B3(p)
    build_B4(p)
    return p.finalize()


def prep_L2_inputs(inputs, x_shards, v_shards, rparts):
    m1 = prep_B1_inputs(inputs, x_shards, v_shards, rparts)
    m2 = prep_B2_inputs(inputs, [None] * NCORE)
    m3 = prep_B3_inputs(inputs, [None] * NCORE)
    m4 = prep_B4_inputs(inputs, [None] * NCORE)
    maps = []
    for c in range(NCORE):
        d = dict(m1[c])
        for m in (m2[c], m3[c], m4[c]):
            for k, v in m.items():
                if k not in ("X1", "X2", "X3") and v is not None:
                    d[k] = v
        maps.append(d)
    return maps


def kernel(**inputs):
    from concourse.bass_utils import run_bass_kernel_spmd
    cores = list(range(NCORE))
    x = np.asarray(inputs["L0_attn_x"], np.float32)
    x_shards = [np.ascontiguousarray(x[2 * c:2 * c + 2].reshape(4096, 1024).T)
                for c in range(NCORE)]

    res = run_bass_kernel_spmd(build_A(), prep_A_inputs(inputs), core_ids=cores)
    v_shards = [np.ascontiguousarray(res.results[c]["V"]) for c in range(NCORE)]
    rparts = np.array([res.results[c]["RP"][0, 0] for c in range(NCORE)], np.float32)

    res = run_bass_kernel_spmd(
        build_L2(), prep_L2_inputs(inputs, x_shards, v_shards, rparts),
        core_ids=cores)
    x3 = [np.ascontiguousarray(res.results[c]["X3"]) for c in range(NCORE)]
    sps = [res.results[c]["SP"] for c in range(NCORE)]

    mapsC = []
    for c in range(NCORE):
        sl = np.stack([sps[r][c * 256:(c + 1) * 256, :] for r in range(NCORE)])
        mapsC.append({"SPS": np.ascontiguousarray(sl.reshape(8, 256 * 2048))})
    res = run_bass_kernel_spmd(build_C(), mapsC, core_ids=cores)
    sinv = np.concatenate([res.results[c]["SR"] for c in range(NCORE)], axis=0)

    res = run_bass_kernel_spmd(build_D(), prep_D_inputs(inputs, x3, sinv),
                               core_ids=cores)
    out = np.empty((16, 2048, 1024), np.float32)
    for c in range(NCORE):
        y = res.results[c]["X4"]
        out[2 * c] = y[:, :2048].T
        out[2 * c + 1] = y[:, 2048:].T
    return out


def build_B1(p=None):
    own = p is None
    if own:
        p = Prog()
    nc = p.nc
    p.stage_begin("b1")
    FDF = 4608
    NDC = 36

    x_h = p.dram("x", [D, NTOK], F32, kind="ExternalInput")
    v_h = p.dram("V", [D, NTOK], F32, kind="ExternalInput")
    rpv_h = p.dram("rparts", [1, 8], F32, kind="ExternalInput")
    g2c_h = p.dram("g2c", [128, 8], F32, kind="ExternalInput")
    wgP_h = p.dram("wgP", [128, NDC * 1024], F32, kind="ExternalInput")
    wuP_h = p.dram("wuP", [128, NDC * 1024], F32, kind="ExternalInput")
    wd_h = p.dram("w_down", [FDF, D], F32, kind="ExternalInput")
    ghc_h = p.dram("ghc", [128, NDC], F32, kind="ExternalInput")
    iota_h = p.dram("iotaF", [128, NDC], F32, kind="ExternalInput")
    onesf_h = p.dram("ones_f", [128, 128], F32, kind="ExternalInput")

    x1_h = p.dram("X1", [D, NTOK], F32, kind="ExternalOutput")
    hid_h = p.dram("hidB1", [FDF, NTOK], BF16)

    onesf_sb = p.sbuf("onesf_sb", [128, 128], F32)
    onesb_sb = p.sbuf("onesb_sb", [128, 128], BF16)
    g2c_sb = p.sbuf("g2c_sb", [128, 8], F32)
    ghc_sb = p.sbuf("ghc_sb", [128, NDC], F32)
    iota_sb = p.sbuf("iota_sb", [128, NDC], F32)
    mcol_sb = p.sbuf("mcol_sb", [128, NDC], F32)
    szb_sb = p.sbuf("szb_sb", [128, 1], F32)
    h2_sb = p.sbuf("h2_sb", [128, 8 * NTOK], BF16)
    vg_sb = p.sbuf("vg_sb", [128, 8 * 512], F32)
    wg_sb = p.sbuf("wg_sb", [128, 2 * 1024], BF16)
    wu_sb = p.sbuf("wu_sb", [128, 2 * 1024], BF16)
    wd_sb = p.sbuf("wd_sb", [128, 2 * 1024], BF16)
    gs_sb = p.sbuf("gs_sb", [128, 512], BF16)
    hid_sb = p.sbuf("hid_sb", [128, 2 * 512], BF16)
    hld_sb = p.sbuf("hld_sb", [128, 2 * 512], BF16)
    hs_sb = p.sbuf("hs_sb", [128, 2 * 512], BF16)
    t_sb = p.sbuf("t_sb", [128, 512], F32)
    rb_sb = p.sbuf("rb_sb", [128, 512], F32)
    row_sb = p.sbuf("row_sb", [1, NTOK], F32)
    srow_sb = p.sbuf("srow_sb", [1, 2048], F32)
    rp_sb = p.sbuf("rp_sb", [1, 8], F32)
    xo_sb = p.sbuf("xo_sb", [128, 2 * 512], F32)

    PS = [p.psum(f"bank{i}", [128, 512]) for i in range(8)]

    def s_(i):
        return srow_sb[0:1, i * 512:(i + 1) * 512]

    LD_C = p.dma("sync", [
        lambda e: e.dma_start(out=onesf_sb[:, :], in_=onesf_h.ap()),
        lambda e: e.dma_start(out=g2c_sb[:, :], in_=g2c_h.ap()),
        lambda e: e.dma_start(out=ghc_sb[:, :], in_=ghc_h.ap()),
        lambda e: e.dma_start(out=iota_sb[:, :], in_=iota_h.ap()),
        lambda e: e.dma_start(out=rp_sb[0:1, :], in_=rpv_h.ap()),
    ])
    LG_C = p.dma("gpsimd", lambda e: e.dma_start(out=onesb_sb[:, :], in_=onesf_h.ap()))

    dv = p.op("vector", lambda e: e.reduce_sum(
        srow_sb[0:1, 1536:1537], rp_sb[0:1, 0:8], axis=AXX), waits=(LD_C,))
    dv = p.op("vector", lambda e: e.tensor_scalar_mul(
        srow_sb[0:1, 1536:1537], srow_sb[0:1, 1536:1537], 3072.0 / 16.0), waits=(dv,))
    dv = p.op("vector", lambda e: e.tensor_scalar_add(
        srow_sb[0:1, 1537:1538], srow_sb[0:1, 1536:1537], 2.0 ** 23 - 0.5), waits=(dv,))
    dv = p.op("vector", lambda e: e.tensor_scalar(
        srow_sb[0:1, 1536:1537], srow_sb[0:1, 1537:1538], 2.0 ** 23, None,
        op0=ALU.subtract), waits=(dv,))
    dv = p.op("vector", lambda e: e.tensor_scalar_max(
        srow_sb[0:1, 1536:1537], srow_sb[0:1, 1536:1537], 1.0), waits=(dv,))
    dv_sz = p.op("vector", lambda e: e.reciprocal(
        srow_sb[0:1, 1537:1538], srow_sb[0:1, 1536:1537]), waits=(dv,))
    pe_sz = p.op("tensor", lambda e: e.matmul(
        PS[0][:, 0:1], onesf_sb[0:1, :], srow_sb[0:1, 1536:1537],
        start=True, stop=True), waits=(dv_sz,))
    dv_szb = p.op("vector", lambda e: e.tensor_copy(szb_sb[:, :], PS[0][:, 0:1]),
                  waits=(pe_sz,))
    dv_mc = p.op("vector", lambda e: e.tensor_scalar(
        mcol_sb[:, :], iota_sb[:, :], szb_sb[:, 0:1], None, op0=ALU.is_lt),
        waits=(dv_szb,))

    prev_dv = None
    for b in range(BPC):
        for g in range(4):
            t0 = b * T + g * 512
            ldv = p.dma("sync", [
                (lambda e, ci=ci, t0=t0: e.dma_start(
                    out=vg_sb[:, ci * 512:(ci + 1) * 512],
                    in_=v_h.ap()[ci * 128:(ci + 1) * 128, t0:t0 + 512]))
                for ci in range(8)], waits=(prev_dv,))
            pe_ss = None
            for ci in range(8):
                ac_sq = p.op("scalar", lambda e, ci=ci: e.activation(
                    t_sb[:, :], vg_sb[:, ci * 512:(ci + 1) * 512], AF.Square),
                    waits=(ldv, pe_ss))
                pe_ss = p.op("tensor", lambda e, ci=ci: e.matmul(
                    PS[1][0:1, :], onesf_sb[:, 0:1], t_sb[:, :],
                    start=(ci == 0), stop=(ci == 7)), waits=(ac_sq,))
            dv = p.op("vector", lambda e: e.tensor_scalar_mul(
                s_(0), PS[1][0:1, :], 1.0 / 1024), waits=(pe_ss,))
            dv = p.op("vector", lambda e: e.tensor_scalar_add(s_(0), s_(0), 1e-6),
                      waits=(dv,))
            ac = p.op("scalar", lambda e: e.activation(s_(0), s_(0), AF.Sqrt),
                      waits=(dv,))
            dv = p.op("vector", lambda e: e.tensor_scalar_max(s_(0), s_(0), 1e-6),
                      waits=(ac,))
            dv = p.op("vector", lambda e: e.tensor_scalar_min(s_(0), s_(0), 1e6),
                      waits=(dv,))
            dv = p.op("vector", lambda e: e.reciprocal(s_(1), s_(0)), waits=(dv,))
            pe_b = p.op("tensor", lambda e: e.matmul(
                PS[2][:, :], onesf_sb[0:1, 0:128], s_(1), start=True, stop=True),
                waits=(dv,))
            dv_rb = p.op("vector", lambda e: e.tensor_copy(rb_sb[:, :], PS[2][:, :]),
                         waits=(pe_b,))
            for ci in range(8):
                prev_dv = p.op("vector", lambda e, ci=ci, t0=t0: e.scalar_tensor_tensor(
                    h2_sb[:, ci * 4096 + t0:ci * 4096 + t0 + 512],
                    vg_sb[:, ci * 512:(ci + 1) * 512], g2c_sb[:, ci:ci + 1],
                    rb_sb[:, :], op0=ALU.mult, op1=ALU.mult),
                    waits=(dv_rb,))
    dv_h2 = p.last("DV")

    dv_ssa = None
    st_hid = [None, None]
    ac_hid = [None, None]
    lg_w = [None, None]
    st_hid_all = []
    for dc in range(NDC):
        wsl = dc % 2
        lg_gu = p.dma("gpsimd", [
            (lambda e, dc=dc, wsl=wsl: e.dma_start(
                out=wg_sb[:, wsl * 1024:(wsl + 1) * 1024],
                in_=wgP_h.ap()[:, dc * 1024:(dc + 1) * 1024])),
            (lambda e, dc=dc, wsl=wsl: e.dma_start(
                out=wu_sb[:, wsl * 1024:(wsl + 1) * 1024],
                in_=wuP_h.ap()[:, dc * 1024:(dc + 1) * 1024])),
        ], waits=(lg_w[wsl],))
        pe_end = None
        for ts_ in range(8):
            tA = ts_ * 512
            bsl = ts_ % 2
            bankG = PS[2 + bsl]
            bankU = PS[4 + bsl]
            pe_g = None
            for ci in range(8):
                pe_g = p.op("tensor", lambda e, ci=ci, tA=tA, wsl=wsl, bankG=bankG: e.matmul(
                    bankG[:, :], wg_sb[:, wsl * 1024 + ci * 128:wsl * 1024 + (ci + 1) * 128],
                    h2_sb[:, ci * 4096 + tA:ci * 4096 + tA + 512],
                    start=(ci == 0), stop=(ci == 7)),
                    waits=((lg_gu, dv_h2, p.last("DV"), p.last("AC"))
                           if ci == 0 else ()))
            pe_u = None
            for ci in range(8):
                pe_u = p.op("tensor", lambda e, ci=ci, tA=tA, wsl=wsl, bankU=bankU: e.matmul(
                    bankU[:, :], wu_sb[:, wsl * 1024 + ci * 128:wsl * 1024 + (ci + 1) * 128],
                    h2_sb[:, ci * 4096 + tA:ci * 4096 + tA + 512],
                    start=(ci == 0), stop=(ci == 7)))
            ac_g = p.op("scalar", lambda e, bankG=bankG: e.activation(
                gs_sb[:, :], bankG[:, :], AF.Silu), waits=(pe_g, p.last("DV")))
            hsl = ts_ % 2
            dv_hd = p.op("vector", lambda e, dc=dc, hsl=hsl, bankU=bankU: e.scalar_tensor_tensor(
                hid_sb[:, hsl * 512:(hsl + 1) * 512], bankU[:, :],
                mcol_sb[:, dc:dc + 1], gs_sb[:, :], op0=ALU.mult, op1=ALU.mult),
                waits=(pe_u, ac_g, dv_mc, st_hid[hsl], ac_hid[hsl]))
            st_hid[hsl] = p.dma("sync", lambda e, dc=dc, tA=tA, hsl=hsl: e.dma_start(
                out=hid_h.ap()[dc * 128:(dc + 1) * 128, tA:tA + 512],
                in_=hid_sb[:, hsl * 512:(hsl + 1) * 512]),
                waits=(dv_hd,))
            st_hid_all.append(st_hid[hsl])
            ac_hid[hsl] = ac_pa = p.op("scalar", lambda e, hsl=hsl: e.activation(
                t_sb[:, :], hid_sb[:, hsl * 512:(hsl + 1) * 512], AF.Square),
                waits=(dv_hd, pe_end))
            pe_sq = p.op("tensor", lambda e: e.matmul(
                PS[6][0:1, :], onesf_sb[:, 0:1], t_sb[:, :],
                start=True, stop=True), waits=(ac_pa, dv_ssa))
            if dc == 0:
                dv_ssa = p.op("vector", lambda e, tA=tA: e.tensor_copy(
                    row_sb[0:1, tA:tA + 512], PS[6][0:1, :]), waits=(pe_sq,))
            else:
                dv_ssa = p.op("vector", lambda e, tA=tA: e.tensor_add(
                    row_sb[0:1, tA:tA + 512], row_sb[0:1, tA:tA + 512], PS[6][0:1, :]),
                    waits=(pe_sq,))
            pe_end = pe_sq
        lg_w[wsl] = pe_end
    dv_ssa_all = p.last("DV")
    st_hid_all = st_hid_all[-12:]

    dv_rr = None
    for gseg in range(8):
        tA = gseg * 512
        dv = p.op("vector", lambda e, tA=tA: e.tensor_scalar(
            row_sb[0:1, tA:tA + 512], row_sb[0:1, tA:tA + 512],
            srow_sb[0:1, 1537:1538], None, op0=ALU.mult), waits=(dv_ssa_all,))
        dv = p.op("vector", lambda e, tA=tA: e.tensor_scalar_add(
            row_sb[0:1, tA:tA + 512], row_sb[0:1, tA:tA + 512], 1e-6), waits=(dv,))
        ac = p.op("scalar", lambda e, tA=tA: e.activation(
            row_sb[0:1, tA:tA + 512], row_sb[0:1, tA:tA + 512], AF.Sqrt), waits=(dv,))
        dv = p.op("vector", lambda e, tA=tA: e.tensor_scalar_max(
            row_sb[0:1, tA:tA + 512], row_sb[0:1, tA:tA + 512], 1e-6), waits=(ac,))
        dv = p.op("vector", lambda e, tA=tA: e.tensor_scalar_min(
            row_sb[0:1, tA:tA + 512], row_sb[0:1, tA:tA + 512], 1e6), waits=(dv,))
        dv_rr = p.op("vector", lambda e, tA=tA: e.reciprocal(
            row_sb[0:1, tA:tA + 512], row_sb[0:1, tA:tA + 512]), waits=(dv,))

    all_st = p.all_dma()
    st_x = [None, None]
    dv_o2 = [None, None]
    for gseg in range(8):
        tA = gseg * 512
        pe_bc = p.op("tensor", lambda e, tA=tA: e.matmul(
            PS[0][:, :], onesf_sb[0:1, 0:128], row_sb[0:1, tA:tA + 512],
            start=True, stop=True),
            waits=(dv_rr, p.last("DV"), p.last("AC")))
        dv_rb2 = p.op("vector", lambda e: e.tensor_copy(rb_sb[:, :], PS[0][:, :]),
                      waits=(pe_bc,))
        pe_y = None
        dv_hs_prev = dv_rb2
        ld_hd = [None, None]
        lg_wd = [None, None]
        for dc in range(NDC):
            wsl = dc % 2
            ld_hd[wsl] = p.dma("sync", lambda e, dc=dc, tA=tA, wsl=wsl: e.dma_start(
                out=hld_sb[:, wsl * 512:(wsl + 1) * 512],
                in_=hid_h.ap()[dc * 128:(dc + 1) * 128, tA:tA + 512]),
                waits=(all_st if (dc < 2 and gseg == 0) else (dv_hs_prev,)))
            lg_wd[wsl] = p.dma("gpsimd", lambda e, dc=dc, wsl=wsl: e.dma_start(
                out=wd_sb[:, wsl * 1024:(wsl + 1) * 1024],
                in_=wd_h.ap()[dc * 128:(dc + 1) * 128, :]),
                waits=((pe_y,) if dc >= 2 else (p.last("PE"),)))
            dv_hs = p.op("vector", lambda e, dc=dc, wsl=wsl: e.scalar_tensor_tensor(
                hs_sb[:, wsl * 512:(wsl + 1) * 512], hld_sb[:, wsl * 512:(wsl + 1) * 512],
                ghc_sb[:, dc:dc + 1], rb_sb[:, :], op0=ALU.mult, op1=ALU.mult),
                waits=(ld_hd[wsl], dv_rb2, pe_y))
            for do in range(8):
                pe_y = p.op("tensor", lambda e, do=do, dc=dc, wsl=wsl: e.matmul(
                    PS[do][:, :], wd_sb[:, wsl * 1024 + do * 128:wsl * 1024 + (do + 1) * 128],
                    hs_sb[:, wsl * 512:(wsl + 1) * 512],
                    start=(dc == 0), stop=(dc == NDC - 1)),
                    waits=((dv_hs, lg_wd[wsl]) if do == 0 else ()))
            dv_hs_prev = dv_hs
        for do in range(8):
            osl = do % 2
            ldx = p.dma("sync", [
                (lambda e, do=do, tA=tA, osl=osl: e.dma_start(
                    out=xo_sb[:, osl * 512:(osl + 1) * 512],
                    in_=x_h.ap()[do * 128:(do + 1) * 128, tA:tA + 512])),
                (lambda e, do=do, tA=tA: e.dma_start(
                    out=vg_sb[:, do * 512:(do + 1) * 512],
                    in_=v_h.ap()[do * 128:(do + 1) * 128, tA:tA + 512])),
            ], waits=(st_x[osl], dv_o2[osl],
                      (p.last("DV") if (gseg == 0 and do < 2) else None)))
            dv_o = p.op("vector", lambda e, do=do, osl=osl: e.tensor_add(
                xo_sb[:, osl * 512:(osl + 1) * 512], xo_sb[:, osl * 512:(osl + 1) * 512],
                vg_sb[:, do * 512:(do + 1) * 512]),
                waits=(ldx, pe_y))
            dv_o2[osl] = dv_o = p.op("vector", lambda e, do=do, osl=osl: e.tensor_add(
                xo_sb[:, osl * 512:(osl + 1) * 512], xo_sb[:, osl * 512:(osl + 1) * 512],
                PS[do][:, :]), waits=(dv_o,))
            st_x[osl] = p.dma("sync", lambda e, do=do, tA=tA, osl=osl: e.dma_start(
                out=x1_h.ap()[do * 128:(do + 1) * 128, tA:tA + 512],
                in_=xo_sb[:, osl * 512:(osl + 1) * 512]),
                waits=(dv_o,))
    p.wait("sync", p.all_dma())
    p.stage_end()
    return p.finalize() if own else None

def prep_B1_inputs(inputs, x_shards, v_shards, rparts):
    wg = np.asarray(inputs["L0_attn_w_gate"])
    wu = np.asarray(inputs["L0_attn_w_up"])
    gh = np.asarray(inputs["L0_attn_g_hidden"])
    g2 = np.asarray(inputs["L0_attn_g2"])
    NDC = 36
    def pmaj(w):
        r = w.reshape(8, 128, NDC, 128).transpose(1, 2, 0, 3).reshape(128, NDC * 1024)
        return np.ascontiguousarray(r)
    base = {
        "g2c": np.ascontiguousarray(g2.reshape(8, 128).T),
        "wgP": pmaj(wg), "wuP": pmaj(wu),
        "w_down": np.ascontiguousarray(inputs["L0_attn_w_down"]),
        "ghc": np.ascontiguousarray(gh.reshape(NDC, 128).T),
        "iotaF": np.ascontiguousarray(
            np.arange(4608, dtype=np.float32).reshape(NDC, 128).T),
        "ones_f": CONSTS["ones_f"],
        "rparts": np.ascontiguousarray(rparts.astype(np.float32).reshape(1, 8)),
    }
    maps = []
    for c in range(NCORE):
        maps.append({"x": x_shards[c], "V": v_shards[c], **base})
    return maps


def build_B2(p=None, elist=None, halves=(0, 1)):
    own = p is None
    if own:
        p = Prog()
    nc = p.nc
    p.stage_begin("b2")
    NE = 18
    if elist is None:
        elist = list(range(NE))

    x1_h = p.dram("X1", [D, NTOK], F32, kind="ExternalInput")
    gwt_h = p.dram("gwtP", [128, 128], F32, kind="ExternalInput")
    ebr_h = p.dram("ebr", [1, 16], F32, kind="ExternalInput")
    w1P_h = p.dram("w1P", [128, NE * 4096], F32, kind="ExternalInput")
    w3P_h = p.dram("w3P", [128, NE * 4096], F32, kind="ExternalInput")
    w2P_h = p.dram("w2P", [128, NE * 4096], F32, kind="ExternalInput")
    b1c_h = p.dram("b1c", [128, NE * 4], F32, kind="ExternalInput")
    b3c_h = p.dram("b3c", [128, NE * 4], F32, kind="ExternalInput")
    b2m_h = p.dram("b2m", [16, D], F32, kind="ExternalInput")
    sb2c_h = p.dram("sb2c", [128, 8], F32, kind="ExternalInput")
    oh_h = p.dram("ohall", [16, 16 * 128], F32, kind="ExternalInput")
    onesf_h = p.dram("ones_f", [128, 128], F32, kind="ExternalInput")
    ident_h = p.dram("ident", [128, 128], F32, kind="ExternalInput")

    x2_h = p.dram("X2", [D, NTOK], F32, kind="ExternalOutput")
    dbg_gw_h = p.dram("DBG_gwT", [16, NTOK], F32, kind="ExternalOutput")

    onesf_sb = p.sbuf("onesf_sb", [128, 128], F32)
    onesb_sb = p.sbuf("onesb_sb", [128, 128], BF16)
    identb_sb = p.sbuf("identb_sb", [128, 128], BF16)
    gwt_sb = p.sbuf("gwt_sb", [128, 128], BF16)
    ebb_sb = p.sbuf("ebb_sb", [128, 16], F32)
    b1c_sb = p.sbuf("b1c_sb", [128, NE * 4], F32)
    b3c_sb = p.sbuf("b3c_sb", [128, NE * 4], F32)
    b2m_sb = p.sbuf("b2m_sb", [16, D], BF16)
    sb2c_sb = p.sbuf("sb2c_sb", [128, 8], F32)
    oh_sb = p.sbuf("oh_sb", [16, 16 * 128], BF16)
    xf_sb = p.sbuf("xf_sb", [128, 8 * NTOK], BF16)
    y_sb = p.sbuf("y_sb", [128, 8 * 2048], F32)
    w1_sb = p.sbuf("w1_sb", [128, 4096], BF16)
    w3_sb = p.sbuf("w3_sb", [128, 4096], BF16)
    w2_sb = p.sbuf("w2_sb", [128, 4096], BF16)
    gs_sb = p.sbuf("gs_sb", [128, 512], BF16)
    ub_sb = p.sbuf("ub_sb", [128, 512], BF16)
    hsc_sb = p.sbuf("hsc_sb", [128, 4 * 512], BF16)
    gbb_sb = p.sbuf("gbb_sb", [128, 512], BF16)
    lgt_sb = p.sbuf("lgt_sb", [128, 16 * 4], F32)
    gw_sb = p.sbuf("gw_sb", [128, 16], BF16)
    gwT_sb = p.sbuf("gwT_sb", [16, NTOK], BF16)
    srow_sb = p.sbuf("srow_sb", [1, 2048], F32)
    rowb_sb = p.sbuf("rowb_sb", [1, 512], BF16)
    rb_sb = p.sbuf("rb_sb", [128, 512], F32)
    t_sb = p.sbuf("t_sb", [128, 512], F32)
    xg_sb = p.sbuf("xg_sb", [128, 8 * 512], F32)
    m2_sb = p.sbuf("m2_sb", [128, 4], F32)

    PS = [p.psum(f"bank{i}", [128, 512]) for i in range(8)]
    PSB = [b_[:, :].bitcast(BF16) for b_ in PS]

    def s_(i):
        return srow_sb[0:1, i * 512:(i + 1) * 512]

    LD_C = p.dma("sync", [
        lambda e: e.dma_start(out=onesf_sb[:, :], in_=onesf_h.ap()),
        lambda e: e.dma_start(out=b1c_sb[:, :], in_=b1c_h.ap()),
        lambda e: e.dma_start(out=b3c_sb[:, :], in_=b3c_h.ap()),
        lambda e: e.dma_start(out=sb2c_sb[:, :], in_=sb2c_h.ap()),
        lambda e: e.dma_start(out=srow_sb[0:1, 0:16], in_=ebr_h.ap()),
    ])
    LG_C = p.dma("gpsimd", [
        lambda e: e.dma_start(out=onesb_sb[:, :], in_=onesf_h.ap()),
        lambda e: e.dma_start(out=identb_sb[:, :], in_=ident_h.ap()),
        lambda e: e.dma_start(out=b2m_sb[:, :], in_=b2m_h.ap()),
        lambda e: e.dma_start(out=oh_sb[:, :], in_=oh_h.ap()),
        lambda e: e.dma_start(out=gwt_sb[:, :], in_=gwt_h.ap()),
    ])
    pe_eb = p.op("tensor", lambda e: e.matmul(
        PS[0][:, 0:16], onesf_sb[0:1, 0:128], srow_sb[0:1, 0:16],
        start=True, stop=True), waits=(LD_C,))
    dv_eb = p.op("vector", lambda e: e.tensor_copy(ebb_sb[:, :], PS[0][:, 0:16]),
                 waits=(pe_eb,))

    prev_dv = None
    for b in range(BPC):
        for g in range(4):
            t0 = b * T + g * 512
            ldx = p.dma("sync", [
                (lambda e, ci=ci, t0=t0: e.dma_start(
                    out=xg_sb[:, ci * 512:(ci + 1) * 512],
                    in_=x1_h.ap()[ci * 128:(ci + 1) * 128, t0:t0 + 512]))
                for ci in range(8)], waits=(prev_dv,))
            pe_sx = None
            for ci in range(8):
                pe_sx = p.op("tensor", lambda e, ci=ci: e.matmul(
                    PS[1][0:1, :], onesf_sb[:, 0:1], xg_sb[:, ci * 512:(ci + 1) * 512],
                    start=(ci == 0), stop=(ci == 7)),
                    waits=((ldx,) if ci == 0 else ()))
            pe_ssq = pe_sx
            for ci in range(8):
                ac_sq = p.op("scalar", lambda e, ci=ci: e.activation(
                    t_sb[:, :], xg_sb[:, ci * 512:(ci + 1) * 512], AF.Square),
                    waits=(ldx, pe_ssq))
                pe_ssq = p.op("tensor", lambda e, ci=ci: e.matmul(
                    PS[2][0:1, :], onesf_sb[:, 0:1], t_sb[:, :],
                    start=(ci == 0), stop=(ci == 7)), waits=(ac_sq,))
            dv = p.op("vector", lambda e: e.tensor_scalar_mul(
                s_(0), PS[1][0:1, :], 1.0 / 1024), waits=(pe_sx, pe_ssq))
            dv = p.op("vector", lambda e: e.tensor_scalar_mul(
                s_(1), PS[2][0:1, :], 1.0 / 1024), waits=(dv,))
            dv = p.op("vector", lambda e: e.tensor_mul(s_(2), s_(0), s_(0)), waits=(dv,))
            dv = p.op("vector", lambda e: e.tensor_sub(s_(1), s_(1), s_(2)), waits=(dv,))
            dv = p.op("vector", lambda e: e.tensor_scalar_add(s_(1), s_(1), 1e-5),
                      waits=(dv,))
            ac = p.op("scalar", lambda e: e.activation(s_(1), s_(1), AF.Sqrt), waits=(dv,))
            dv = p.op("vector", lambda e: e.reciprocal(s_(1), s_(1)), waits=(ac,))
            dv = p.op("vector", lambda e: e.tensor_mul(s_(2), s_(0), s_(1)), waits=(dv,))
            dv_mr = p.op("vector", lambda e: e.tensor_copy(rowb_sb[0:1, :], s_(2)),
                         waits=(dv,))
            pe_b = p.op("tensor", lambda e: e.matmul(
                PS[3][:, :], onesb_sb[0:1, :], rowb_sb[0:1, :],
                start=True, stop=True), waits=(dv_mr, LG_C))
            pe_rs = p.op("tensor", lambda e: e.matmul(
                PS[4][:, :], onesf_sb[0:1, 0:128], s_(1), start=True, stop=True),
                waits=(dv_mr,))
            dv_rb = p.op("vector", lambda e: e.tensor_copy(rb_sb[:, :], PS[4][:, :]),
                         waits=(pe_rs,))
            for ci in range(8):
                dvt = p.op("vector", lambda e, ci=ci: e.tensor_mul(
                    t_sb[:, :], xg_sb[:, ci * 512:(ci + 1) * 512], rb_sb[:, :]),
                    waits=(dv_rb, pe_ssq, prev_dv))
                prev_dv = p.op("vector", lambda e, ci=ci, t0=t0: e.tensor_sub(
                    xf_sb[:, ci * 4096 + t0:ci * 4096 + t0 + 512], t_sb[:, :],
                    PS[3][:, :]), waits=(dvt, pe_b))
    dv_xf = p.last("DV")

    dv_gw = None
    for mt in range(32):
        tm = mt * 128
        tsl = mt % 4
        L = lgt_sb[:, tsl * 16:(tsl + 1) * 16]
        pe_l = None
        for ci in range(8):
            pe_l = p.op("tensor", lambda e, ci=ci, tm=tm: e.matmul(
                PS[5][0:128, 0:16], xf_sb[:, ci * 4096 + tm:ci * 4096 + tm + 128],
                gwt_sb[:, ci * 16:(ci + 1) * 16], start=(ci == 0), stop=(ci == 7)),
                waits=((dv_xf, LG_C, p.last("DV")) if ci == 0 else ()))
        dv = p.op("vector", lambda e, L=L: e.tensor_copy(L, PS[5][0:128, 0:16]),
                  waits=(pe_l,))
        ac = p.op("scalar", lambda e, L=L: e.activation(t_sb[:, 0:16], L, AF.Sigmoid),
                  waits=(dv,))
        dv = p.op("vector", lambda e, L=L: e.tensor_add(L, L, ebb_sb[:, :]),
                  waits=(dv, dv_eb, ac))
        dv = p.op("vector", lambda e, L=L: e.reduce_max(m2_sb[:, 0:1], L, axis=AXX),
                  waits=(dv,))
        dv = p.op("vector", lambda e, L=L: e.tensor_scalar(
            t_sb[:, 16:32], L, m2_sb[:, 0:1], None, op0=ALU.is_ge), waits=(dv,))
        dv = p.op("vector", lambda e, L=L: e.scalar_tensor_tensor(
            t_sb[:, 32:48], t_sb[:, 16:32], -1e30, L, op0=ALU.mult, op1=ALU.add),
            waits=(dv,))
        dv = p.op("vector", lambda e: e.reduce_max(m2_sb[:, 1:2], t_sb[:, 32:48],
                                                   axis=AXX), waits=(dv,))
        dv = p.op("vector", lambda e, L=L: e.tensor_scalar(
            t_sb[:, 16:32], L, m2_sb[:, 1:2], None, op0=ALU.is_ge), waits=(dv,))
        dv = p.op("vector", lambda e: e.tensor_mul(
            t_sb[:, 48:64], t_sb[:, 16:32], t_sb[:, 0:16]), waits=(dv,))
        dv = p.op("vector", lambda e: e.reduce_sum(m2_sb[:, 2:3], t_sb[:, 48:64],
                                                   axis=AXX), waits=(dv,))
        dv = p.op("vector", lambda e: e.tensor_scalar_add(
            m2_sb[:, 2:3], m2_sb[:, 2:3], 1e-10), waits=(dv,))
        dv = p.op("vector", lambda e: e.reciprocal(m2_sb[:, 3:4], m2_sb[:, 2:3]),
                  waits=(dv,))
        dv = p.op("vector", lambda e: e.tensor_scalar(
            gw_sb[:, :], t_sb[:, 48:64], m2_sb[:, 3:4], None, op0=ALU.mult),
            waits=(dv,))
        pe_t = p.op("tensor", lambda e: e.transpose(
            PSB[6][0:16, 0:128], gw_sb[:, :], identb_sb[:, :]),
            waits=(dv, dv_gw))
        dv_gw = p.op("vector", lambda e, tm=tm: e.tensor_copy(
            gwT_sb[:, tm:tm + 128], PSB[6][0:16, 0:128]), waits=(pe_t,))
    dv_gw_all = p.last("DV")
    for seg in range(2):
        p.dma("gpsimd", lambda e, seg=seg: e.dma_start(
            out=dbg_gw_h.ap()[:, seg * 2048:(seg + 1) * 2048],
            in_=gwT_sb[:, seg * 2048:(seg + 1) * 2048]),
            waits=(dv_gw_all,))

    st_x2_all = (None,)
    for half in halves:
        tH0 = half * 2048
        ld_y = p.dma("sync", [
            (lambda e, do=do, tH0=tH0: e.dma_start(
                out=y_sb[:, do * 2048:(do + 1) * 2048],
                in_=x1_h.ap()[do * 128:(do + 1) * 128, tH0:tH0 + 2048]))
            for do in range(8)],
            waits=(p.last("DV"),) + st_x2_all)
        pe_yend = p.last("PE")
        dv_yend = None
        lg_w = None
        for e_ in elist:
            lg_w = p.dma("gpsimd", [
                (lambda e, e_=e_: e.dma_start(
                    out=w1_sb[:, :], in_=w1P_h.ap()[:, e_ * 4096:(e_ + 1) * 4096])),
                (lambda e, e_=e_: e.dma_start(
                    out=w3_sb[:, :], in_=w3P_h.ap()[:, e_ * 4096:(e_ + 1) * 4096])),
                (lambda e, e_=e_: e.dma_start(
                    out=w2_sb[:, :], in_=w2P_h.ap()[:, e_ * 4096:(e_ + 1) * 4096])),
            ], waits=(pe_yend,))
            for ts_ in range(4):
                tA = tH0 + ts_ * 512
                if e_ < 16:
                    pe_gb = p.op("tensor", lambda e, e_=e_, tA=tA: e.matmul(
                        PS[6][:, :], oh_sb[0:16, e_ * 128:(e_ + 1) * 128],
                        gwT_sb[:, tA:tA + 512], start=True, stop=True),
                        waits=(dv_gw_all, p.last("DV"), ld_y))
                    dv_gb = p.op("vector", lambda e: e.tensor_copy(
                        gbb_sb[:, :], PS[6][:, :]), waits=(pe_gb,))
                else:
                    dv_gb = None
                dv_hs = None
                for fc in range(4):
                    bG = PS[0] if fc % 2 == 0 else PS[1]
                    bU = PS[2] if fc % 2 == 0 else PS[3]
                    pe_g = None
                    for ci in range(8):
                        pe_g = p.op("tensor", lambda e, ci=ci, fc=fc, tA=tA, bG=bG: e.matmul(
                            bG[:, :], w1_sb[:, ci * 512 + fc * 128:ci * 512 + (fc + 1) * 128],
                            xf_sb[:, ci * 4096 + tA:ci * 4096 + tA + 512],
                            start=(ci == 0), stop=(ci == 7)),
                            waits=((lg_w, ld_y, p.last("DV"), p.last("AC"))
                                   if ci == 0 else ()))
                    pe_u = None
                    for ci in range(8):
                        pe_u = p.op("tensor", lambda e, ci=ci, fc=fc, tA=tA, bU=bU: e.matmul(
                            bU[:, :], w3_sb[:, ci * 512 + fc * 128:ci * 512 + (fc + 1) * 128],
                            xf_sb[:, ci * 4096 + tA:ci * 4096 + tA + 512],
                            start=(ci == 0), stop=(ci == 7)))
                    ac_g = p.op("scalar", lambda e, e_=e_, fc=fc, bG=bG: e.activation(
                        gs_sb[:, :], bG[:, :], AF.Silu,
                        bias=b1c_sb[:, e_ * 4 + fc:e_ * 4 + fc + 1]),
                        waits=(pe_g, dv_hs))
                    if e_ < 16:
                        dv_u = p.op("vector", lambda e, e_=e_, fc=fc, bU=bU: e.scalar_tensor_tensor(
                            ub_sb[:, :], bU[:, :], b3c_sb[:, e_ * 4 + fc:e_ * 4 + fc + 1],
                            gbb_sb[:, :], op0=ALU.add, op1=ALU.mult),
                            waits=(pe_u, dv_gb, dv_hs))
                    else:
                        dv_u = p.op("vector", lambda e, e_=e_, fc=fc, bU=bU: e.tensor_scalar(
                            ub_sb[:, :], bU[:, :], b3c_sb[:, e_ * 4 + fc:e_ * 4 + fc + 1],
                            None, op0=ALU.add), waits=(pe_u, dv_hs))
                    dv_hs = p.op("vector", lambda e, fc=fc: e.tensor_mul(
                        hsc_sb[:, fc * 512:(fc + 1) * 512], gs_sb[:, :], ub_sb[:, :]),
                        waits=(dv_u, ac_g))
                for do in range(8):
                    pe_y = None
                    for fc in range(4):
                        pe_y = p.op("tensor", lambda e, do=do, fc=fc, e_=e_: e.matmul(
                            PS[do][:, :], w2_sb[:, fc * 1024 + do * 128:fc * 1024 + (do + 1) * 128],
                            hsc_sb[:, fc * 512:(fc + 1) * 512],
                            start=(fc == 0), stop=(fc == 3 and e_ != elist[0])),
                            waits=((dv_hs, dv_yend) if fc == 0 else ()))
                    if e_ == elist[0]:
                        pe_y = p.op("tensor", lambda e, do=do, tA=tA: e.matmul(
                            PS[do][:, :], b2m_sb[0:16, do * 128:(do + 1) * 128],
                            gwT_sb[:, tA:tA + 512], start=False, stop=True),
                            waits=(LG_C,))
                    pe_yend = pe_y
                for do in range(8):
                    dv_yend = p.op("vector", lambda e, do=do, ts_=ts_: e.tensor_add(
                        y_sb[:, do * 2048 + ts_ * 512:do * 2048 + (ts_ + 1) * 512],
                        y_sb[:, do * 2048 + ts_ * 512:do * 2048 + (ts_ + 1) * 512],
                        PS[do][:, :]),
                        waits=((pe_yend, ld_y) if do == 0 else (dv_yend,)))
        st_list = []
        st_slot = [None, None]
        for do in range(8):
            for ts_ in range(4):
                ssl = (do * 4 + ts_) % 2
                dv_o = p.op("vector", lambda e, do=do, ts_=ts_, ssl=ssl: e.tensor_scalar(
                    xg_sb[:, ssl * 512:(ssl + 1) * 512],
                    y_sb[:, do * 2048 + ts_ * 512:do * 2048 + (ts_ + 1) * 512],
                    sb2c_sb[:, do:do + 1], None, op0=ALU.add),
                    waits=(dv_yend, st_slot[ssl]))
                st_slot[ssl] = p.dma("sync", lambda e, do=do, ts_=ts_, tH0=tH0, ssl=ssl: e.dma_start(
                    out=x2_h.ap()[do * 128:(do + 1) * 128,
                                  tH0 + ts_ * 512:tH0 + (ts_ + 1) * 512],
                    in_=xg_sb[:, ssl * 512:(ssl + 1) * 512]), waits=(dv_o,))
                st_list.append(st_slot[ssl])
        st_x2_all = tuple(st_list[-10:])
    p.wait("sync", p.all_dma())
    p.stage_end()
    return p.finalize() if own else None

def build_B3(p=None):
    own = p is None
    if own:
        p = Prog()
    nc = p.nc
    p.stage_begin("b3")

    x2_h = p.dram("X2", [D, NTOK], F32, kind="ExternalInput")
    wqP_h = p.dram("WqkvP", [128, 24 * 1024], F32, kind="ExternalInput")
    bqc_h = p.dram("bqkvc", [128, 24], F32, kind="ExternalInput")
    wpP_h = p.dram("WprojP", [128, 8 * 1024], F32, kind="ExternalInput")
    bpc_h = p.dram("bprojc", [128, 8], F32, kind="ExternalInput")
    wb_h = p.dram("wb_lhs", [128, 16 * 128], F32, kind="ExternalInput")
    w2c2_h = p.dram("w2c2", [64, 16], F32, kind="ExternalInput")
    fb2c_h = p.dram("fb2c", [16, 1], F32, kind="ExternalInput")
    cpar_h = p.dram("cpar", [1, 1], F32, kind="ExternalInput")
    lmul_h = p.dram("lmul", [1, 1], F32, kind="ExternalInput")
    onesf_h = p.dram("ones_f", [128, 128], F32, kind="ExternalInput")
    ident_h = p.dram("ident", [128, 128], F32, kind="ExternalInput")

    x3_h = p.dram("X3", [D, NTOK], F32, kind="ExternalOutput")
    bias_h = p.dram("biasB3", [NH * T, T], BF16)
    q_hbm = p.dram("qB3", [D, T], BF16)
    ao_hbm = p.dram("aoB3", [D, T], BF16)

    onesf_sb = p.sbuf("onesf_sb", [128, 128], F32)
    identb_sb = p.sbuf("identb_sb", [128, 128], BF16)
    wb_sb = p.sbuf("wb_sb", [128, 16 * 128], BF16)
    w2c2_sb = p.sbuf("w2c2_sb", [64, 16], BF16)
    fb2c_sb = p.sbuf("fb2c_sb", [16, 1], F32)
    bqc_sb = p.sbuf("bqc_sb", [128, 24], F32)
    bpc_sb = p.sbuf("bpc_sb", [128, 8], F32)
    sc_sb = p.sbuf("sc_sb", [16, 8], F32)
    row1_sb = p.sbuf("row1_sb", [1, 8], F32)
    tc16_sb = p.sbuf("tc16_sb", [16, 128], F32)
    ctc_sb = p.sbuf("ctc_sb", [16, 128], F32)
    nd_f = p.sbuf("nd_f", [16, 512], F32)
    nd_t = p.sbuf("nd_t", [128, 512], BF16)
    hA_sb = p.sbuf("hA_sb", [64, 512], BF16)
    hB_sb = p.sbuf("hB_sb", [64, 512], BF16)
    stage_sb = p.sbuf("stage_sb", [16, 16 * 512], BF16)
    wq_sb = p.sbuf("wq_sb", [128, 2 * 1024], BF16)
    wp_sb = p.sbuf("wp_sb", [128, 8 * 1024], BF16)
    xg_sb = p.sbuf("xg_sb", [128, 8 * 512], F32)
    xln_sb = p.sbuf("xln_sb", [128, 8 * 512], BF16)
    kvt_sb = p.sbuf("kvt_sb", [128, 16 * 2048], BF16)
    qth_sb = p.sbuf("qth_sb", [128, 2048], BF16)
    vtm_sb = p.sbuf("vtm_sb", [128, 16 * 64], BF16)
    biast_sb = p.sbuf("biast_sb", [128, 4 * 512], BF16)
    sbrow_sb = p.sbuf("sbrow_sb", [128, 2 * 2048], F32)
    pn_sb = p.sbuf("pn_sb", [128, 2 * 2048], BF16)
    ptb_sb = p.sbuf("ptb_sb", [128, 2 * 2048], BF16)
    aost_sb = p.sbuf("aost_sb", [64, 512], BF16)
    ao_sb = p.sbuf("ao_sb", [128, 8 * 512], BF16)
    srow_sb = p.sbuf("srow_sb", [1, 2048], F32)
    rowb_sb = p.sbuf("rowb_sb", [1, 512], BF16)
    rb_sb = p.sbuf("rb_sb", [128, 512], F32)
    t_sb = p.sbuf("t_sb", [128, 512], F32)
    stg_sb = p.sbuf("stg_sb", [128, 2 * 512], BF16)
    xo_sb = p.sbuf("xo_sb", [128, 2 * 512], F32)
    mcol_sb = p.sbuf("mcol_sb", [128, 4], F32)

    PS = [p.psum(f"bank{i}", [128, 512]) for i in range(8)]
    PSB = [b_[:, :].bitcast(BF16) for b_ in PS]

    def s_(i):
        return srow_sb[0:1, i * 512:(i + 1) * 512]

    LD_C = p.dma("sync", [
        lambda e: e.dma_start(out=onesf_sb[:, :], in_=onesf_h.ap()),
        lambda e: e.dma_start(out=fb2c_sb[:, :], in_=fb2c_h.ap()),
        lambda e: e.dma_start(out=bqc_sb[:, :], in_=bqc_h.ap()),
        lambda e: e.dma_start(out=bpc_sb[:, :], in_=bpc_h.ap()),
        lambda e: e.dma_start(out=row1_sb[0:1, 0:1], in_=cpar_h.ap()),
        lambda e: e.dma_start(out=row1_sb[0:1, 1:2], in_=lmul_h.ap()),
    ])
    LG_C = p.dma("gpsimd", [
        lambda e: e.dma_start(out=identb_sb[:, :], in_=ident_h.ap()),
        lambda e: e.dma_start(out=wb_sb[:, :], in_=wb_h.ap()),
        lambda e: e.dma_start(out=w2c2_sb[:, :], in_=w2c2_h.ap()),
    ])

    ac = p.op("scalar", lambda e: e.activation(
        row1_sb[0:1, 2:3], row1_sb[0:1, 0:1], AF.Abs), waits=(LD_C,))
    ac = p.op("scalar", lambda e: e.activation(
        row1_sb[0:1, 3:4], row1_sb[0:1, 1:2], AF.Abs, scale=512.0), waits=(ac,))
    pe = p.op("tensor", lambda e: e.matmul(
        PS[0][0:16, 0:2], onesf_sb[0:1, 0:16], row1_sb[0:1, 2:4],
        start=True, stop=True), waits=(ac,))
    dv_sc = p.op("vector", lambda e: e.tensor_copy(sc_sb[:, 0:2], PS[0][0:16, 0:2]),
                 waits=(pe,))
    gp = p.op("gpsimd", lambda e: e.iota(
        tc16_sb[:, :], pattern=[[16, 128]], base=0, channel_multiplier=1,
        allow_small_or_imprecise_dtypes=True), waits=(LG_C,))
    dv = p.op("vector", lambda e: e.tensor_scalar(
        ctc_sb[:, :], tc16_sb[:, :], sc_sb[:, 1:2], None, op0=ALU.max),
        waits=(gp, dv_sc))
    ac = p.op("scalar", lambda e: e.activation(
        ctc_sb[:, :], ctc_sb[:, :], AF.Ln, bias=1.0, scale=sc_sb[:, 0:1]),
        waits=(dv,))
    dv = p.op("vector", lambda e: e.tensor_scalar_add(ctc_sb[:, :], ctc_sb[:, :], 1e-6),
              waits=(ac,))
    dv_ct = p.op("vector", lambda e: e.reciprocal(ctc_sb[:, :], ctc_sb[:, :]),
                 waits=(dv,))
    dv_nd0 = p.op("vector", lambda e: e.memset(nd_t[:, :], 0.0), waits=(LG_C,))
    dv_nd0 = p.op("vector", lambda e: e.memset(nd_t[64:80, :], 1.0), waits=(dv_nd0,))

    st_stage = None
    dv_ev_last = None
    gp_last = None
    for st in range(128):
        t0q = st * 16
        nsc = t0q // 512 + 1
        for sc in range(nsc):
            s0 = sc * 512
            gp_nd = p.op("gpsimd", lambda e, t0q=t0q, s0=s0: e.iota(
                nd_f[:, :], pattern=[[-1, 512]], base=t0q - s0, channel_multiplier=1,
                allow_small_or_imprecise_dtypes=True),
                waits=(p.last("AC"), gp_last))
            ac_nd = p.op("scalar", lambda e: e.activation(
                nd_f[:, :], nd_f[:, :], AF.Relu, scale=sc_sb[0:16, 0:1]),
                waits=(gp_nd, dv_sc))
            ac_nd = p.op("scalar", lambda e: e.activation(
                nd_f[:, :], nd_f[:, :], AF.Ln, bias=1.0), waits=(ac_nd,))
            dv_nd = p.op("vector", lambda e, st=st: e.tensor_scalar(
                nd_t[0:16, :], nd_f[:, :], ctc_sb[:, st:st + 1], None, op0=ALU.mult),
                waits=(ac_nd, dv_ct, dv_nd0, p.last("PE")))
            gp_last = gp_nd
            for v in range(4):
                pe_h = p.op("tensor", lambda e, v=v: e.matmul(
                    PS[4][:, :], wb_sb[:, v * 128:(v + 1) * 128], nd_t[:, :],
                    start=True, stop=True),
                    waits=(dv_nd, p.last("AC")))
                ac_A = p.op("scalar", lambda e: e.activation(
                    hA_sb[:, :], PS[4][0:64, :], AF.Relu), waits=(pe_h,))
                ac_B = p.op("scalar", lambda e: e.activation(
                    hB_sb[:, :], PS[4][64:128, :], AF.Relu), waits=(pe_h,))
                for j in range(4):
                    hsrc = hA_sb if j < 2 else hB_sb
                    base = (j % 2) * 32
                    t_loc = 4 * v + j
                    tg = t0q + t_loc
                    pe_e = p.op("tensor", lambda e, hsrc=hsrc, base=base: e.matmul(
                        PS[5][0:16, :], w2c2_sb[base:base + 32, :],
                        hsrc[base:base + 32, :], start=True, stop=True),
                        waits=(ac_A, ac_B, dv_ev_last))
                    dv_ev = p.op("vector", lambda e, t_loc=t_loc: e.tensor_scalar(
                        stage_sb[:, t_loc * 512:(t_loc + 1) * 512], PS[5][0:16, :],
                        fb2c_sb[:, 0:1], None, op0=ALU.add),
                        waits=(pe_e, st_stage, gp_last))
                    if s0 + 511 > tg:
                        def _aff(e, t_loc=t_loc, tg=tg, s0=s0):
                            if not hasattr(e, "_fill_reg"):
                                e._fill_reg = e.to_reg(-1e30)
                            return e.affine_select(
                                stage_sb[:, t_loc * 512:(t_loc + 1) * 512],
                                stage_sb[:, t_loc * 512:(t_loc + 1) * 512],
                                pattern=[[-1, 512]], compare_op=ALU.is_ge,
                                fill=e._fill_reg, base=tg - s0,
                                channel_multiplier=0)
                        gp_last = p.op("gpsimd", _aff, waits=(dv_ev,))
                        dv_ev_last = dv_ev
                    else:
                        dv_ev_last = dv_ev
            st_stage = p.dma("sync", lambda e, st=st, s0=s0: e.dma_start(
                out=_ap(bias_h.ap(), [[T * T, 16], [T, 16], [1, 512]],
                        off=st * 16 * T + s0),
                in_=_ap(stage_sb[0:16, 0:512], [[8192, 16], [512, 16], [1, 512]])),
                waits=(dv_ev_last, gp_last))
    bias_done = p.all_dma()

    prev_dv = None
    for b in range(BPC):
        lg_w = [None, None]
        st_q = [None, None]
        for g in range(4):
            t0 = b * T + g * 512
            tl0 = g * 512
            ldx = p.dma("sync", [
                (lambda e, ci=ci, t0=t0: e.dma_start(
                    out=xg_sb[:, ci * 512:(ci + 1) * 512],
                    in_=x2_h.ap()[ci * 128:(ci + 1) * 128, t0:t0 + 512]))
                for ci in range(8)], waits=(prev_dv, p.last("PE")))
            pe_sx = None
            for ci in range(8):
                pe_sx = p.op("tensor", lambda e, ci=ci: e.matmul(
                    PS[1][0:1, :], onesf_sb[:, 0:1], xg_sb[:, ci * 512:(ci + 1) * 512],
                    start=(ci == 0), stop=(ci == 7)),
                    waits=((ldx, LD_C, p.last("DV")) if ci == 0 else ()))
            pe_ssq = pe_sx
            for ci in range(8):
                ac_sq = p.op("scalar", lambda e, ci=ci: e.activation(
                    t_sb[:, :], xg_sb[:, ci * 512:(ci + 1) * 512], AF.Square),
                    waits=(ldx, pe_ssq, p.last("DV") if ci == 0 else None))
                pe_ssq = p.op("tensor", lambda e, ci=ci: e.matmul(
                    PS[2][0:1, :], onesf_sb[:, 0:1], t_sb[:, :],
                    start=(ci == 0), stop=(ci == 7)), waits=(ac_sq,))
            dv = p.op("vector", lambda e: e.tensor_scalar_mul(
                s_(0), PS[1][0:1, :], 1.0 / 1024), waits=(pe_sx, pe_ssq))
            dv = p.op("vector", lambda e: e.tensor_scalar_mul(
                s_(1), PS[2][0:1, :], 1.0 / 1024), waits=(dv,))
            dv = p.op("vector", lambda e: e.tensor_mul(s_(2), s_(0), s_(0)), waits=(dv,))
            dv = p.op("vector", lambda e: e.tensor_sub(s_(1), s_(1), s_(2)), waits=(dv,))
            dv = p.op("vector", lambda e: e.tensor_scalar_add(s_(1), s_(1), 1e-5),
                      waits=(dv,))
            ac = p.op("scalar", lambda e: e.activation(s_(1), s_(1), AF.Sqrt), waits=(dv,))
            dv = p.op("vector", lambda e: e.reciprocal(s_(1), s_(1)), waits=(ac,))
            dv = p.op("vector", lambda e: e.tensor_mul(s_(2), s_(0), s_(1)), waits=(dv,))
            dv_mr = p.op("vector", lambda e: e.tensor_copy(rowb_sb[0:1, :], s_(2)),
                         waits=(dv,))
            pe_b = p.op("tensor", lambda e: e.matmul(
                PS[3][:, :], identb_sb[0:1, 0:1] if False else onesf_sb[0:1, 0:128],
                s_(2), start=True, stop=True), waits=(dv_mr,))
            pe_rs = p.op("tensor", lambda e: e.matmul(
                PS[6][:, :], onesf_sb[0:1, 0:128], s_(1), start=True, stop=True),
                waits=(dv_mr,))
            dv_rb = p.op("vector", lambda e: e.tensor_copy(rb_sb[:, :], PS[6][:, :]),
                         waits=(pe_rs,))
            dv_mb = p.op("vector", lambda e: e.tensor_copy(t_sb[:, :], PS[3][:, :]),
                         waits=(pe_b, dv_rb))
            dvx = None
            for ci in range(8):
                dvx = p.op("vector", lambda e, ci=ci: e.tensor_mul(
                    xln_sb[:, ci * 512:(ci + 1) * 512],
                    xg_sb[:, ci * 512:(ci + 1) * 512], rb_sb[:, :]),
                    waits=((dv_mb, pe_ssq) if ci == 0 else (dvx,)))
                dvx = p.op("vector", lambda e, ci=ci: e.tensor_sub(
                    xln_sb[:, ci * 512:(ci + 1) * 512],
                    xln_sb[:, ci * 512:(ci + 1) * 512], t_sb[:, :]),
                    waits=(dvx,))
            dv_xl = dvx
            pe_q_slot = [None, None]
            for fc in range(24):
                wsl = fc % 2
                lg_w[wsl] = p.dma("gpsimd", lambda e, fc=fc, wsl=wsl: e.dma_start(
                    out=wq_sb[:, wsl * 1024:(wsl + 1) * 1024],
                    in_=wqP_h.ap()[:, fc * 1024:(fc + 1) * 1024]),
                    waits=((pe_q_slot[wsl],) if pe_q_slot[wsl] else (LG_C, p.last("PE"))))
                bank = PS[4] if fc % 2 == 0 else PS[5]
                pe_q = None
                for ci in range(8):
                    pe_q = p.op("tensor", lambda e, ci=ci, fc=fc, wsl=wsl, bank=bank: e.matmul(
                        bank[:, :], wq_sb[:, wsl * 1024 + ci * 128:wsl * 1024 + (ci + 1) * 128],
                        xln_sb[:, ci * 512:(ci + 1) * 512],
                        start=(ci == 0), stop=(ci == 7)),
                        waits=((dv_xl, lg_w[wsl], p.last("AC"), st_q[wsl])
                               if ci == 0 else ()))
                pe_q_slot[wsl] = pe_q
                if fc < 8:
                    ssl = fc % 2
                    ac_e = p.op("scalar", lambda e, fc=fc, ssl=ssl, bank=bank: e.activation(
                        stg_sb[:, ssl * 512:(ssl + 1) * 512], bank[:, :], AF.Identity,
                        bias=bqc_sb[:, fc:fc + 1]), waits=(pe_q, st_q[ssl]))
                    st_q[ssl] = p.dma("sync", lambda e, fc=fc, tl0=tl0, ssl=ssl: e.dma_start(
                        out=q_hbm.ap()[fc * 128:(fc + 1) * 128, tl0:tl0 + 512],
                        in_=stg_sb[:, ssl * 512:(ssl + 1) * 512]), waits=(ac_e,))
                else:
                    ac_e = p.op("scalar", lambda e, fc=fc, tl0=tl0, bank=bank: e.activation(
                        kvt_sb[:, (fc - 8) * 2048 + tl0:(fc - 8) * 2048 + tl0 + 512],
                        bank[:, :], AF.Identity, bias=bqc_sb[:, fc:fc + 1]),
                        waits=(pe_q, prev_dv))
            prev_dv = p.last("DV")
        dv_qkv = p.last("DV")
        ac_qkv = p.last("AC")
        q_done = p.all_dma()
        sb_cons = [dv_qkv, dv_qkv, ac_qkv, ac_qkv]
        if b == 0:
            st_ao = None
            dv_ao = None
            ld_bias = [None] * 4
            bia_cons = [None] * 4
            ac_cons = [None, None]
            pn_cons = [None, None]
            tp_cons = [None, None]
            pe_ao_g = [None, None]
            sc_i = 0
        SCB = (0, 1, 4, 5)
        for hh in range(NH):
            hb = (hh % 2) * 64
            ld_q = p.dma("sync", lambda e, hh=hh, hb=hb: e.dma_start(
                out=qth_sb[hb:hb + 64, :], in_=q_hbm.ap()[hh * 64:(hh + 1) * 64, :]),
                waits=q_done + (p.last("PE"),))
            for ti in range(16):
                vfc = 8 + hh // 2
                pe_t = p.op("tensor", lambda e, ti=ti, vfc=vfc, hb=hb: e.transpose(
                    PSB[7][0:128, 0:64],
                    kvt_sb[hb:hb + 64, vfc * 2048 + ti * 128:vfc * 2048 + (ti + 1) * 128],
                    identb_sb[hb:hb + 64, hb:hb + 64]),
                    waits=((ac_qkv, p.last("DV")) if ti == 0 else (dv_vt,)))
                dv_vt = p.op("vector", lambda e, ti=ti: e.tensor_copy(
                    vtm_sb[:, ti * 64:(ti + 1) * 64], PSB[7][0:128, 0:64]),
                    waits=(pe_t,))
            pn_state = [None] * 16
            dv_vt_h = dv_vt

            def trans_ao(q, hh=hh, dv_vt_h=dv_vt_h, pn_state=pn_state):
                nonlocal st_ao, dv_ao
                qp = q % 2
                nkt = q + 1
                dvpts = []
                for kt in range(nkt):
                    tb = kt % 2
                    pe_pt = p.op("tensor", lambda e, kt=kt, qp=qp, tb=tb: e.transpose(
                        PSB[6 + tb][0:128, 0:128],
                        pn_sb[:, qp * 2048 + kt * 128:qp * 2048 + (kt + 1) * 128],
                        identb_sb[:, :]),
                        waits=(pn_state[q] if kt < 2 else None, tp_cons[tb]))
                    pn_cons[qp] = pe_pt
                    dv_pt = p.op("vector", lambda e, kt=kt, qp=qp, tb=tb: e.tensor_copy(
                        ptb_sb[:, qp * 2048 + kt * 128:qp * 2048 + (kt + 1) * 128],
                        PSB[6 + tb][0:128, 0:128]),
                        waits=(pe_pt, pe_ao_g[qp] if kt == 0 else None))
                    tp_cons[tb] = dv_pt
                    dvpts.append(dv_pt)
                pe_ao = None
                for kt in range(nkt):
                    pe_ao = p.op("tensor", lambda e, kt=kt, qp=qp, nkt=nkt: e.matmul(
                        PS[2][0:64, 0:128], vtm_sb[:, kt * 64:(kt + 1) * 64],
                        ptb_sb[:, qp * 2048 + kt * 128:qp * 2048 + (kt + 1) * 128],
                        start=(kt == 0), stop=(kt == nkt - 1)),
                        waits=(dvpts[kt], dv_vt_h, dv_ao if kt == 0 else None))
                pe_ao_g[qp] = pe_ao
                qsl = q % 4
                dv_ao = p.op("vector", lambda e, qsl=qsl: e.tensor_copy(
                    aost_sb[:, qsl * 128:(qsl + 1) * 128], PS[2][0:64, 0:128]),
                    waits=(pe_ao, st_ao))
                if qsl == 3:
                    st_ao = p.dma("sync", lambda e, hh=hh, q=q: e.dma_start(
                        out=ao_hbm.ap()[hh * 64:(hh + 1) * 64,
                                        (q - 3) * 128:(q + 1) * 128],
                        in_=aost_sb[:, :]), waits=(dv_ao,))

            for qt_ in range(16):
                nsc = qt_ // 4 + 1
                kfc = hh // 2
                qp = qt_ % 2
                tq = qp * 4
                dv_sb2 = None
                for kc in range(nsc):
                    sl4 = sc_i % 4
                    bank = PS[SCB[sl4]]
                    ld_bias[sl4] = p.dma("sync", lambda e, hh=hh, qt_=qt_, kc=kc, sl4=sl4: e.dma_start(
                        out=biast_sb[:, sl4 * 512:(sl4 + 1) * 512],
                        in_=bias_h.ap()[hh * T + qt_ * 128:hh * T + (qt_ + 1) * 128,
                                        kc * 512:(kc + 1) * 512]),
                        waits=(bias_done if (sc_i == 0 and b == 0)
                               else (bia_cons[sl4],)))
                    pe_s = p.op("tensor", lambda e, qt_=qt_, kc=kc, kfc=kfc, hb=hb, bank=bank: e.matmul(
                        bank[:, :], qth_sb[hb:hb + 64, qt_ * 128:(qt_ + 1) * 128],
                        kvt_sb[hb:hb + 64, kfc * 2048 + kc * 512:kfc * 2048 + (kc + 1) * 512],
                        start=True, stop=True),
                        waits=(ld_q if kc == 0 else None, sb_cons[sl4]))
                    dv_sb2 = p.op("vector", lambda e, kc=kc, sl4=sl4, qp=qp, bank=bank: e.scalar_tensor_tensor(
                        sbrow_sb[:, qp * 2048 + kc * 512:qp * 2048 + (kc + 1) * 512],
                        bank[:, :], 0.125,
                        biast_sb[:, sl4 * 512:(sl4 + 1) * 512],
                        op0=ALU.mult, op1=ALU.add),
                        waits=(pe_s, ld_bias[sl4], ac_cons[qp] if kc == 0 else None))
                    sb_cons[sl4] = dv_sb2
                    bia_cons[sl4] = dv_sb2
                    sc_i += 1
                dv_m = p.op("vector", lambda e, nsc=nsc, qp=qp, tq=tq: e.reduce_max(
                    t_sb[:, tq:tq + 1], sbrow_sb[:, qp * 2048:qp * 2048 + nsc * 512],
                    axis=AXX), waits=(dv_sb2,))
                dv_m = p.op("vector", lambda e, tq=tq: e.tensor_scalar_mul(
                    t_sb[:, tq + 1:tq + 2], t_sb[:, tq:tq + 1], -1.0), waits=(dv_m,))
                ac_x = p.op("scalar", lambda e, nsc=nsc, qp=qp, tq=tq: e.activation(
                    pn_sb[:, qp * 2048:qp * 2048 + nsc * 512],
                    sbrow_sb[:, qp * 2048:qp * 2048 + nsc * 512], AF.Exp,
                    bias=t_sb[:, tq + 1:tq + 2], accum_out=t_sb[:, tq + 2:tq + 3]),
                    waits=(dv_m, pn_cons[qp]))
                ac_cons[qp] = ac_x
                dv_rc = p.op("vector", lambda e, tq=tq: e.reciprocal(
                    t_sb[:, tq + 3:tq + 4], t_sb[:, tq + 2:tq + 3]), waits=(ac_x,))
                dv_pn = p.op("vector", lambda e, nsc=nsc, qp=qp, tq=tq: e.tensor_scalar(
                    pn_sb[:, qp * 2048:qp * 2048 + nsc * 512],
                    pn_sb[:, qp * 2048:qp * 2048 + nsc * 512], t_sb[:, tq + 3:tq + 4],
                    None, op0=ALU.mult), waits=(dv_rc,))
                pn_state[qt_] = dv_pn
                if qt_ >= 1:
                    trans_ao(qt_ - 1)
            trans_ao(15)
        ao_done = p.all_dma()
        lg_wp = p.dma("gpsimd", lambda e: e.dma_start(
            out=wp_sb[:, :], in_=wpP_h.ap()), waits=(p.last("PE"),))
        st_x3 = [None, None]
        dv_x3 = [None, None]
        for g in range(4):
            t0 = b * T + g * 512
            tl0 = g * 512
            ld_ao = p.dma("sync", [
                (lambda e, ci=ci, tl0=tl0: e.dma_start(
                    out=ao_sb[:, ci * 512:(ci + 1) * 512],
                    in_=ao_hbm.ap()[ci * 128:(ci + 1) * 128, tl0:tl0 + 512]))
                for ci in range(8)],
                waits=(ao_done + (p.last("PE"),) if g == 0 else (p.last("PE"),)))
            for do in range(8):
                bank = PS[4] if do % 2 == 0 else PS[5]
                osl = do % 2
                pe_pj = None
                for ci in range(8):
                    pe_pj = p.op("tensor", lambda e, ci=ci, do=do, bank=bank: e.matmul(
                        bank[:, :], wp_sb[:, ci * 1024 + do * 128:ci * 1024 + (do + 1) * 128],
                        ao_sb[:, ci * 512:(ci + 1) * 512],
                        start=(ci == 0), stop=(ci == 7)),
                        waits=((ld_ao, lg_wp, p.last("DV")) if ci == 0 else ()))
                ldx2 = p.dma("sync", lambda e, do=do, t0=t0, osl=osl: e.dma_start(
                    out=xo_sb[:, osl * 512:(osl + 1) * 512],
                    in_=x2_h.ap()[do * 128:(do + 1) * 128, t0:t0 + 512]),
                    waits=(st_x3[osl], dv_x3[osl]))
                dv_x3[osl] = p.op("vector", lambda e, do=do, osl=osl, bank=bank: e.scalar_tensor_tensor(
                    xo_sb[:, osl * 512:(osl + 1) * 512], bank[:, :],
                    bpc_sb[:, do:do + 1], xo_sb[:, osl * 512:(osl + 1) * 512],
                    op0=ALU.add, op1=ALU.add), waits=(pe_pj, ldx2))
                st_x3[osl] = p.dma("sync", lambda e, do=do, t0=t0, osl=osl: e.dma_start(
                    out=x3_h.ap()[do * 128:(do + 1) * 128, t0:t0 + 512],
                    in_=xo_sb[:, osl * 512:(osl + 1) * 512]), waits=(dv_x3[osl],))
    p.wait("sync", p.all_dma())
    p.stage_end()
    return p.finalize() if own else None


def prep_B3_inputs(inputs, x2_shards):
    Wq = np.asarray(inputs["L1_attn_Wqkv"])
    bq = np.asarray(inputs["L1_attn_bqkv"])
    Wp = np.asarray(inputs["L1_attn_Wproj"])
    bp = np.asarray(inputs["L1_attn_bproj"])
    w1 = np.asarray(inputs["L1_attn_w1"])
    b1 = np.asarray(inputs["L1_attn_b1"])
    w2 = np.asarray(inputs["L1_attn_w2"])
    b2 = np.asarray(inputs["L1_attn_b2"])
    wb = np.zeros((128, 16 * 128), np.float32)
    for v in range(4):
        for j in range(4):
            for w in range(32):
                wb[4 * v + j, v * 128 + j * 32 + w] = w1[0, w]
                wb[64 + 4 * v + j, v * 128 + j * 32 + w] = b1[w]
    base = {
        "WqkvP": np.ascontiguousarray(
            Wq.reshape(8, 128, 24, 128).transpose(1, 2, 0, 3).reshape(128, 24 * 1024)),
        "bqkvc": np.ascontiguousarray(bq.reshape(24, 128).T),
        "WprojP": np.ascontiguousarray(
            Wp.reshape(8, 128, 8, 128).transpose(1, 0, 2, 3).reshape(128, 8 * 1024)),
        "bprojc": np.ascontiguousarray(bp.reshape(8, 128).T),
        "wb_lhs": wb,
        "w2c2": np.ascontiguousarray(np.concatenate([w2, w2], axis=0)),
        "fb2c": np.ascontiguousarray(b2[:, None]),
        "cpar": np.asarray(inputs["L1_attn_c_param"]).reshape(1, 1).astype(np.float32),
        "lmul": np.asarray(inputs["L1_attn_L_multiplier"]).reshape(1, 1).astype(np.float32),
        "ones_f": CONSTS["ones_f"], "ident": CONSTS["ident"],
    }
    return [{"X2": x2_shards[c], **base} for c in range(NCORE)]


def build_B4(p=None):
    own = p is None
    if own:
        p = Prog()
    nc = p.nc
    p.stage_begin("b4")

    x3_h = p.dram("X3", [D, NTOK], F32, kind="ExternalInput")
    phi_h = p.dram("phiP", [128, 8 * 2048], F32, kind="ExternalInput")
    onesf_h = p.dram("ones_f", [128, 128], F32, kind="ExternalInput")

    sp_h = p.dram("SP", [T, 2048], F32, kind="ExternalOutput")

    onesf_sb = p.sbuf("onesf_sb", [128, 128], F32)
    phi_sb = p.sbuf("phi_sb", [128, 8 * 2048], BF16)
    xg_sb = p.sbuf("xg_sb", [128, 8 * 512], F32)
    xln_sb = p.sbuf("xln_sb", [128, 2 * 8 * 512], BF16)
    ex_sb = p.sbuf("ex_sb", [128, 512], F32)
    sacc_sb = p.sbuf("sacc_sb", [128, 2048], F32)
    srow_sb = p.sbuf("srow_sb", [1, 2048], F32)
    t_sb = p.sbuf("t_sb", [128, 512], F32)
    rb_sb = p.sbuf("rb_sb", [128, 512], F32)
    mb_sb = p.sbuf("mb_sb", [128, 512], F32)

    PS = [p.psum(f"bank{i}", [128, 512]) for i in range(8)]

    def s_(i):
        return srow_sb[0:1, i * 512:(i + 1) * 512]

    LD_C = p.dma("sync", lambda e: e.dma_start(out=onesf_sb[:, :], in_=onesf_h.ap()))
    LG_C = p.dma("gpsimd", lambda e: e.dma_start(out=phi_sb[:, :], in_=phi_h.ap()))

    prev_dv = None
    st_sp = None
    for g in range(4):
        for b in range(2):
            t0 = b * T + g * 512
            ldx = p.dma("sync", [
                (lambda e, ci=ci, t0=t0: e.dma_start(
                    out=xg_sb[:, ci * 512:(ci + 1) * 512],
                    in_=x3_h.ap()[ci * 128:(ci + 1) * 128, t0:t0 + 512]))
                for ci in range(8)], waits=(prev_dv, p.last("PE")))
            pe_sx = None
            for ci in range(8):
                pe_sx = p.op("tensor", lambda e, ci=ci: e.matmul(
                    PS[1][0:1, :], onesf_sb[:, 0:1], xg_sb[:, ci * 512:(ci + 1) * 512],
                    start=(ci == 0), stop=(ci == 7)),
                    waits=((ldx, LD_C) if ci == 0 else ()))
            pe_ssq = pe_sx
            for ci in range(8):
                ac_sq = p.op("scalar", lambda e, ci=ci: e.activation(
                    t_sb[:, :], xg_sb[:, ci * 512:(ci + 1) * 512], AF.Square),
                    waits=(ldx, pe_ssq))
                pe_ssq = p.op("tensor", lambda e, ci=ci: e.matmul(
                    PS[2][0:1, :], onesf_sb[:, 0:1], t_sb[:, :],
                    start=(ci == 0), stop=(ci == 7)), waits=(ac_sq,))
            dv = p.op("vector", lambda e: e.tensor_scalar_mul(
                s_(0), PS[1][0:1, :], 1.0 / 1024), waits=(pe_sx, pe_ssq))
            dv = p.op("vector", lambda e: e.tensor_scalar_mul(
                s_(1), PS[2][0:1, :], 1.0 / 1024), waits=(dv,))
            dv = p.op("vector", lambda e: e.tensor_mul(s_(2), s_(0), s_(0)), waits=(dv,))
            dv = p.op("vector", lambda e: e.tensor_sub(s_(1), s_(1), s_(2)), waits=(dv,))
            dv = p.op("vector", lambda e: e.tensor_scalar_add(s_(1), s_(1), 1e-5),
                      waits=(dv,))
            ac = p.op("scalar", lambda e: e.activation(s_(1), s_(1), AF.Sqrt), waits=(dv,))
            dv = p.op("vector", lambda e: e.reciprocal(s_(1), s_(1)), waits=(ac,))
            dv = p.op("vector", lambda e: e.tensor_mul(s_(2), s_(0), s_(1)), waits=(dv,))
            pe_b = p.op("tensor", lambda e: e.matmul(
                PS[3][:, :], onesf_sb[0:1, 0:128], s_(2), start=True, stop=True),
                waits=(dv,))
            pe_rs = p.op("tensor", lambda e: e.matmul(
                PS[6][:, :], onesf_sb[0:1, 0:128], s_(1), start=True, stop=True),
                waits=(dv,))
            dv_rb = p.op("vector", lambda e: e.tensor_copy(rb_sb[:, :], PS[6][:, :]),
                         waits=(pe_rs,))
            dv_mb = p.op("vector", lambda e: e.tensor_copy(mb_sb[:, :], PS[3][:, :]),
                         waits=(pe_b,))
            dvx = None
            for ci in range(8):
                dvx = p.op("vector", lambda e, ci=ci, b=b: e.tensor_mul(
                    xln_sb[:, b * 4096 + ci * 512:b * 4096 + (ci + 1) * 512],
                    xg_sb[:, ci * 512:(ci + 1) * 512], rb_sb[:, :]),
                    waits=((dv_mb, dv_rb, pe_ssq) if ci == 0 else (dvx,)))
                dvx = p.op("vector", lambda e, ci=ci, b=b: e.tensor_sub(
                    xln_sb[:, b * 4096 + ci * 512:b * 4096 + (ci + 1) * 512],
                    xln_sb[:, b * 4096 + ci * 512:b * 4096 + (ci + 1) * 512],
                    mb_sb[:, :]), waits=(dvx,))
            prev_dv = dvx
        dv_xl = p.last("DV")
        for mtl in range(4):
            mt = g * 4 + mtl
            for npc in range(4):
                dv_sa = None
                for b in range(2):
                    bank = PS[4] if b == 0 else PS[5]
                    pe_l = None
                    for ci in range(8):
                        pe_l = p.op("tensor", lambda e, ci=ci, b=b, mtl=mtl, npc=npc, bank=bank: e.matmul(
                            bank[:, :],
                            xln_sb[:, b * 4096 + ci * 512 + mtl * 128:b * 4096 + ci * 512 + mtl * 128 + 128],
                            phi_sb[:, ci * 2048 + npc * 512:ci * 2048 + (npc + 1) * 512],
                            start=(ci == 0), stop=(ci == 7)),
                            waits=((dv_xl, LG_C, p.last("DV"), p.last("AC"))
                                   if ci == 0 else ()))
                    if b == 0:
                        ac_x = p.op("scalar", lambda e, npc=npc, bank=bank: e.activation(
                            sacc_sb[:, npc * 512:(npc + 1) * 512], bank[:, :], AF.Exp),
                            waits=(pe_l, st_sp))
                        dv_sa = ac_x
                    else:
                        ac_x = p.op("scalar", lambda e, bank=bank: e.activation(
                            ex_sb[:, :], bank[:, :], AF.Exp), waits=(pe_l,))
                        dv_sa = p.op("vector", lambda e, npc=npc: e.tensor_add(
                            sacc_sb[:, npc * 512:(npc + 1) * 512],
                            sacc_sb[:, npc * 512:(npc + 1) * 512], ex_sb[:, :]),
                            waits=(ac_x, dv_sa))
                dv_spd = dv_sa
            st_sp = p.dma("sync", lambda e, mt=mt: e.dma_start(
                out=sp_h.ap()[mt * 128:(mt + 1) * 128, :], in_=sacc_sb[:, :]),
                waits=(p.last("DV"), p.last("AC")))
    p.wait("sync", p.all_dma())
    p.stage_end()
    return p.finalize() if own else None


def prep_B4_inputs(inputs, x3_shards):
    phi = np.asarray(inputs["L1_ffn_phi"]).reshape(1024, 2048)
    base = {
        "phiP": np.ascontiguousarray(
            phi.reshape(8, 128, 2048).transpose(1, 0, 2).reshape(128, 16384)),
        "ones_f": CONSTS["ones_f"],
    }
    return [{"X3": x3_shards[c], **base} for c in range(NCORE)]


def build_C(p=None):
    own = p is None
    if own:
        p = Prog()
    nc = p.nc
    p.stage_begin("c")
    sps_h = p.dram("SPS", [8, 256 * 2048], F32, kind="ExternalInput")
    sr_h = p.dram("SR", [256, 2048], F32, kind="ExternalOutput")
    acc_sb = p.sbuf("acc_sb", [128, 2048], F32)
    ld_sb = p.sbuf("ld_sb", [128, 2 * 2048], F32)
    st_sr = None
    for rt in range(2):
        ld0 = p.dma("sync", lambda e, rt=rt: e.dma_start(
            out=acc_sb[:, :],
            in_=_ap(sps_h.ap(), [[2048, 128], [1, 2048]], off=rt * 128 * 2048)),
            waits=(p.last("DV"), st_sr))
        dv = None
        lds = [None, None]
        for r in range(1, 8):
            sl = r % 2
            lds[sl] = p.dma("sync", lambda e, r=r, rt=rt, sl=sl: e.dma_start(
                out=ld_sb[:, sl * 2048:(sl + 1) * 2048],
                in_=_ap(sps_h.ap(), [[2048, 128], [1, 2048]],
                        off=r * 256 * 2048 + rt * 128 * 2048)),
                waits=((dv,) if r >= 3 else ()))
            dv = p.op("vector", lambda e, sl=sl: e.tensor_add(
                acc_sb[:, :], acc_sb[:, :], ld_sb[:, sl * 2048:(sl + 1) * 2048]),
                waits=(lds[sl], ld0, dv))
        dv = p.op("vector", lambda e: e.reciprocal(acc_sb[:, :], acc_sb[:, :]),
                  waits=(dv,))
        st_sr = p.dma("sync", lambda e, rt=rt: e.dma_start(
            out=sr_h.ap()[rt * 128:(rt + 1) * 128, :], in_=acc_sb[:, :]),
            waits=(dv,))
    p.wait("sync", p.all_dma())
    p.stage_end()
    return p.finalize() if own else None


def build_D(p=None):
    own = p is None
    if own:
        p = Prog()
    nc = p.nc
    p.stage_begin("d")

    x3_h = p.dram("X3", [D, NTOK], F32, kind="ExternalInput")
    phi_h = p.dram("phiP", [128, 8 * 2048], F32, kind="ExternalInput")
    sinv_h = p.dram("SINV", [T, 2048], F32, kind="ExternalInput")
    wp_h = p.dram("WP", [64 * 2 * 128, 4096], F32, kind="ExternalInput")
    bh_h = p.dram("BH", [64, 1024], F32, kind="ExternalInput")
    onesf_h = p.dram("ones_f", [128, 128], F32, kind="ExternalInput")
    ident_h = p.dram("ident", [128, 128], F32, kind="ExternalInput")

    x4_h = p.dram("X4", [D, NTOK], F32, kind="ExternalOutput")

    xlnh = p.dram("xlnD", [2 * D, T], BF16)
    slots_hbm = p.dram("slotsD", [128, 2 * 8 * 2048], BF16)
    yyh = p.dram("yyD", [2 * 2048, 1024], BF16)

    onesf_sb = p.sbuf("onesf_sb", [128, 128], F32)
    onesb_sb = p.sbuf("onesb_sb", [128, 128], BF16)
    identb_sb = p.sbuf("identb_sb", [128, 128], BF16)
    phi_sb = p.sbuf("phi_sb", [128, 16384], BF16)
    xg_sb = p.sbuf("xg_sb", [128, 8 * 512], F32)
    xln_sb = p.sbuf("xln_sb", [128, 16384], BF16)
    xlt_sb = p.sbuf("xlt_sb", [128, 16384], BF16)
    dsp_sb = p.sbuf("dsp_sb", [128, 8192], BF16)
    est_sb = p.sbuf("est_sb", [128, 2 * 512], F32)
    siv_sb = p.sbuf("siv_sb", [128, 2 * 512], F32)
    slst_sb = p.sbuf("slst_sb", [128, 2 * 512], BF16)
    w_sb = p.sbuf("w_sb", [128, 2 * 4096], BF16)
    sn_sb = p.sbuf("sn_sb", [128, 4 * 256], BF16)
    yst_sb = p.sbuf("yst_sb", [32, 2 * 512], BF16)
    bst_sb = p.sbuf("bst_sb", [1, 2 * 1024], BF16)
    denb_sb = p.sbuf("denb_sb", [128, 512], F32)
    x3t_sb = p.sbuf("x3t_sb", [128, 2 * 512], F32)
    x4t_sb = p.sbuf("x4t_sb", [128, 2 * 512], F32)
    srow_sb = p.sbuf("srow_sb", [1, 2048], F32)
    rb_sb = p.sbuf("rb_sb", [128, 512], F32)
    mb_sb = p.sbuf("mb_sb", [128, 512], F32)
    t_sb = p.sbuf("t_sb", [128, 512], F32)

    PS = [p.psum(f"bank{i}", [128, 512]) for i in range(8)]
    PSB = [b_[:, :].bitcast(BF16) for b_ in PS]
    bcons = [None] * 8

    def s_(i):
        return srow_sb[0:1, i * 512:(i + 1) * 512]

    LD_C = p.dma("sync", [
        lambda e: e.dma_start(out=onesf_sb[:, :], in_=onesf_h.ap()),
    ])
    LG_C = p.dma("gpsimd", [
        lambda e: e.dma_start(out=onesb_sb[:, :], in_=onesf_h.ap()),
        lambda e: e.dma_start(out=identb_sb[:, :], in_=ident_h.ap()),
        lambda e: e.dma_start(out=phi_sb[:, :], in_=phi_h.ap()),
    ])

    ldS_s = [None, None]; dsp_cons = [None] * 16
    est_cons = [None, None]; slst_st = [None, None]
    tr_cons = [None, None]
    x3_cons = [None, None]; x4_st = [None, None]
    prev_ldx = None

    for b in range(2):
        dvx = None
        for g in range(4):
            t0 = b * T + g * 512
            ldx = p.dma("sync", [
                (lambda e, ci=ci, t0=t0: e.dma_start(
                    out=xg_sb[:, ci * 512:(ci + 1) * 512],
                    in_=x3_h.ap()[ci * 128:(ci + 1) * 128, t0:t0 + 512]))
                for ci in range(8)], waits=(prev_ldx, p.last("AC")))
            pe_sx = None
            for ci in range(8):
                pe_sx = p.op("tensor", lambda e, ci=ci: e.matmul(
                    PS[1][0:1, :], onesf_sb[:, 0:1], xg_sb[:, ci * 512:(ci + 1) * 512],
                    start=(ci == 0), stop=(ci == 7)),
                    waits=((ldx, LD_C, bcons[1]) if ci == 0 else ()))
            pe_ssq = pe_sx
            for ci in range(8):
                ac_sq = p.op("scalar", lambda e, ci=ci: e.activation(
                    t_sb[:, :], xg_sb[:, ci * 512:(ci + 1) * 512], AF.Square),
                    waits=(ldx, pe_ssq))
                pe_ssq = p.op("tensor", lambda e, ci=ci: e.matmul(
                    PS[2][0:1, :], onesf_sb[:, 0:1], t_sb[:, :],
                    start=(ci == 0), stop=(ci == 7)),
                    waits=((ac_sq, bcons[2]) if ci == 0 else (ac_sq,)))
            dv = p.op("vector", lambda e: e.tensor_scalar_mul(
                s_(0), PS[1][0:1, :], 1.0 / 1024), waits=(pe_sx, pe_ssq))
            bcons[1] = dv
            dv = p.op("vector", lambda e: e.tensor_scalar_mul(
                s_(1), PS[2][0:1, :], 1.0 / 1024), waits=(dv,))
            bcons[2] = dv
            dv = p.op("vector", lambda e: e.tensor_mul(s_(2), s_(0), s_(0)), waits=(dv,))
            dv = p.op("vector", lambda e: e.tensor_sub(s_(1), s_(1), s_(2)), waits=(dv,))
            dv = p.op("vector", lambda e: e.tensor_scalar_add(s_(1), s_(1), 1e-5),
                      waits=(dv,))
            ac = p.op("scalar", lambda e: e.activation(s_(1), s_(1), AF.Sqrt), waits=(dv,))
            dv = p.op("vector", lambda e: e.reciprocal(s_(1), s_(1)), waits=(ac,))
            dv = p.op("vector", lambda e: e.tensor_mul(s_(2), s_(0), s_(1)), waits=(dv,))
            pe_b = p.op("tensor", lambda e: e.matmul(
                PS[3][:, :], onesf_sb[0:1, 0:128], s_(2), start=True, stop=True),
                waits=(dv, bcons[3]))
            pe_rs = p.op("tensor", lambda e: e.matmul(
                PS[6][:, :], onesf_sb[0:1, 0:128], s_(1), start=True, stop=True),
                waits=(dv, bcons[6]))
            dv_rb = p.op("vector", lambda e: e.tensor_copy(rb_sb[:, :], PS[6][:, :]),
                         waits=(pe_rs,))
            bcons[6] = dv_rb
            dv_mb = p.op("vector", lambda e: e.tensor_copy(mb_sb[:, :], PS[3][:, :]),
                         waits=(pe_b,))
            bcons[3] = dv_mb
            for ci in range(8):
                dvx = p.op("vector", lambda e, ci=ci, g=g: e.tensor_mul(
                    xln_sb[:, ci * 2048 + g * 512:ci * 2048 + (g + 1) * 512],
                    xg_sb[:, ci * 512:(ci + 1) * 512], rb_sb[:, :]),
                    waits=((dv_mb, dv_rb, pe_ssq) if ci == 0 else (dvx,)))
                dvx = p.op("vector", lambda e, ci=ci, g=g: e.tensor_sub(
                    xln_sb[:, ci * 2048 + g * 512:ci * 2048 + (g + 1) * 512],
                    xln_sb[:, ci * 2048 + g * 512:ci * 2048 + (g + 1) * 512],
                    mb_sb[:, :]), waits=(dvx,))
            prev_ldx = dvx
        dv_xln = dvx
        st_xln = p.dma("sync", [
            (lambda e, dt=dt, b=b: e.dma_start(
                out=xlnh.ap()[b * D + dt * 128:b * D + (dt + 1) * 128, :],
                in_=xln_sb[:, dt * 2048:(dt + 1) * 2048]))
            for dt in range(8)], waits=(dv_xln,))
        for dt in range(8):
            for mt in range(16):
                pr = (dt * 16 + mt) % 2
                bank = 4 + pr
                pe_t = p.op("tensor", lambda e, dt=dt, mt=mt, bank=bank: e.transpose(
                    PSB[bank][0:128, 0:128],
                    xln_sb[:, dt * 2048 + mt * 128:dt * 2048 + mt * 128 + 128],
                    identb_sb[:, :]),
                    waits=(dv_xln, LG_C, bcons[bank], tr_cons[pr]))
                dv_t = p.op("vector", lambda e, dt=dt, mt=mt, bank=bank: e.tensor_copy(
                    xlt_sb[:, mt * 1024 + dt * 128:mt * 1024 + dt * 128 + 128],
                    PSB[bank][0:128, 0:128]), waits=(pe_t,))
                bcons[bank] = dv_t
                tr_cons[pr] = dv_t
        for npc in range(4):
            dv_dsp_last = None
            for mt in range(16):
                bank = mt % 2
                pe_l = None
                for dt in range(8):
                    pe_l = p.op("tensor", lambda e, dt=dt, mt=mt, npc=npc, bank=bank: e.matmul(
                        PS[bank][:, :],
                        xln_sb[:, dt * 2048 + mt * 128:dt * 2048 + mt * 128 + 128],
                        phi_sb[:, dt * 2048 + npc * 512:dt * 2048 + (npc + 1) * 512],
                        start=(dt == 0), stop=(dt == 7)),
                        waits=((dv_xln, LG_C, bcons[bank]) if dt == 0 else ()))
                sl = mt % 2
                ldS_s[sl] = p.dma("sync", lambda e, mt=mt, npc=npc, sl=sl: e.dma_start(
                    out=siv_sb[:, sl * 512:(sl + 1) * 512],
                    in_=sinv_h.ap()[mt * 128:(mt + 1) * 128, npc * 512:(npc + 1) * 512]),
                    waits=(est_cons[sl],))
                ac_e = p.op("scalar", lambda e, sl=sl, bank=bank: e.activation(
                    est_sb[:, sl * 512:(sl + 1) * 512], PS[bank][:, :], AF.Exp),
                    waits=(pe_l, est_cons[sl]))
                bcons[bank] = ac_e
                dv_d = p.op("vector", lambda e, mt=mt, sl=sl: e.tensor_mul(
                    dsp_sb[:, mt * 512:(mt + 1) * 512],
                    est_sb[:, sl * 512:(sl + 1) * 512],
                    siv_sb[:, sl * 512:(sl + 1) * 512]),
                    waits=(ac_e, ldS_s[sl]))
                est_cons[sl] = dv_d
                dsp_cons[mt] = dv_d
                dv_dsp_last = dv_d
            for dt in range(8):
                bank = 2 + dt % 2
                pe_s = None
                for mt in range(16):
                    pe_s = p.op("tensor", lambda e, dt=dt, mt=mt, bank=bank: e.matmul(
                        PS[bank][:, :],
                        xlt_sb[:, mt * 1024 + dt * 128:mt * 1024 + dt * 128 + 128],
                        dsp_sb[:, mt * 512:(mt + 1) * 512],
                        start=(mt == 0), stop=(mt == 15)),
                        waits=((dv_dsp_last, bcons[bank]) if mt == 0 else ()))
                sl2 = dt % 2
                dv_sl = p.op("vector", lambda e, sl2=sl2, bank=bank: e.tensor_copy(
                    slst_sb[:, sl2 * 512:(sl2 + 1) * 512], PS[bank][:, :]),
                    waits=(pe_s, slst_st[sl2]))
                bcons[bank] = dv_sl
                slst_st[sl2] = p.dma("sync", lambda e, b=b, dt=dt, npc=npc, sl2=sl2: e.dma_start(
                    out=slots_hbm.ap()[:, b * 16384 + dt * 2048 + npc * 512:
                                       b * 16384 + dt * 2048 + (npc + 1) * 512],
                    in_=slst_sb[:, sl2 * 512:(sl2 + 1) * 512]), waits=(dv_sl,))
        for mt in range(16):
            dsp_cons[mt] = p.last("PE")

    all_slst = p.all_dma()
    dv_Bstart = p.last("DV")
    sn_cons = [None] * 4
    w_cons = [None, None]
    bst_cons = [None, None]
    yst_st = [None, None]
    for i in range(4, 8):
        if bcons[i] is None:
            bcons[i] = dv_Bstart
    for n in range(64):
        ldsn = [None, None]
        for b in range(2):
            snsl = (n % 2) * 2 + b
            ldsn[b] = p.dma("sync", lambda e, n=n, b=b, snsl=snsl: e.dma_start(
                out=sn_sb[:, snsl * 256:(snsl + 1) * 256],
                in_=_ap(slots_hbm.ap(), [[32768, 128], [2048, 8], [1, 32]],
                        off=b * 16384 + n * 32)),
                waits=(all_slst if n == 0 else (sn_cons[snsl],)))
        bsl = n % 2
        ldb = p.dma("gpsimd", lambda e, n=n, bsl=bsl: e.dma_start(
            out=bst_sb[0:1, bsl * 1024:(bsl + 1) * 1024],
            in_=bh_h.ap()[n:n + 1, :]), waits=(bst_cons[bsl],))
        for ec in range(2):
            wsl = (n * 2 + ec) % 2
            ldw = p.dma("gpsimd", lambda e, n=n, ec=ec, wsl=wsl: e.dma_start(
                out=w_sb[:, wsl * 4096:(wsl + 1) * 4096],
                in_=wp_h.ap()[(n * 2 + ec) * 128:(n * 2 + ec) * 128 + 128, :]),
                waits=(w_cons[wsl],))
            for b in range(2):
                bank = 4 + (n * 4 + ec * 2 + b) % 2
                snsl = (n % 2) * 2 + b
                pe_y = p.op("tensor", lambda e, bsl=bsl, ec=ec, bank=bank: e.matmul(
                    PS[bank][0:32, :], onesb_sb[0:1, 0:32],
                    bst_sb[0:1, bsl * 1024 + ec * 512:bsl * 1024 + (ec + 1) * 512],
                    start=True, stop=False),
                    waits=(ldb, ldw, ldsn[b], bcons[bank], LG_C))
                for dt in range(8):
                    pe_y = p.op("tensor", lambda e, dt=dt, snsl=snsl, wsl=wsl, bank=bank: e.matmul(
                        PS[bank][0:32, :],
                        sn_sb[:, snsl * 256 + dt * 32:snsl * 256 + (dt + 1) * 32],
                        w_sb[:, wsl * 4096 + dt * 512:wsl * 4096 + (dt + 1) * 512],
                        start=False, stop=(dt == 7)))
                sn_cons[snsl] = pe_y
                w_cons[wsl] = pe_y
                bst_cons[bsl] = pe_y
                ysl = (n * 4 + ec * 2 + b) % 2
                dv_y = p.op("vector", lambda e, ysl=ysl, bank=bank: e.tensor_copy(
                    yst_sb[0:32, ysl * 512:(ysl + 1) * 512], PS[bank][0:32, :]),
                    waits=(pe_y, yst_st[ysl]))
                bcons[bank] = dv_y
                yst_st[ysl] = p.dma("sync", lambda e, n=n, ec=ec, b=b, ysl=ysl: e.dma_start(
                    out=yyh.ap()[b * 2048 + n * 32:b * 2048 + (n + 1) * 32,
                                 ec * 512:(ec + 1) * 512],
                    in_=yst_sb[0:32, ysl * 512:(ysl + 1) * 512]), waits=(dv_y,))

    for b in range(2):
        all_pe = p.last("PE")
        all_yy = p.all_dma()
        ldyy = p.dma("sync", [
            (lambda e, npt=npt, b=b: e.dma_start(
                out=xlt_sb[:, npt * 1024:(npt + 1) * 1024],
                in_=yyh.ap()[b * 2048 + npt * 128:b * 2048 + (npt + 1) * 128, :]))
            for npt in range(16)], waits=all_yy + (all_pe,))
        ldxl = p.dma("sync", [
            (lambda e, dt=dt, b=b: e.dma_start(
                out=xln_sb[:, dt * 2048:(dt + 1) * 2048],
                in_=xlnh.ap()[b * D + dt * 128:b * D + (dt + 1) * 128, :]))
            for dt in range(8)], waits=(all_pe,))
        for mc in range(4):
            ac_last = None
            for npt in range(16):
                bank = npt % 2
                pe_e = None
                for dt in range(8):
                    pe_e = p.op("tensor", lambda e, dt=dt, npt=npt, mc=mc, b=b, bank=bank: e.matmul(
                        PS[bank][:, :],
                        phi_sb[:, dt * 2048 + npt * 128:dt * 2048 + npt * 128 + 128],
                        xln_sb[:, dt * 2048 + mc * 512:dt * 2048 + (mc + 1) * 512],
                        start=(dt == 0), stop=(dt == 7)),
                        waits=((ldxl, bcons[bank]) if dt == 0 else ()))
                ac_last = p.op("scalar", lambda e, npt=npt, bank=bank: e.activation(
                    dsp_sb[:, npt * 512:(npt + 1) * 512], PS[bank][:, :], AF.Exp),
                    waits=(pe_e,))
                bcons[bank] = ac_last
            pe_d = None
            for npt in range(16):
                pe_d = p.op("tensor", lambda e, npt=npt: e.matmul(
                    PS[6][0:1, :], onesb_sb[:, 0:1],
                    dsp_sb[:, npt * 512:(npt + 1) * 512],
                    start=(npt == 0), stop=(npt == 15)),
                    waits=((ac_last, bcons[6]) if npt == 0 else ()))
            dv_r = p.op("vector", lambda e: e.reciprocal(s_(1), PS[6][0:1, :]),
                        waits=(pe_d,))
            bcons[6] = dv_r
            pe_bc = p.op("tensor", lambda e: e.matmul(
                PS[7][:, :], onesf_sb[0:1, 0:128], s_(1), start=True, stop=True),
                waits=(dv_r, bcons[7]))
            dv_db = p.op("vector", lambda e: e.tensor_copy(denb_sb[:, :], PS[7][:, :]),
                         waits=(pe_bc,))
            bcons[7] = dv_db
            for et in range(8):
                bank = 2 + et % 2
                pe_o = None
                for npt in range(16):
                    pe_o = p.op("tensor", lambda e, npt=npt, et=et, bank=bank: e.matmul(
                        PS[bank][:, :],
                        xlt_sb[:, npt * 1024 + et * 128:npt * 1024 + et * 128 + 128],
                        dsp_sb[:, npt * 512:(npt + 1) * 512],
                        start=(npt == 0), stop=(npt == 15)),
                        waits=((ldyy, bcons[bank]) if npt == 0 else ()))
                xsl = et % 2
                ldx3 = p.dma("sync", lambda e, et=et, b=b, mc=mc, xsl=xsl: e.dma_start(
                    out=x3t_sb[:, xsl * 512:(xsl + 1) * 512],
                    in_=x3_h.ap()[et * 128:(et + 1) * 128,
                                  b * T + mc * 512:b * T + (mc + 1) * 512]),
                    waits=(x3_cons[xsl],))
                dv_o = p.op("vector", lambda e, xsl=xsl, bank=bank: e.tensor_mul(
                    x4t_sb[:, xsl * 512:(xsl + 1) * 512], PS[bank][:, :], denb_sb[:, :]),
                    waits=(pe_o, x4_st[xsl], dv_db))
                bcons[bank] = dv_o
                dv_o = p.op("vector", lambda e, xsl=xsl: e.tensor_add(
                    x4t_sb[:, xsl * 512:(xsl + 1) * 512],
                    x4t_sb[:, xsl * 512:(xsl + 1) * 512],
                    x3t_sb[:, xsl * 512:(xsl + 1) * 512]), waits=(ldx3, dv_o))
                x3_cons[xsl] = dv_o
                x4_st[xsl] = p.dma("sync", lambda e, et=et, b=b, mc=mc, xsl=xsl: e.dma_start(
                    out=x4_h.ap()[et * 128:(et + 1) * 128,
                                  b * T + mc * 512:b * T + (mc + 1) * 512],
                    in_=x4t_sb[:, xsl * 512:(xsl + 1) * 512]), waits=(dv_o,))
    p.wait("sync", p.all_dma())
    p.stage_end()
    return p.finalize() if own else None


def prep_D_inputs(inputs, x3_shards, sinv_full):
    phi = np.asarray(inputs["L1_ffn_phi"]).reshape(1024, 2048)
    W = np.asarray(inputs["L1_ffn_W"])
    base = {
        "phiP": np.ascontiguousarray(
            phi.reshape(8, 128, 2048).transpose(1, 0, 2).reshape(128, 16384)),
        "WP": np.ascontiguousarray(
            W.reshape(64, 8, 128, 2, 512).transpose(0, 3, 2, 1, 4)
            .reshape(64 * 2 * 128, 4096)),
        "BH": np.ascontiguousarray(np.asarray(inputs["L1_ffn_b"], np.float32)),
        "SINV": np.ascontiguousarray(sinv_full.astype(np.float32)),
        "ones_f": CONSTS["ones_f"], "ident": CONSTS["ident"],
    }
    return [{"X3": x3_shards[c], **base} for c in range(NCORE)]


def prep_B2_inputs(inputs, x1_shards):
    NE = 18
    w1 = np.asarray(inputs["L0_ffn_w1"])
    w3 = np.asarray(inputs["L0_ffn_w3"])
    w2 = np.asarray(inputs["L0_ffn_w2"])
    b1 = np.asarray(inputs["L0_ffn_b1"])
    b3 = np.asarray(inputs["L0_ffn_b3"])
    sw1 = np.asarray(inputs["L0_ffn_sw1"])
    sw3 = np.asarray(inputs["L0_ffn_sw3"])
    sw2 = np.asarray(inputs["L0_ffn_sw2"])
    sb1 = np.asarray(inputs["L0_ffn_sb1"])
    sb3 = np.asarray(inputs["L0_ffn_sb3"])

    def pmaj_w1(wT):
        return wT.reshape(8, 128, 512).transpose(1, 0, 2).reshape(128, 4096)

    def pmaj_w2(wT):
        return wT.reshape(4, 128, 1024).transpose(1, 0, 2).reshape(128, 4096)

    w1P = np.zeros((128, NE * 4096), np.float32)
    w3P = np.zeros((128, NE * 4096), np.float32)
    w2P = np.zeros((128, NE * 4096), np.float32)
    b1c = np.zeros((128, NE * 4), np.float32)
    b3c = np.zeros((128, NE * 4), np.float32)
    for e in range(16):
        w1P[:, e * 4096:(e + 1) * 4096] = pmaj_w1(w1[e].T)
        w3P[:, e * 4096:(e + 1) * 4096] = pmaj_w1(w3[e].T)
        w2P[:, e * 4096:(e + 1) * 4096] = pmaj_w2(w2[e].T)
        b1c[:, e * 4:(e + 1) * 4] = b1[e].reshape(4, 128).T
        b3c[:, e * 4:(e + 1) * 4] = b3[e].reshape(4, 128).T
    sw1T = sw1.T
    sw3T = sw3.T
    sw2T = sw2.T
    for j in range(2):
        e = 16 + j
        w1P[:, e * 4096:(e + 1) * 4096] = pmaj_w1(sw1T[:, j * 512:(j + 1) * 512])
        w3P[:, e * 4096:(e + 1) * 4096] = pmaj_w1(sw3T[:, j * 512:(j + 1) * 512])
        w2P[:, e * 4096:(e + 1) * 4096] = pmaj_w2(sw2T[j * 512:(j + 1) * 512, :])
        b1c[:, e * 4:(e + 1) * 4] = sb1[j * 512:(j + 1) * 512].reshape(4, 128).T
        b3c[:, e * 4:(e + 1) * 4] = sb3[j * 512:(j + 1) * 512].reshape(4, 128).T
    oh = np.zeros((16, 16 * 128), np.float32)
    for e in range(16):
        oh[e, e * 128:(e + 1) * 128] = 1.0
    base = {
        "gwtP": np.ascontiguousarray(
            np.asarray(inputs["L0_ffn_gate_w"]).T.reshape(8, 128, 16)
            .transpose(1, 0, 2).reshape(128, 128)),
        "ebr": np.ascontiguousarray(np.asarray(inputs["L0_ffn_expert_bias"])[None, :]),
        "w1P": np.ascontiguousarray(w1P), "w3P": np.ascontiguousarray(w3P),
        "w2P": np.ascontiguousarray(w2P),
        "b1c": np.ascontiguousarray(b1c), "b3c": np.ascontiguousarray(b3c),
        "b2m": np.ascontiguousarray(np.asarray(inputs["L0_ffn_b2"])),
        "sb2c": np.ascontiguousarray(np.asarray(inputs["L0_ffn_sb2"]).reshape(8, 128).T),
        "ohall": np.ascontiguousarray(oh),
        "ones_f": CONSTS["ones_f"], "ident": CONSTS["ident"],
    }
    return [{"X1": x1_shards[c], **base} for c in range(NCORE)]
